# version 08
# baseline (speedup 1.0000x reference)
import numpy as np
import concourse.bass as bass
import concourse.mybir as mybir
from concourse.bass_utils import run_bass_kernel_spmd
from contextlib import ExitStack

F32 = mybir.dt.float32
BF16 = mybir.dt.bfloat16
AF = mybir.ActivationFunctionType
ALU = mybir.AluOpType

D = 2048
S = 2048
II = 8192
NH, NKV, HD = 16, 4, 128
NDC = D // 128
NIC = II // 128
TSUP = 512
NSUP = S // TSUP
QG = 512
NQG = S // QG
NTT = TSUP // 128

C_TAU1, C_TAU2, C_TAU3, C_15, C_INV2048, C_EPS, C_NEG1, C_HALF, C_ISQHD, C_ONE = range(10)
EXP_SHIFT = 10.0

ENGS = ("sync", "gpsimd", "tensor", "vector", "scalar")


class Prog:

    NDMASEM = 8

    def __init__(self, nc, es):
        self.nc = nc
        self.steps = {e: [] for e in ENGS}
        self.sems = {e: es.enter_context(nc.semaphore("sem_" + e)) for e in ENGS}
        for q in ("sync", "gpsimd"):
            for i in range(self.NDMASEM):
                n = f"{q}_d{i}"
                self.sems[n] = es.enter_context(nc.semaphore("sem_" + n))
        self.counts = {e: 0 for e in self.sems}
        self.rr = {"sync": 0, "gpsimd": 0}
        self.guard = {}

    def emit(self, eng, fn, waits=(), inc=0):
        sems = self.sems
        wl = []
        for w in waits:
            if w is None:
                continue
            en, val = w
            if val > 0:
                wl.append((en, val))
        wl = tuple(wl)

        def step(e, fn=fn, wl=wl, inc=inc, sem=self.sems[eng]):
            for (en, val) in wl:
                e.wait_ge(sems[en], val)
            ins = fn(e) if fn is not None else None
            if inc:
                ins.then_inc(sem, inc)

        self.steps[eng].append(step)
        if inc:
            self.counts[eng] += inc
        return (eng, self.counts[eng])

    def dma(self, queue, fn, waits=()):
        i = self.rr[queue] % self.NDMASEM
        self.rr[queue] += 1
        sname = f"{queue}_d{i}"
        w = list(waits)
        if self.counts[sname] > 0:
            w.append((sname, self.counts[sname]))
        sems = self.sems
        wl = []
        for x in w:
            if x is None:
                continue
            en, val = x
            if val > 0:
                wl.append((en, val))
        wl = tuple(wl)

        def step(e, fn=fn, wl=wl, sem=self.sems[sname]):
            for (en, val) in wl:
                e.wait_ge(sems[en], val)
            fn(e).then_inc(sem, 16)

        self.steps[queue].append(step)
        self.counts[sname] += 16
        return (sname, self.counts[sname])

    def reads(self, key, milestone):
        d = self.guard.setdefault(key, {})
        d[milestone[0]] = max(d.get(milestone[0], 0), milestone[1])

    def war(self, key):
        d = self.guard.get(key, {})
        return [(e, v) for e, v in d.items()]

    def fence(self):
        snap = dict(self.counts)
        for e in ENGS:
            self.emit(e, None, waits=[(o, v) for o, v in snap.items() if o != e])


class SB:
    pass


def emit_ln(P, pb, sb, load_chunk, xln_dst, xln_key):
    cs = sb.consts
    m_sum = [None] * NDC
    for c in range(NDC):
        par = c % 2
        ap, w, key = load_chunk(c, 0)
        w = list(w) + P.war(("lncast", par))
        mcast = P.emit("scalar", lambda s, ap=ap, par=par: s.activation(
            sb.xb[:, par, :], ap, AF.Copy), waits=w, inc=1)
        msq = P.emit("scalar", lambda s, ap=ap, par=par: s.activation(
            sb.x2b[:, par, :], ap, AF.Square), inc=1)
        P.reads(key, mcast)
        P.reads(key, msq)
        w = [msq]
        if c == 0:
            w += P.war(("pb", 4)) + P.war(("pb", 5)) + [P.m_ones]
        P.emit("tensor", lambda t, c=c, par=par: t.matmul(
            pb[4][0:1, :], sb.ones_bf[:, :], sb.xb[:, par, :],
            start=(c == 0), stop=(c == NDC - 1)), waits=w, inc=1)
        m_sum[c] = P.emit("tensor", lambda t, c=c, par=par: t.matmul(
            pb[5][0:1, :], sb.ones_bf[:, :], sb.x2b[:, par, :],
            start=(c == 0), stop=(c == NDC - 1)), inc=1)
        P.guard[("lncast", par)] = {}
        P.reads(("lncast", par), m_sum[c])
    m1 = P.emit("vector", lambda v: v.tensor_scalar(
        sb.st_m[:, :], pb[4][0:1, :], cs[:1, C_INV2048:C_INV2048 + 1], None,
        op0=ALU.mult), waits=[m_sum[NDC - 1]], inc=1)
    m2 = P.emit("vector", lambda v: v.tensor_scalar(
        sb.st_e2[:, :], pb[5][0:1, :], cs[:1, C_INV2048:C_INV2048 + 1],
        cs[:1, C_EPS:C_EPS + 1], op0=ALU.mult, op1=ALU.add), inc=1)
    P.guard[("pb", 4)] = {}
    P.guard[("pb", 5)] = {}
    P.reads(("pb", 4), m1)
    P.reads(("pb", 5), m2)
    m3 = P.emit("vector", lambda v: v.tensor_tensor(
        sb.st_m2[:, :], sb.st_m[:, :], sb.st_m[:, :], op=ALU.mult), inc=1)
    m4 = P.emit("vector", lambda v: v.tensor_tensor(
        sb.st_e2[:, :], sb.st_e2[:, :], sb.st_m2[:, :], op=ALU.subtract), inc=1)
    m5 = P.emit("scalar", lambda s: s.activation(
        sb.st_m2[:, :], sb.st_e2[:, :], AF.Sqrt), waits=[m4], inc=1)
    m6 = P.emit("vector", lambda v: v.reciprocal(
        sb.st_rstd[:, :], sb.st_m2[:, :]),
        waits=[m5] + P.war("st_rstd"), inc=1)
    m7 = P.emit("vector", lambda v: v.scalar_tensor_tensor(
        sb.st_b[:, :], sb.st_m[:, :], cs[:1, C_NEG1:C_NEG1 + 1],
        sb.st_rstd[:, :], op0=ALU.mult, op1=ALU.mult),
        waits=P.war("st_b"), inc=1)
    mb1 = P.emit("tensor", lambda t: t.matmul(
        pb[4][:, :], sb.ones_row[:1, :], sb.st_rstd[:, :],
        start=True, stop=True), waits=[m6, m1] + P.war(("pb", 4)), inc=1)
    mb2 = P.emit("tensor", lambda t: t.matmul(
        pb[5][:, :], sb.ones_row[:1, :], sb.st_b[:, :],
        start=True, stop=True), waits=[m7, m2] + P.war(("pb", 5)), inc=1)
    P.guard["st_rstd"] = {}
    P.guard["st_b"] = {}
    P.reads("st_rstd", mb1)
    P.reads("st_b", mb2)
    mc1 = P.emit("vector", lambda v: v.tensor_copy(
        sb.bc_a[:, :], pb[4][:, :]), waits=[mb1], inc=1)
    mc2 = P.emit("vector", lambda v: v.tensor_copy(
        sb.bc_b[:, :], pb[5][:, :]), waits=[mb2], inc=1)
    P.guard[("pb", 4)] = {}
    P.guard[("pb", 5)] = {}
    P.reads(("pb", 4), mc1)
    P.reads(("pb", 5), mc2)
    mnorm = [None] * NDC
    for c in range(NDC):
        ap, w2, key2 = load_chunk(c, 1)
        w = list(w2) + ([mc2] if c == 0 else []) + (P.war(xln_key) if c == 0 else [])
        mn1 = P.emit("vector", lambda v, ap=ap: v.tensor_tensor(
            sb.t_norm[:, :], ap, sb.bc_a[:, :], op=ALU.mult),
            waits=w, inc=1)
        P.reads(key2, mn1)
        mnorm[c] = P.emit("vector", lambda v, c=c: v.tensor_tensor(
            xln_dst[:, c, :], sb.t_norm[:, :], sb.bc_b[:, :], op=ALU.add), inc=1)
    P.guard[xln_key] = {}
    return mnorm


def build(debug_streams=False):
    nc = bass.Bass("TRN2", target_bir_lowering=False)

    x_in = nc.dram_tensor("x", [S, D], F32, kind="ExternalInput")
    wq_in = nc.dram_tensor("wq", [D, D], F32, kind="ExternalInput")
    wk_in = nc.dram_tensor("wk", [D, 512], F32, kind="ExternalInput")
    wv_in = nc.dram_tensor("wv", [D, 512], F32, kind="ExternalInput")
    wo_in = nc.dram_tensor("wo", [D, D], F32, kind="ExternalInput")
    wg_in = nc.dram_tensor("wg", [D, II], F32, kind="ExternalInput")
    wu_in = nc.dram_tensor("wu", [D, II], F32, kind="ExternalInput")
    wd_in = nc.dram_tensor("wd", [II, D], F32, kind="ExternalInput")
    consts_in = nc.dram_tensor("consts", [128, 16], F32, kind="ExternalInput")
    ident_in = nc.dram_tensor("ident", [128, 128], F32, kind="ExternalInput")
    sd_in = nc.dram_tensor("sd", [3, 128, 128], F32, kind="ExternalInput")
    swap_in = nc.dram_tensor("swapm", [128, 128], F32, kind="ExternalInput")
    ctab_in = nc.dram_tensor("ctab", [128, S], F32, kind="ExternalInput")
    stab_in = nc.dram_tensor("stab", [128, S], F32, kind="ExternalInput")
    out = nc.dram_tensor("out", [S, D], F32, kind="ExternalOutput")

    skind = "ExternalOutput" if debug_streams else "Internal"
    s1 = nc.dram_tensor("s1", [NDC, 128, S], F32, kind=skind)
    s2 = nc.dram_tensor("s2", [NDC, 128, S], F32, kind=skind)
    if debug_streams:
        pt_dbg = nc.dram_tensor("pt_dbg", [128, NDC, QG], F32, kind="ExternalOutput")
        o_dbg = nc.dram_tensor("o_dbg", [128, NH, QG], F32, kind="ExternalOutput")
        q_dbg = nc.dram_tensor("q_dbg", [128, NH, QG], F32, kind="ExternalOutput")
        k_dbg = nc.dram_tensor("k_dbg", [128, NKV, S], F32, kind="ExternalOutput")
    else:
        pt_dbg = o_dbg = q_dbg = k_dbg = None

    xlnf_d = nc.dram_tensor("xlnf_d", [NDC, 128, S], BF16)
    wg_c = nc.dram_tensor("wg_c", [NIC // 2, 128, 2, NDC, 128], BF16)
    wu_c = nc.dram_tensor("wu_c", [NIC // 2, 128, 2, NDC, 128], BF16)
    wd_c = nc.dram_tensor("wd_c", [NDC, 4, 128, 16, 128], BF16)
    wq_c = nc.dram_tensor("wq_c", [NH, 128, NDC, 128], BF16)
    wk_c = nc.dram_tensor("wk_c", [NKV, 128, NDC, 128], BF16)
    wo_c = nc.dram_tensor("wo_c", [NDC, 128, NDC, 128], BF16)
    wv_c = nc.dram_tensor("wv_c", [128, NDC, 512], BF16)

    es = ExitStack()
    with es:
        sb = SB()
        sb.consts = es.enter_context(nc.sbuf_tensor("consts_sb", [128, 16], F32))
        sb.ones_bf = es.enter_context(nc.sbuf_tensor("ones_bf", [128, 1], BF16))
        sb.ones_row = es.enter_context(nc.sbuf_tensor("ones_row", [1, 128], F32))
        sb.ident = es.enter_context(nc.sbuf_tensor("ident_sb", [128, 128], F32))
        sb.sd = es.enter_context(nc.sbuf_tensor("sd_sb", [128, 3, 128], BF16))
        sb.swapm = es.enter_context(nc.sbuf_tensor("swap_sb", [128, 128], F32))
        for n in ("st_m", "st_e2", "st_m2", "st_rstd", "st_b", "st_r"):
            setattr(sb, n, es.enter_context(nc.sbuf_tensor(n, [1, 512], F32)))
        sb.xb = es.enter_context(nc.sbuf_tensor("xb_sb", [128, 2, 512], BF16))
        sb.x2b = es.enter_context(nc.sbuf_tensor("x2b_sb", [128, 2, 512], BF16))
        sb.bc_a = es.enter_context(nc.sbuf_tensor("bc_a", [128, 512], F32))
        sb.bc_b = es.enter_context(nc.sbuf_tensor("bc_b", [128, 512], F32))
        sb.t_norm = es.enter_context(nc.sbuf_tensor("t_norm", [128, 512], F32))
        sb.x_fm = es.enter_context(nc.sbuf_tensor("x_fm", [128, NDC, 512], F32))
        sb.xs = es.enter_context(nc.sbuf_tensor("xs", [128, 3, 512], F32))
        sb.xout = es.enter_context(nc.sbuf_tensor("xout", [128, 2, 512], F32))

        pbig = es.enter_context(nc.psum_tensor("pbig", [128, 8, 512], F32))
        pb = [pbig[:, i, :] for i in range(8)]

        P = Prog(nc, es)

        P.dma("gpsimd", lambda g: g.dma_start(sb.consts[:, :], consts_in[:, :]))
        P.dma("gpsimd", lambda g: g.dma_start(sb.ident[:, :], ident_in[:, :]))
        P.dma("gpsimd", lambda g: g.dma_start(
            sb.sd[:, :, :], sd_in.rearrange("k p f -> p k f")))
        m_const = P.dma("gpsimd", lambda g: g.dma_start(sb.swapm[:, :], swap_in[:, :]))
        m_ones = P.emit("vector", lambda v: v.memset(sb.ones_bf[:, :], 1.0),
                        waits=[m_const], inc=1)
        m_ones = P.emit("vector", lambda v: v.memset(sb.ones_row[:, :], 1.0), inc=1)
        P.m_ones = m_ones

        wg_src = wg_in.rearrange("(dc p) (ib icl f) -> ib p icl dc f", p=128, icl=2, f=128)
        wu_src = wu_in.rearrange("(dc p) (ib icl f) -> ib p icl dc f", p=128, icl=2, f=128)
        m_wg = [None] * (NIC // 2)
        m_wu = [None] * (NIC // 2)
        wd_src = wd_in.rearrange("(ibt ibl p) (dc f) -> dc ibt p ibl f", ibl=16, p=128, f=128)
        m_wd = [[None] * 4 for _ in range(NDC)]
        for b in range(NIC // 2):
            P.dma("gpsimd", lambda g, b=b: g.dma_start(wg_c[b][:, 0], wg_src[b][:, 0]))
            m_wg[b] = P.dma("gpsimd", lambda g, b=b: g.dma_start(wg_c[b][:, 1], wg_src[b][:, 1]))
            P.dma("gpsimd", lambda g, b=b: g.dma_start(wu_c[b][:, 0], wu_src[b][:, 0]))
            m_wu[b] = P.dma("gpsimd", lambda g, b=b: g.dma_start(wu_c[b][:, 1], wu_src[b][:, 1]))
            for j in (2 * b, 2 * b + 1):
                dc, ibt = j // 4, j % 4
                m_wd[dc][ibt] = P.dma("gpsimd", lambda g, dc=dc, ibt=ibt: g.dma_start(
                    wd_c[dc, ibt], wd_src[dc, ibt]))
        wq_src = wq_in.rearrange("(dc p) (h f) -> h p dc f", p=128, f=128)
        wk_src = wk_in.rearrange("(dc p) (h f) -> h p dc f", p=128, f=128)
        wo_src = wo_in.rearrange("(dc p) (h f) -> h p dc f", p=128, f=128)
        wv_src = wv_in.rearrange("(dc p) f -> p dc f", p=128)
        m_wk = [P.dma("gpsimd", lambda g, h=h: g.dma_start(wk_c[h], wk_src[h]))
                for h in range(NKV)]
        m_wv = P.dma("gpsimd", lambda g: g.dma_start(wv_c[:, :, :], wv_src))
        m_wq = [P.dma("gpsimd", lambda g, h=h: g.dma_start(wq_c[h], wq_src[h]))
                for h in range(NH)]
        m_wo = [P.dma("gpsimd", lambda g, h=h: g.dma_start(wo_c[h], wo_src[h]))
                for h in range(NDC)]

        P.emit("vector", None, waits=[m_const])
        P.emit("scalar", None, waits=[m_const])
        P.emit("tensor", None, waits=[m_const])

        xs_rr = [0]

        def stream_chunk(src_fm, cols):
            def load(c, phase=0):
                slot = xs_rr[0] % 3
                xs_rr[0] += 1
                m = P.dma("sync", lambda e, c=c, slot=slot: e.dma_start(
                    sb.xs[:, slot, :], src_fm[c, :, cols]),
                    waits=P.war(("xs", slot)))
                P.guard[("xs", slot)] = {}
                return (sb.xs[:, slot, :], [m], ("xs", slot))
            return load

        def emit_ffn(src_tm, src_fm, dst_fm, dst_tm, tag):
            esf = ExitStack()
            with esf:
                xln = esf.enter_context(nc.sbuf_tensor("xln_f_" + tag, [128, NDC, TSUP], BF16))
                h_sp = esf.enter_context(nc.sbuf_tensor("h_sp_" + tag, [128, NIC, TSUP], BF16))
                xtm = esf.enter_context(nc.sbuf_tensor("xtm_" + tag, [128, 2, D], F32))
                wsg = esf.enter_context(nc.sbuf_tensor("wsg_" + tag, [128, 2, NDC, 128], BF16))
                wsu = esf.enter_context(nc.sbuf_tensor("wsu_" + tag, [128, 2, NDC, 128], BF16))
                wsd = esf.enter_context(nc.sbuf_tensor("wsd_" + tag, [128, 2, 16, 128], BF16))
                g_sb = esf.enter_context(nc.sbuf_tensor("g_sb_" + tag, [128, 2, TSUP], F32))
                h_f32 = esf.enter_context(nc.sbuf_tensor("h_f32_" + tag, [128, 2, TSUP], F32))
                a_bf = esf.enter_context(nc.sbuf_tensor("a_bf_" + tag, [128, 2, TSUP], BF16))
                sp_c = esf.enter_context(nc.sbuf_tensor("sp_c_" + tag, [128, TSUP], BF16))

                for st in range(NSUP):
                    cols = slice(st * TSUP, (st + 1) * TSUP)
                    if src_tm is not None:
                        for tt in range(NTT):
                            par = tt % 2
                            row0 = st * TSUP + tt * 128
                            mld = P.dma("sync", lambda e, par=par, row0=row0: e.dma_start(
                                xtm[:, par, :], src_tm[row0:row0 + 128, :]),
                                waits=P.war(("xtm", par)))
                            P.guard[("xtm", par)] = {}
                            for grp in range(4):
                                bk = 6 + grp % 2
                                w = [mld] + (P.war("x_fm") if (tt, grp) == (0, 0) else []) \
                                    + P.war(("pb", bk))
                                for k in range(4):
                                    dc = grp * 4 + k
                                    mt = P.emit("tensor", lambda t, par=par, dc=dc, bk=bk, k=k: t.transpose(
                                        pb[bk][:, k * 128:(k + 1) * 128],
                                        xtm[:, par, dc * 128:(dc + 1) * 128], sb.ident[:, :]),
                                        waits=w if k == 0 else (), inc=1)
                                P.reads(("xtm", par), mt)
                                P.guard[("pb", bk)] = {}
                                dst_ap = sb.x_fm[:, grp * 4:(grp + 1) * 4, tt * 128:(tt + 1) * 128]
                                mcp = P.emit("vector", lambda v, bk=bk, dst_ap=dst_ap: v.tensor_copy(
                                    dst_ap, pb[bk].rearrange("p (k f) -> p k f", k=4)),
                                    waits=[mt], inc=1)
                                P.reads(("pb", bk), mcp)
                        if (st, 0) == (0, 0):
                            P.guard["x_fm"] = {}
                        m_xready = [mcp]
                        def loader(c, phase=0, m_xready=tuple(m_xready)):
                            return (sb.x_fm[:, c, :],
                                    list(m_xready) if c == 0 else [], "x_fm")
                    else:
                        loader = stream_chunk(src_fm, cols)

                    mln = emit_ln(P, pb, sb, loader, xln, "xln")[NDC - 1]

                    m_hsp_last = None
                    pend = None
                    for ic in range(NIC + 1):
                        if ic < NIC:
                            par = ic % 2
                            b, icl = ic // 2, ic % 2
                            bpar = ic % 2
                            mwg_l = P.dma("sync", lambda e, b=b, icl=icl, bpar=bpar: e.dma_start(
                                wsg[:, bpar], wg_c[b][:, icl]),
                                waits=[m_wg[b]] + P.war(("wsg", bpar)))
                            mwu_l = P.dma("sync", lambda e, b=b, icl=icl, bpar=bpar: e.dma_start(
                                wsu[:, bpar], wu_c[b][:, icl]),
                                waits=[m_wu[b]] + P.war(("wsu", bpar)))
                            P.guard[("wsg", bpar)] = {}
                            P.guard[("wsu", bpar)] = {}
                            gbk, ubk = 0 + par, 2 + par
                            w = [mwg_l, mln] + P.war(("pb", gbk))
                            for dc in range(NDC):
                                mg = P.emit("tensor", lambda t, icl=icl, bpar=bpar, dc=dc, gbk=gbk: t.matmul(
                                    pb[gbk][:, :], wsg[:, bpar, dc, :], xln[:, dc, :],
                                    start=(dc == 0), stop=(dc == NDC - 1)),
                                    waits=w if dc == 0 else (), inc=1)
                            P.guard[("pb", gbk)] = {}
                            P.reads(("wsg", bpar), mg)
                            P.reads("xln", mg)
                            w = [mwu_l] + P.war(("pb", ubk))
                            for dc in range(NDC):
                                mu = P.emit("tensor", lambda t, icl=icl, bpar=bpar, dc=dc, ubk=ubk: t.matmul(
                                    pb[ubk][:, :], wsu[:, bpar, dc, :], xln[:, dc, :],
                                    start=(dc == 0), stop=(dc == NDC - 1)),
                                    waits=w if dc == 0 else (), inc=1)
                            P.guard[("pb", ubk)] = {}
                            P.reads(("wsu", bpar), mu)
                            P.reads("xln", mu)
                            msl = P.emit("scalar", lambda s, par=par, gbk=gbk: s.activation(
                                g_sb[:, par, :], pb[gbk][:, :], AF.Silu),
                                waits=[mg] + P.war(("g_sb", par)), inc=1)
                            P.guard[("g_sb", par)] = {}
                            P.reads(("pb", gbk), msl)
                            mh = P.emit("vector", lambda v, par=par, ubk=ubk: v.tensor_tensor(
                                h_f32[:, par, :], pb[ubk][:, :], g_sb[:, par, :], op=ALU.mult),
                                waits=[msl, mu] + P.war(("h_f32", par)), inc=1)
                            P.guard[("h_f32", par)] = {}
                            P.reads(("pb", ubk), mh)
                            P.reads(("g_sb", par), mh)
                            mab = P.emit("scalar", lambda s, par=par: s.activation(
                                a_bf[:, par, :], h_f32[:, par, :], AF.Abs),
                                waits=[mh] + P.war(("a_bf", par)), inc=1)
                            P.guard[("a_bf", par)] = {}
                            P.reads(("h_f32", par), mab)
                            this = (ic, par, mab)
                        if pend is not None:
                            pic, ppar, pmab = pend
                            w = [pmab] + P.war(("pb", 4)) + P.war(("pb", 5)) + P.war(("pb", 6))
                            for k in range(3):
                                msh = P.emit("tensor", lambda t, k=k, ppar=ppar: t.matmul(
                                    pb[4 + k][:, :], sb.sd[:, k, :], a_bf[:, ppar, :],
                                    start=True, stop=True), waits=w if k == 0 else (), inc=1)
                            P.guard[("pb", 4)] = {}
                            P.guard[("pb", 5)] = {}
                            P.guard[("pb", 6)] = {}
                            P.reads(("a_bf", ppar), msh)
                            cs = sb.consts
                            mg1 = P.emit("vector", lambda v: v.tensor_scalar(
                                sp_c[:, :], pb[4][:, :], cs[:, C_TAU1:C_TAU1 + 1], None,
                                op0=ALU.is_ge), waits=[msh], inc=1)
                            P.reads(("pb", 4), mg1)
                            mg2 = P.emit("vector", lambda v: v.scalar_tensor_tensor(
                                sp_c[:, :], pb[5][:, :], cs[:, C_TAU2:C_TAU2 + 1],
                                sp_c[:, :], op0=ALU.is_ge, op1=ALU.add), inc=1)
                            P.reads(("pb", 5), mg2)
                            mg3 = P.emit("vector", lambda v: v.scalar_tensor_tensor(
                                sp_c[:, :], pb[6][:, :], cs[:, C_TAU3:C_TAU3 + 1],
                                sp_c[:, :], op0=ALU.is_ge, op1=ALU.add), inc=1)
                            P.reads(("pb", 6), mg3)
                            m_hsp_last = P.emit("vector", lambda v, ppar=ppar, pic=pic: v.scalar_tensor_tensor(
                                h_sp[:, pic, :], sp_c[:, :], cs[:, C_15:C_15 + 1],
                                h_f32[:, ppar, :], op0=ALU.is_le, op1=ALU.mult),
                                waits=P.war("h_sp") if pic == 0 else (), inc=1)
                            if pic == 0:
                                P.guard["h_sp"] = {}
                            P.reads(("h_f32", ppar), m_hsp_last)
                        if ic < NIC:
                            pend = this

                    loader_b = (stream_chunk(src_fm, cols) if src_fm is not None else None)
                    for dout in range(NDC):
                        ybk = 0 + dout % 2
                        md = None
                        for ibt in range(4):
                            bpar = (dout * 4 + ibt) % 2
                            mwd_l = P.dma("sync", lambda e, dout=dout, ibt=ibt, bpar=bpar: e.dma_start(
                                wsd[:, bpar], wd_c[dout, ibt]),
                                waits=[m_wd[dout][ibt]] + P.war(("wsd", bpar)))
                            P.guard[("wsd", bpar)] = {}
                            w = [mwd_l]
                            if ibt == 0:
                                w += [m_hsp_last] + P.war(("pb", ybk))
                            for ibl in range(16):
                                ic = ibt * 16 + ibl
                                md = P.emit("tensor", lambda t, bpar=bpar, ibl=ibl, ic=ic, ybk=ybk: t.matmul(
                                    pb[ybk][:, :], wsd[:, bpar, ibl, :], h_sp[:, ic, :],
                                    start=(ic == 0), stop=(ic == NIC - 1)),
                                    waits=w if ibl == 0 else (), inc=1)
                            if ibt == 0:
                                P.guard[("pb", ybk)] = {}
                            P.reads(("wsd", bpar), md)
                            P.reads("h_sp", md)
                        opar = dout % 2
                        if dst_fm is not None:
                            mev = P.emit("vector", lambda v, opar=opar, dout=dout, ybk=ybk: v.scalar_tensor_tensor(
                                sb.xout[:, opar, :], pb[ybk][:, :], sb.consts[:, C_HALF:C_HALF + 1],
                                sb.x_fm[:, dout, :], op0=ALU.mult, op1=ALU.add),
                                waits=[md] + P.war(("xout", opar)), inc=1)
                            P.guard[("xout", opar)] = {}
                            P.reads(("pb", ybk), mev)
                            P.reads("x_fm", mev)
                            mst = P.dma("gpsimd", lambda g, opar=opar, dout=dout, cols=cols: g.dma_start(
                                dst_fm[dout, :, cols], sb.xout[:, opar, :]),
                                waits=[mev])
                            P.reads(("xout", opar), mst)
                        else:
                            xap, xw, xkey = loader_b(dout)
                            mev = P.emit("vector", lambda v, dout=dout, ybk=ybk, xap=xap: v.scalar_tensor_tensor(
                                sb.x_fm[:, dout, :], pb[ybk][:, :], sb.consts[:, C_HALF:C_HALF + 1],
                                xap, op0=ALU.mult, op1=ALU.add),
                                waits=[md] + list(xw) + (P.war("x_fm") if dout == 0 else []), inc=1)
                            if dout == 0:
                                P.guard["x_fm"] = {}
                            P.reads(("pb", ybk), mev)
                            P.reads(xkey, mev)
                            P.reads("x_fm", mev)
                            m_evac_last = mev

                    if dst_tm is not None:
                        for tt in range(NTT):
                            par = tt % 2
                            row0 = st * TSUP + tt * 128
                            for grp in range(4):
                                bk = 6 + grp % 2
                                w = [m_evac_last] + P.war(("pb", bk)) if grp < 2 else P.war(("pb", bk))
                                for k in range(4):
                                    dc = grp * 4 + k
                                    mt = P.emit("tensor", lambda t, dc=dc, bk=bk, k=k, tt=tt: t.transpose(
                                        pb[bk][:, k * 128:(k + 1) * 128],
                                        sb.x_fm[:, dc, tt * 128:(tt + 1) * 128], sb.ident[:, :]),
                                        waits=w if k == 0 else (), inc=1)
                                P.reads("x_fm", mt)
                                P.guard[("pb", bk)] = {}
                                dst_ap = xtm[:, par, grp * 512:(grp + 1) * 512]
                                mcp = P.emit("vector", lambda v, bk=bk, dst_ap=dst_ap: v.tensor_copy(
                                    dst_ap, pb[bk][:, :]),
                                    waits=[mt] + (P.war(("xtm", par)) if grp == 0 else []), inc=1)
                                if grp == 0:
                                    P.guard[("xtm", par)] = {}
                                P.reads(("pb", bk), mcp)
                            mst = P.dma("gpsimd", lambda g, par=par, row0=row0: g.dma_start(
                                dst_tm[row0:row0 + 128, :], xtm[:, par, :]),
                                waits=[mcp])
                            P.reads(("xtm", par), mst)
                P.fence()

        def emit_attn(src_fm, dst_fm):
            esf = ExitStack()
            with esf:
                xlnb = esf.enter_context(nc.sbuf_tensor("xlnb", [128, NDC, 512], BF16))
                kr = esf.enter_context(nc.sbuf_tensor("kr", [128, NKV, S], BF16))
                v_tm = esf.enter_context(nc.sbuf_tensor("v_tm", [128, NDC, 512], BF16))
                ct_sb = esf.enter_context(nc.sbuf_tensor("ct_sb", [128, 512], F32))
                st_sb = esf.enter_context(nc.sbuf_tensor("st_sb", [128, 512], F32))
                q_qg = esf.enter_context(nc.sbuf_tensor("q_qg", [128, NH, QG], BF16))
                pt = esf.enter_context(nc.sbuf_tensor("pt", [128, NDC, QG], BF16))
                o_sb = esf.enter_context(nc.sbuf_tensor("o_sb", [128, NH, QG], BF16))
                wsq = esf.enter_context(nc.sbuf_tensor("wsq", [128, 2, NDC, 128], BF16))
                wv_sb = esf.enter_context(nc.sbuf_tensor("wv_sb", [128, NDC, 512], BF16))
                q_f32 = esf.enter_context(nc.sbuf_tensor("q_f32", [128, 2, 512], F32))
                t_rope = esf.enter_context(nc.sbuf_tensor("t_rope", [128, 2, 512], F32))
                rbc = esf.enter_context(nc.sbuf_tensor("rbc", [128, 512], F32))
                cs = sb.consts

                mwv_l = P.dma("sync", lambda e: e.dma_start(wv_sb[:, :, :], wv_c[:, :, :]),
                               waits=[m_wv])

                def load_tabs(ccols):
                    m1 = P.dma("sync", lambda e, ccols=ccols: e.dma_start(
                        ct_sb[:, :], ctab_in[:, ccols]), waits=P.war("tabs"))
                    m2 = P.dma("sync", lambda e, ccols=ccols: e.dma_start(
                        st_sb[:, :], stab_in[:, ccols]), waits=P.war("tabs"))
                    P.guard["tabs"] = {}
                    return [m1, m2]

                def rope(src_bank, par2, ccols, dst_ap, scaled, extra_war_key, ready, mtab):
                    mev = P.emit("vector", lambda v, par2=par2, src_bank=src_bank: v.tensor_copy(
                        q_f32[:, par2, :], pb[src_bank][:, :]),
                        waits=[ready] + P.war(("q_f32", par2)), inc=1)
                    P.guard[("q_f32", par2)] = {}
                    P.reads(("pb", src_bank), mev)
                    msw = P.emit("tensor", lambda t, par2=par2: t.matmul(
                        pb[2 + par2][:, :], sb.swapm[:, :], q_f32[:, par2, :],
                        start=True, stop=True),
                        waits=[mev] + P.war(("pb", 2 + par2)), inc=1)
                    P.guard[("pb", 2 + par2)] = {}
                    P.reads(("q_f32", par2), msw)
                    if scaled:
                        mr1 = P.emit("vector", lambda v, par2=par2, ccols=ccols: v.scalar_tensor_tensor(
                            t_rope[:, par2, :], q_f32[:, par2, :], cs[:, C_ISQHD:C_ISQHD + 1],
                            ct_sb[:, :], op0=ALU.mult, op1=ALU.mult),
                            waits=mtab + [msw] + P.war(("t_rope", par2)), inc=1)
                        mr2 = P.emit("vector", lambda v, par2=par2, ccols=ccols: v.scalar_tensor_tensor(
                            q_f32[:, par2, :], pb[2 + par2][:, :], cs[:, C_ISQHD:C_ISQHD + 1],
                            st_sb[:, :], op0=ALU.mult, op1=ALU.mult), inc=1)
                    else:
                        mr1 = P.emit("vector", lambda v, par2=par2: v.tensor_tensor(
                            t_rope[:, par2, :], q_f32[:, par2, :], ct_sb[:, :], op=ALU.mult),
                            waits=mtab + [msw] + P.war(("t_rope", par2)), inc=1)
                        mr2 = P.emit("vector", lambda v, par2=par2: v.tensor_tensor(
                            q_f32[:, par2, :], pb[2 + par2][:, :], st_sb[:, :], op=ALU.mult),
                            inc=1)
                    P.guard[("t_rope", par2)] = {}
                    P.guard[("q_f32", par2)] = {}
                    P.reads("tabs", mr1)
                    P.reads("tabs", mr2)
                    P.reads(("pb", 2 + par2), mr2)
                    mr3 = P.emit("vector", lambda v, par2=par2, dst_ap=dst_ap: v.tensor_tensor(
                        dst_ap, t_rope[:, par2, :], q_f32[:, par2, :], op=ALU.add),
                        waits=P.war(extra_war_key), inc=1)
                    P.guard[extra_war_key] = {}
                    P.reads(("t_rope", par2), mr3)
                    P.reads(("q_f32", par2), mr3)
                    return mr3

                m_kr = None
                m_v = None
                m_xst = [None] * NSUP
                for st in range(NSUP):
                    cols = slice(st * TSUP, (st + 1) * TSUP)
                    mln = emit_ln(P, pb, sb, stream_chunk(src_fm, cols), xlnb, "xlnf")[NDC - 1]
                    m_xst[st] = P.dma("gpsimd", lambda g, cols=cols: g.dma_start(
                        xlnf_d[:, :, cols].rearrange("c p n -> p c n"), xlnb[:, :, :]),
                        waits=[mln])
                    P.reads("xlnf", m_xst[st])
                    mtab_st = load_tabs(cols)
                    for kv in range(NKV):
                        wpar = kv % 2
                        mwl = P.dma("sync", lambda e, kv=kv, wpar=wpar: e.dma_start(
                            wsq[:, wpar], wk_c[kv]),
                            waits=[m_wk[kv]] + P.war(("wsq", wpar)))
                        P.guard[("wsq", wpar)] = {}
                        bk = kv % 2
                        w = [mwl, mln] + P.war(("pb", bk))
                        for dc in range(NDC):
                            mk = P.emit("tensor", lambda t, wpar=wpar, dc=dc, bk=bk: t.matmul(
                                pb[bk][:, :], wsq[:, wpar, dc, :], xlnb[:, dc, :],
                                start=(dc == 0), stop=(dc == NDC - 1)),
                                waits=w if dc == 0 else (), inc=1)
                        P.guard[("pb", bk)] = {}
                        P.reads(("wsq", wpar), mk)
                        P.reads("xlnf", mk)
                        m_kr = rope(bk, kv % 2, cols, kr[:, kv, cols], False, "kr", mk, mtab_st)
                    for ttl in range(NTT):
                        tc = st * NTT + ttl
                        vbk = 4 + ttl % 2
                        w = [mwv_l, mln] + P.war(("pb", vbk))
                        for dc in range(NDC):
                            mv = P.emit("tensor", lambda t, ttl=ttl, dc=dc, vbk=vbk: t.matmul(
                                pb[vbk][:, :], xlnb[:, dc, ttl * 128:(ttl + 1) * 128],
                                wv_sb[:, dc, :],
                                start=(dc == 0), stop=(dc == NDC - 1)),
                                waits=w if dc == 0 else (), inc=1)
                        P.guard[("pb", vbk)] = {}
                        P.reads("xlnf", mv)
                        m_v = P.emit("vector", lambda v, tc=tc, vbk=vbk: v.tensor_copy(
                            v_tm[:, tc, :], pb[vbk][:, :]),
                            waits=[mv] + (P.war("v_tm") if tc == 0 else []), inc=1)
                        if tc == 0:
                            P.guard["v_tm"] = {}
                        P.reads(("pb", vbk), m_v)

                for qg in range(NQG):
                    ccols = slice(qg * QG, (qg + 1) * QG)
                    mxq = P.dma("sync", lambda e, ccols=ccols: e.dma_start(
                        xlnb[:, :, :], xlnf_d[:, :, ccols].rearrange("c p n -> p c n")),
                        waits=[m_xst[qg]] + P.war("xlnf"))
                    P.guard["xlnf"] = {}
                    m_qr = None
                    mtab_qg = load_tabs(ccols)
                    for h in range(NH):
                        wpar = h % 2
                        mwl = P.dma("sync", lambda e, h=h, wpar=wpar: e.dma_start(
                            wsq[:, wpar], wq_c[h]),
                            waits=[m_wq[h]] + P.war(("wsq", wpar)))
                        P.guard[("wsq", wpar)] = {}
                        bk = h % 2
                        w = [mwl, mxq] + P.war(("pb", bk))
                        for dc in range(NDC):
                            mq = P.emit("tensor", lambda t, wpar=wpar, dc=dc, bk=bk: t.matmul(
                                pb[bk][:, :], wsq[:, wpar, dc, :], xlnb[:, dc, :],
                                start=(dc == 0), stop=(dc == NDC - 1)),
                                waits=w if dc == 0 else (), inc=1)
                        P.guard[("pb", bk)] = {}
                        P.reads(("wsq", wpar), mq)
                        P.reads("xlnf", mq)
                        m_qr = rope(bk, h % 2, ccols, q_qg[:, h, :], True, ("q_qg", h), mq, mtab_qg)

                    for h in range(NH):
                        kvh = h % NKV
                        obk = 5 + h % 2
                        m_exp_pr = [None] * 8
                        msum = None
                        mo = None
                        pend_pr = None

                        def emit_pair_consumers(pr):
                            nonlocal msum, mo
                            kc0 = 2 * pr
                            w = [m_exp_pr[pr]]
                            if pr == 0:
                                w += [P.m_ones] + P.war(("pb", 4))
                            for kc in (kc0, kc0 + 1):
                                msum_ = P.emit("tensor", lambda t, kc=kc: t.matmul(
                                    pb[4][0:1, :], sb.ones_bf[:, :], pt[:, kc, :],
                                    start=(kc == 0), stop=(kc == NDC - 1)),
                                    waits=w if kc == kc0 else (), inc=1)
                            if pr == 0:
                                P.guard[("pb", 4)] = {}
                            w2 = []
                            if pr == 0:
                                w2 = [m_v] + P.war(("pb", obk))
                            for kc in (kc0, kc0 + 1):
                                mo_ = P.emit("tensor", lambda t, kc=kc, kvh=kvh, obk=obk: t.matmul(
                                    pb[obk][:, :], v_tm[:, kc, kvh * 128:(kvh + 1) * 128], pt[:, kc, :],
                                    start=(kc == 0), stop=(kc == NDC - 1)),
                                    waits=w2 if kc == kc0 else (), inc=1)
                            if pr == 0:
                                P.guard[("pb", obk)] = {}
                            P.reads("pt", mo_)
                            msum, mo = msum_, mo_

                        for pr in range(8):
                            pbase = (pr % 2) * 2
                            w = [m_qr, m_kr] if pr == 0 else []
                            w = w + P.war(("pb", pbase)) + P.war(("pb", pbase + 1))
                            for j in (0, 1):
                                kc = 2 * pr + j
                                msmm = P.emit("tensor", lambda t, kvh=kvh, kc=kc, bk=pbase + j, h=h: t.matmul(
                                    pb[bk][:, :], kr[:, kvh, kc * 128:(kc + 1) * 128], q_qg[:, h, :],
                                    start=True, stop=True), waits=w if j == 0 else (), inc=1)
                            P.guard[("pb", pbase)] = {}
                            P.guard[("pb", pbase + 1)] = {}
                            P.reads(("q_qg", h), msmm)
                            P.reads("kr", msmm)
                            w2 = [msmm]
                            if pr == 0:
                                w2 += P.war("pt")
                                P.guard["pt"] = {}
                            m_exp_pr[pr] = P.emit("scalar", lambda s, pr=pr, pbase=pbase: s.activation(
                                pt[:, 2 * pr:2 * pr + 2, :], pbig[:, pbase:pbase + 2, :],
                                AF.Exp, bias=np.float32(-EXP_SHIFT)),
                                waits=w2, inc=1)
                            P.reads(("pb", pbase), m_exp_pr[pr])
                            P.reads(("pb", pbase + 1), m_exp_pr[pr])
                            if pend_pr is not None:
                                emit_pair_consumers(pend_pr)
                            pend_pr = pr
                        emit_pair_consumers(pend_pr)
                        if debug_streams and qg == 0 and h == 0:
                            mdump = P.dma("gpsimd", lambda g: g.dma_start(
                                pt_dbg[:, :, :], pt[:, :, :]), waits=[mo])
                            P.reads("pt", mdump)
                        mr = P.emit("vector", lambda v: v.reciprocal(
                            sb.st_r[:, :], pb[4][0:1, :]),
                            waits=[msum] + P.war("st_r"), inc=1)
                        P.guard["st_r"] = {}
                        P.reads(("pb", 4), mr)
                        mb = P.emit("tensor", lambda t: t.matmul(
                            pb[7][:, :], sb.ones_row[:1, :], sb.st_r[:, :],
                            start=True, stop=True),
                            waits=[mr] + P.war(("pb", 7)), inc=1)
                        P.guard[("pb", 7)] = {}
                        P.reads("st_r", mb)
                        mbc = P.emit("vector", lambda v: v.tensor_copy(
                            rbc[:, :], pb[7][:, :]),
                            waits=[mb] + P.war("rbc"), inc=1)
                        P.guard["rbc"] = {}
                        P.reads(("pb", 7), mbc)
                        moev = P.emit("vector", lambda v, h=h, obk=obk: v.tensor_tensor(
                            o_sb[:, h, :], pb[obk][:, :], rbc[:, :], op=ALU.mult),
                            waits=[mo, mbc] + (P.war("o_sb") if h == 0 else []), inc=1)
                        if h == 0:
                            P.guard["o_sb"] = {}
                        P.reads(("pb", obk), moev)
                        P.reads("rbc", moev)
                        m_oev = moev

                    if debug_streams and qg == 0:
                        P.dma("gpsimd", lambda g: g.dma_start(o_dbg[:, :, :], o_sb[:, :, :]),
                              waits=[m_oev])
                        P.dma("gpsimd", lambda g: g.dma_start(q_dbg[:, :, :], q_qg[:, :, :]),
                              waits=[m_qr])
                        P.dma("gpsimd", lambda g: g.dma_start(k_dbg[:, :, :], kr[:, :, :]),
                              waits=[m_kr])
                    mxl = P.dma("sync", lambda e, ccols=ccols: e.dma_start(
                        sb.x_fm[:, :, :], src_fm[:, :, ccols].rearrange("c p n -> p c n")),
                        waits=P.war("x_fm"))
                    P.guard["x_fm"] = {}
                    for dout in range(NDC):
                        wpar = dout % 2
                        mwl = P.dma("sync", lambda e, dout=dout, wpar=wpar: e.dma_start(
                            wsq[:, wpar], wo_c[dout]),
                            waits=[m_wo[dout]] + P.war(("wsq", wpar)))
                        P.guard[("wsq", wpar)] = {}
                        ybk = 2 + dout % 2
                        w = [mwl, m_oev] + P.war(("pb", ybk))
                        my = None
                        for dinc in range(NDC):
                            my = P.emit("tensor", lambda t, dout=dout, dinc=dinc, wpar=wpar, ybk=ybk: t.matmul(
                                pb[ybk][:, :], wsq[:, wpar, dinc, :], o_sb[:, dinc, :],
                                start=(dinc == 0), stop=(dinc == NDC - 1)),
                                waits=w if dinc == 0 else (), inc=1)
                        P.guard[("pb", ybk)] = {}
                        P.reads(("wsq", wpar), my)
                        P.reads("o_sb", my)
                        opar = dout % 2
                        mev = P.emit("vector", lambda v, opar=opar, dout=dout, ybk=ybk: v.tensor_tensor(
                            sb.xout[:, opar, :], pb[ybk][:, :], sb.x_fm[:, dout, :], op=ALU.add),
                            waits=[my, mxl] + P.war(("xout", opar)), inc=1)
                        P.guard[("xout", opar)] = {}
                        P.reads(("pb", ybk), mev)
                        P.reads("x_fm", mev)
                        mst = P.dma("gpsimd", lambda g, opar=opar, dout=dout, ccols=ccols: g.dma_start(
                            dst_fm[dout, :, ccols], sb.xout[:, opar, :]),
                            waits=[mev])
                        P.reads(("xout", opar), mst)
                P.fence()

        emit_ffn(src_tm=x_in, src_fm=None, dst_fm=s1, dst_tm=None, tag="f1")
        emit_attn(src_fm=s1, dst_fm=s2)
        emit_ffn(src_tm=None, src_fm=s2, dst_fm=None, dst_tm=out, tag="f2")

        P.emit("gpsimd", None, waits=[(f"gpsimd_d{i}", P.counts[f"gpsimd_d{i}"])
                                      for i in range(P.NDMASEM)])

        block = es.enter_context(nc.Block())

        @block.sync
        def _(e):
            for f in P.steps["sync"]:
                f(e)

        @block.gpsimd
        def _(e):
            for f in P.steps["gpsimd"]:
                f(e)

        @block.tensor
        def _(e):
            for f in P.steps["tensor"]:
                f(e)

        @block.vector
        def _(e):
            for f in P.steps["vector"]:
                f(e)

        @block.scalar
        def _(e):
            for f in P.steps["scalar"]:
                f(e)

    return nc


def host_constants():
    TINY = np.float32(2.0 ** -126)
    consts = np.zeros((128, 16), dtype=np.float32)
    p = np.arange(128)
    for k in (1, 2, 3):
        consts[:, k - 1] = np.where((p % 4) + k < 4, TINY, 0.0)
    consts[:, C_15] = 1.5
    consts[:, C_INV2048] = 1.0 / 2048.0
    consts[:, C_EPS] = 1e-5
    consts[:, C_NEG1] = -1.0
    consts[:, C_HALF] = 0.5
    consts[:, C_ISQHD] = 1.0 / np.sqrt(np.float32(HD))
    consts[:, C_ONE] = 1.0
    ident = np.eye(128, dtype=np.float32)
    sd = np.zeros((3, 128, 128), dtype=np.float32)
    for k in (1, 2, 3):
        for c in range(128):
            r = 4 * (c // 4) + (c + k) % 4
            sd[k - 1, r, c] += 1.0
            sd[k - 1, c, c] -= 1.0
    swapm = np.zeros((128, 128), dtype=np.float32)
    for c in range(128):
        r = c + 1 if c % 2 == 0 else c - 1
        swapm[r, c] = 1.0
    half = HD // 2
    inv = 1.0 / 10000.0 ** (np.arange(half, dtype=np.float64) / half)
    pos = np.arange(S, dtype=np.float64)
    ang = pos[None, :] * inv[:, None]
    ctab = np.repeat(np.cos(ang), 2, axis=0).astype(np.float32)
    sin2 = np.repeat(np.sin(ang), 2, axis=0)
    sign = np.where(np.arange(128) % 2 == 0, -1.0, 1.0)
    stab = (sin2 * sign[:, None]).astype(np.float32)
    return consts, ident, sd, swapm, ctab, stab


def kernel(L0_attn_x, L0_attn_W_q, L0_attn_W_k, L0_attn_W_v, L0_attn_W_o,
           L0_ffn_w_gate, L0_ffn_w_up, L0_ffn_w_down):
    x = np.ascontiguousarray(np.asarray(L0_attn_x, dtype=np.float32))
    consts, ident, sd, swapm, ctab, stab = host_constants()
    common = {
        "wq": np.ascontiguousarray(np.asarray(L0_attn_W_q, dtype=np.float32)),
        "wk": np.ascontiguousarray(np.asarray(L0_attn_W_k, dtype=np.float32)),
        "wv": np.ascontiguousarray(np.asarray(L0_attn_W_v, dtype=np.float32)),
        "wo": np.ascontiguousarray(np.asarray(L0_attn_W_o, dtype=np.float32)),
        "wg": np.ascontiguousarray(np.asarray(L0_ffn_w_gate, dtype=np.float32)),
        "wu": np.ascontiguousarray(np.asarray(L0_ffn_w_up, dtype=np.float32)),
        "wd": np.ascontiguousarray(np.asarray(L0_ffn_w_down, dtype=np.float32)),
        "consts": consts, "ident": ident, "sd": sd, "swapm": swapm,
        "ctab": ctab, "stab": stab,
    }
    nc = build()
    in_maps = [dict(common, x=np.ascontiguousarray(x[i])) for i in range(8)]
    res = run_bass_kernel_spmd(nc, in_maps, core_ids=list(range(8)))
    return np.stack([res.results[i]["out"] for i in range(8)], axis=0)


# version 09
# speedup vs baseline: 1.0285x; 1.0285x over previous
import numpy as np
import concourse.bass as bass
import concourse.mybir as mybir
from concourse.bass_utils import run_bass_kernel_spmd
from contextlib import ExitStack

F32 = mybir.dt.float32
BF16 = mybir.dt.bfloat16
AF = mybir.ActivationFunctionType
ALU = mybir.AluOpType

D = 2048
S = 2048
II = 8192
NH, NKV, HD = 16, 4, 128
NDC = D // 128
NIC = II // 128
TSUP = 512
NSUP = S // TSUP
QG = 512
NQG = S // QG
NTT = TSUP // 128

C_TAU1, C_TAU2, C_TAU3, C_15, C_INV2048, C_EPS, C_NEG1, C_HALF, C_ISQHD, C_ONE = range(10)
EXP_SHIFT = 10.0

ENGS = ("sync", "gpsimd", "tensor", "vector", "scalar")


class Prog:

    NDMASEM = 8

    def __init__(self, nc, es):
        self.nc = nc
        self.steps = {e: [] for e in ENGS}
        self.sems = {e: es.enter_context(nc.semaphore("sem_" + e)) for e in ENGS}
        for q in ("sync", "gpsimd"):
            for i in range(self.NDMASEM):
                n = f"{q}_d{i}"
                self.sems[n] = es.enter_context(nc.semaphore("sem_" + n))
        self.counts = {e: 0 for e in self.sems}
        self.rr = {"sync": 0, "gpsimd": 0}
        self.guard = {}

    def emit(self, eng, fn, waits=(), inc=0):
        sems = self.sems
        wl = []
        for w in waits:
            if w is None:
                continue
            en, val = w
            if val > 0:
                wl.append((en, val))
        wl = tuple(wl)

        def step(e, fn=fn, wl=wl, inc=inc, sem=self.sems[eng]):
            for (en, val) in wl:
                e.wait_ge(sems[en], val)
            ins = fn(e) if fn is not None else None
            if inc:
                ins.then_inc(sem, inc)

        self.steps[eng].append(step)
        if inc:
            self.counts[eng] += inc
        return (eng, self.counts[eng])

    def dma(self, queue, fn, waits=()):
        i = self.rr[queue] % self.NDMASEM
        self.rr[queue] += 1
        sname = f"{queue}_d{i}"
        w = list(waits)
        if self.counts[sname] > 0:
            w.append((sname, self.counts[sname]))
        sems = self.sems
        wl = []
        for x in w:
            if x is None:
                continue
            en, val = x
            if val > 0:
                wl.append((en, val))
        wl = tuple(wl)

        def step(e, fn=fn, wl=wl, sem=self.sems[sname]):
            for (en, val) in wl:
                e.wait_ge(sems[en], val)
            fn(e).then_inc(sem, 16)

        self.steps[queue].append(step)
        self.counts[sname] += 16
        return (sname, self.counts[sname])

    def reads(self, key, milestone):
        d = self.guard.setdefault(key, {})
        d[milestone[0]] = max(d.get(milestone[0], 0), milestone[1])

    def war(self, key):
        d = self.guard.get(key, {})
        return [(e, v) for e, v in d.items()]

    def fence(self):
        snap = dict(self.counts)
        for e in ENGS:
            self.emit(e, None, waits=[(o, v) for o, v in snap.items() if o != e])


class SB:
    pass


def emit_ln(P, pb, sb, load_chunk, xln_dst, xln_key):
    cs = sb.consts
    m_sum = [None] * NDC
    for c in range(NDC):
        par = c % 2
        ap, w, key = load_chunk(c, 0)
        w = list(w) + P.war(("lncast", par))
        mcast = P.emit("scalar", lambda s, ap=ap, par=par: s.activation(
            sb.xb[:, par, :], ap, AF.Copy), waits=w, inc=1)
        msq = P.emit("scalar", lambda s, ap=ap, par=par: s.activation(
            sb.x2b[:, par, :], ap, AF.Square), inc=1)
        P.reads(key, mcast)
        P.reads(key, msq)
        w = [msq]
        if c == 0:
            w += P.war(("pb", 4)) + P.war(("pb", 5)) + [P.m_ones]
        P.emit("tensor", lambda t, c=c, par=par: t.matmul(
            pb[4][0:1, :], sb.ones_bf[:, :], sb.xb[:, par, :],
            start=(c == 0), stop=(c == NDC - 1)), waits=w, inc=1)
        m_sum[c] = P.emit("tensor", lambda t, c=c, par=par: t.matmul(
            pb[5][0:1, :], sb.ones_bf[:, :], sb.x2b[:, par, :],
            start=(c == 0), stop=(c == NDC - 1)), inc=1)
        P.guard[("lncast", par)] = {}
        P.reads(("lncast", par), m_sum[c])
    m1 = P.emit("vector", lambda v: v.tensor_scalar(
        sb.st_m[:, :], pb[4][0:1, :], cs[:1, C_INV2048:C_INV2048 + 1], None,
        op0=ALU.mult), waits=[m_sum[NDC - 1]], inc=1)
    m2 = P.emit("vector", lambda v: v.tensor_scalar(
        sb.st_e2[:, :], pb[5][0:1, :], cs[:1, C_INV2048:C_INV2048 + 1],
        cs[:1, C_EPS:C_EPS + 1], op0=ALU.mult, op1=ALU.add), inc=1)
    P.guard[("pb", 4)] = {}
    P.guard[("pb", 5)] = {}
    P.reads(("pb", 4), m1)
    P.reads(("pb", 5), m2)
    m3 = P.emit("vector", lambda v: v.tensor_tensor(
        sb.st_m2[:, :], sb.st_m[:, :], sb.st_m[:, :], op=ALU.mult), inc=1)
    m4 = P.emit("vector", lambda v: v.tensor_tensor(
        sb.st_e2[:, :], sb.st_e2[:, :], sb.st_m2[:, :], op=ALU.subtract), inc=1)
    m5 = P.emit("scalar", lambda s: s.activation(
        sb.st_m2[:, :], sb.st_e2[:, :], AF.Sqrt), waits=[m4], inc=1)
    m6 = P.emit("vector", lambda v: v.reciprocal(
        sb.st_rstd[:, :], sb.st_m2[:, :]),
        waits=[m5] + P.war("st_rstd"), inc=1)
    m7 = P.emit("vector", lambda v: v.scalar_tensor_tensor(
        sb.st_b[:, :], sb.st_m[:, :], cs[:1, C_NEG1:C_NEG1 + 1],
        sb.st_rstd[:, :], op0=ALU.mult, op1=ALU.mult),
        waits=P.war("st_b"), inc=1)
    mb1 = P.emit("tensor", lambda t: t.matmul(
        pb[4][:, :], sb.ones_row[:1, :], sb.st_rstd[:, :],
        start=True, stop=True), waits=[m6, m1] + P.war(("pb", 4)), inc=1)
    mb2 = P.emit("tensor", lambda t: t.matmul(
        pb[5][:, :], sb.ones_row[:1, :], sb.st_b[:, :],
        start=True, stop=True), waits=[m7, m2] + P.war(("pb", 5)), inc=1)
    P.guard["st_rstd"] = {}
    P.guard["st_b"] = {}
    P.reads("st_rstd", mb1)
    P.reads("st_b", mb2)
    mc1 = P.emit("vector", lambda v: v.tensor_copy(
        sb.bc_a[:, :], pb[4][:, :]), waits=[mb1], inc=1)
    mc2 = P.emit("vector", lambda v: v.tensor_copy(
        sb.bc_b[:, :], pb[5][:, :]), waits=[mb2], inc=1)
    P.guard[("pb", 4)] = {}
    P.guard[("pb", 5)] = {}
    P.reads(("pb", 4), mc1)
    P.reads(("pb", 5), mc2)
    mnorm = [None] * NDC
    for c in range(NDC):
        ap, w2, key2 = load_chunk(c, 1)
        w = list(w2) + ([mc2] if c == 0 else []) + (P.war(xln_key) if c == 0 else [])
        mn1 = P.emit("vector", lambda v, ap=ap: v.tensor_tensor(
            sb.t_norm[:, :], ap, sb.bc_a[:, :], op=ALU.mult),
            waits=w, inc=1)
        P.reads(key2, mn1)
        mnorm[c] = P.emit("vector", lambda v, c=c: v.tensor_tensor(
            xln_dst[:, c, :], sb.t_norm[:, :], sb.bc_b[:, :], op=ALU.add), inc=1)
    P.guard[xln_key] = {}
    return mnorm


def build(debug_streams=False):
    nc = bass.Bass("TRN2", target_bir_lowering=False)

    x_in = nc.dram_tensor("x", [S, D], F32, kind="ExternalInput")
    wq_in = nc.dram_tensor("wq", [D, D], F32, kind="ExternalInput")
    wk_in = nc.dram_tensor("wk", [D, 512], F32, kind="ExternalInput")
    wv_in = nc.dram_tensor("wv", [D, 512], F32, kind="ExternalInput")
    wo_in = nc.dram_tensor("wo", [D, D], F32, kind="ExternalInput")
    wg_in = nc.dram_tensor("wg", [D, II], F32, kind="ExternalInput")
    wu_in = nc.dram_tensor("wu", [D, II], F32, kind="ExternalInput")
    wd_in = nc.dram_tensor("wd", [II, D], F32, kind="ExternalInput")
    consts_in = nc.dram_tensor("consts", [128, 16], F32, kind="ExternalInput")
    ident_in = nc.dram_tensor("ident", [128, 128], F32, kind="ExternalInput")
    sd_in = nc.dram_tensor("sd", [3, 128, 128], F32, kind="ExternalInput")
    swap_in = nc.dram_tensor("swapm", [128, 128], F32, kind="ExternalInput")
    ctab_in = nc.dram_tensor("ctab", [128, S], F32, kind="ExternalInput")
    stab_in = nc.dram_tensor("stab", [128, S], F32, kind="ExternalInput")
    out = nc.dram_tensor("out", [S, D], F32, kind="ExternalOutput")

    skind = "ExternalOutput" if debug_streams else "Internal"
    s1 = nc.dram_tensor("s1", [NDC, 128, S], F32, kind=skind)
    s2 = nc.dram_tensor("s2", [NDC, 128, S], F32, kind=skind)
    if debug_streams:
        pt_dbg = nc.dram_tensor("pt_dbg", [128, NDC, QG], F32, kind="ExternalOutput")
        o_dbg = nc.dram_tensor("o_dbg", [128, NH, QG], F32, kind="ExternalOutput")
        q_dbg = nc.dram_tensor("q_dbg", [128, NH, QG], F32, kind="ExternalOutput")
        k_dbg = nc.dram_tensor("k_dbg", [128, NKV, S], F32, kind="ExternalOutput")
    else:
        pt_dbg = o_dbg = q_dbg = k_dbg = None

    xlnf_d = nc.dram_tensor("xlnf_d", [NDC, 128, S], BF16, kind=skind)
    wk_dbg = nc.dram_tensor("wk_c_dbg", [NKV, 128, NDC, 128], BF16) if False else None
    wg_c = nc.dram_tensor("wg_c", [NIC // 2, 128, 2, NDC, 128], BF16)
    wu_c = nc.dram_tensor("wu_c", [NIC // 2, 128, 2, NDC, 128], BF16)
    wd_c = nc.dram_tensor("wd_c", [NDC, 4, 128, 16, 128], BF16)
    wq_c = nc.dram_tensor("wq_c", [NH, 128, NDC, 128], BF16)
    wk_c = nc.dram_tensor("wk_c", [NKV, 128, NDC, 128], BF16)
    wo_c = nc.dram_tensor("wo_c", [NDC, 128, NDC, 128], BF16)
    wv_c = nc.dram_tensor("wv_c", [128, NDC, 512], BF16)

    es = ExitStack()
    with es:
        sb = SB()
        sb.consts = es.enter_context(nc.sbuf_tensor("consts_sb", [128, 16], F32))
        sb.ones_bf = es.enter_context(nc.sbuf_tensor("ones_bf", [128, 1], BF16))
        sb.ones_row = es.enter_context(nc.sbuf_tensor("ones_row", [1, 128], F32))
        sb.ident = es.enter_context(nc.sbuf_tensor("ident_sb", [128, 128], F32))
        sb.sd = es.enter_context(nc.sbuf_tensor("sd_sb", [128, 3, 128], BF16))
        sb.swapm = es.enter_context(nc.sbuf_tensor("swap_sb", [128, 128], F32))
        for n in ("st_m", "st_e2", "st_m2", "st_rstd", "st_b", "st_r"):
            setattr(sb, n, es.enter_context(nc.sbuf_tensor(n, [1, 512], F32)))
        sb.xb = es.enter_context(nc.sbuf_tensor("xb_sb", [128, 2, 512], BF16))
        sb.x2b = es.enter_context(nc.sbuf_tensor("x2b_sb", [128, 2, 512], BF16))
        sb.bc_a = es.enter_context(nc.sbuf_tensor("bc_a", [128, 512], F32))
        sb.bc_b = es.enter_context(nc.sbuf_tensor("bc_b", [128, 512], F32))
        sb.t_norm = es.enter_context(nc.sbuf_tensor("t_norm", [128, 512], F32))
        sb.x_fm = es.enter_context(nc.sbuf_tensor("x_fm", [128, NDC, 512], F32))
        sb.xs = es.enter_context(nc.sbuf_tensor("xs", [128, 3, 512], F32))
        sb.xout = es.enter_context(nc.sbuf_tensor("xout", [128, 2, 512], F32))

        pbig = es.enter_context(nc.psum_tensor("pbig", [128, 8, 512], F32))
        pb = [pbig[:, i, :] for i in range(8)]

        P = Prog(nc, es)

        P.dma("gpsimd", lambda g: g.dma_start(sb.consts[:, :], consts_in[:, :]))
        P.dma("gpsimd", lambda g: g.dma_start(sb.ident[:, :], ident_in[:, :]))
        P.dma("gpsimd", lambda g: g.dma_start(
            sb.sd[:, :, :], sd_in.rearrange("k p f -> p k f")))
        m_const = P.dma("gpsimd", lambda g: g.dma_start(sb.swapm[:, :], swap_in[:, :]))
        m_ones = P.emit("vector", lambda v: v.memset(sb.ones_bf[:, :], 1.0),
                        waits=[m_const], inc=1)
        m_ones = P.emit("vector", lambda v: v.memset(sb.ones_row[:, :], 1.0), inc=1)
        P.m_ones = m_ones

        wg_src = wg_in.rearrange("(dc p) (ib icl f) -> ib p icl dc f", p=128, icl=2, f=128)
        wu_src = wu_in.rearrange("(dc p) (ib icl f) -> ib p icl dc f", p=128, icl=2, f=128)
        m_wg = [None] * (NIC // 2)
        m_wu = [None] * (NIC // 2)
        wd_src = wd_in.rearrange("(ibt ibl p) (dc f) -> dc ibt p ibl f", ibl=16, p=128, f=128)
        m_wd = [[None] * 4 for _ in range(NDC)]
        wq_src = wq_in.rearrange("(dc p) (h f) -> h p dc f", p=128, f=128)
        wk_src = wk_in.rearrange("(dc p) (h f) -> h p dc f", p=128, f=128)
        wo_src = wo_in.rearrange("(dc p) (h f) -> h p dc f", p=128, f=128)
        wv_src = wv_in.rearrange("(dc p) f -> p dc f", p=128)
        m_wk = [None] * NKV
        m_wq = [None] * NH
        m_wo = [None] * NDC
        m_wv = None

        def emit_attn_w_casts():
            nonlocal m_wv
            for h in range(NKV):
                m_wk[h] = P.dma("gpsimd", lambda g, h=h: g.dma_start(wk_c[h], wk_src[h]))
            m_wv = P.dma("gpsimd", lambda g: g.dma_start(wv_c[:, :, :], wv_src))
            for h in range(NH):
                m_wq[h] = P.dma("gpsimd", lambda g, h=h: g.dma_start(wq_c[h], wq_src[h]))
            for h in range(NDC):
                m_wo[h] = P.dma("gpsimd", lambda g, h=h: g.dma_start(wo_c[h], wo_src[h]))

        P.emit("vector", None, waits=[m_const])
        P.emit("scalar", None, waits=[m_const])
        P.emit("tensor", None, waits=[m_const])

        xs_rr = [0]

        def stream_chunk(src_fm, cols):
            def load(c, phase=0):
                slot = xs_rr[0] % 3
                xs_rr[0] += 1
                m = P.dma("sync", lambda e, c=c, slot=slot: e.dma_start(
                    sb.xs[:, slot, :], src_fm[c, :, cols]),
                    waits=P.war(("xs", slot)))
                P.guard[("xs", slot)] = {}
                return (sb.xs[:, slot, :], [m], ("xs", slot))
            return load

        def emit_ffn(src_tm, src_fm, dst_fm, dst_tm, tag):
            esf = ExitStack()
            with esf:
                xln = esf.enter_context(nc.sbuf_tensor("xln_f_" + tag, [128, NDC, TSUP], BF16))
                h_sp = esf.enter_context(nc.sbuf_tensor("h_sp_" + tag, [128, NIC, TSUP], BF16))
                xtm = esf.enter_context(nc.sbuf_tensor("xtm_" + tag, [128, 2, D], F32))
                wsg = esf.enter_context(nc.sbuf_tensor("wsg_" + tag, [128, 2, NDC, 128], BF16))
                wsu = esf.enter_context(nc.sbuf_tensor("wsu_" + tag, [128, 2, NDC, 128], BF16))
                wsd = esf.enter_context(nc.sbuf_tensor("wsd_" + tag, [128, 2, 16, 128], BF16))
                g_sb = esf.enter_context(nc.sbuf_tensor("g_sb_" + tag, [128, 2, TSUP], F32))
                h_f32 = esf.enter_context(nc.sbuf_tensor("h_f32_" + tag, [128, 2, TSUP], F32))
                a_bf = esf.enter_context(nc.sbuf_tensor("a_bf_" + tag, [128, 2, TSUP], BF16))
                sp_c = esf.enter_context(nc.sbuf_tensor("sp_c_" + tag, [128, TSUP], BF16))

                for st in range(NSUP):
                    first_sweep = (tag == "f1" and st == 0)
                    cols = slice(st * TSUP, (st + 1) * TSUP)
                    if src_tm is not None:
                        for tt in range(NTT):
                            par = tt % 2
                            row0 = st * TSUP + tt * 128
                            mld = P.dma("sync", lambda e, par=par, row0=row0: e.dma_start(
                                xtm[:, par, :], src_tm[row0:row0 + 128, :]),
                                waits=P.war(("xtm", par)))
                            P.guard[("xtm", par)] = {}
                            for grp in range(4):
                                bk = 6 + grp % 2
                                w = [mld] + (P.war("x_fm") if (tt, grp) == (0, 0) else []) \
                                    + P.war(("pb", bk))
                                for k in range(4):
                                    dc = grp * 4 + k
                                    mt = P.emit("tensor", lambda t, par=par, dc=dc, bk=bk, k=k: t.transpose(
                                        pb[bk][:, k * 128:(k + 1) * 128],
                                        xtm[:, par, dc * 128:(dc + 1) * 128], sb.ident[:, :]),
                                        waits=w if k == 0 else (), inc=1)
                                P.reads(("xtm", par), mt)
                                P.guard[("pb", bk)] = {}
                                dst_ap = sb.x_fm[:, grp * 4:(grp + 1) * 4, tt * 128:(tt + 1) * 128]
                                mcp = P.emit("vector", lambda v, bk=bk, dst_ap=dst_ap: v.tensor_copy(
                                    dst_ap, pb[bk].rearrange("p (k f) -> p k f", k=4)),
                                    waits=[mt], inc=1)
                                P.reads(("pb", bk), mcp)
                        if (st, 0) == (0, 0):
                            P.guard["x_fm"] = {}
                        m_xready = [mcp]
                        def loader(c, phase=0, m_xready=tuple(m_xready)):
                            return (sb.x_fm[:, c, :],
                                    list(m_xready) if c == 0 else [], "x_fm")
                    else:
                        loader = stream_chunk(src_fm, cols)

                    mln = emit_ln(P, pb, sb, loader, xln, "xln")[NDC - 1]

                    m_hsp_last = None
                    pend = None
                    for ic in range(NIC + 1):
                        if ic < NIC:
                            par = ic % 2
                            b, icl = ic // 2, ic % 2
                            bpar = ic % 2
                            if first_sweep:
                                mwg_l = P.dma("gpsimd", lambda e, b=b, icl=icl, bpar=bpar: e.dma_start(
                                    wsg[:, bpar], wg_src[b][:, icl]),
                                    waits=P.war(("wsg", bpar)))
                                P.guard[("wsg", bpar)] = {}
                                m_wg[b] = P.dma("gpsimd", lambda e, b=b, icl=icl, bpar=bpar: e.dma_start(
                                    wg_c[b][:, icl], wsg[:, bpar]), waits=[mwg_l])
                                P.reads(("wsg", bpar), m_wg[b])
                                mwu_l = P.dma("gpsimd", lambda e, b=b, icl=icl, bpar=bpar: e.dma_start(
                                    wsu[:, bpar], wu_src[b][:, icl]),
                                    waits=P.war(("wsu", bpar)))
                                P.guard[("wsu", bpar)] = {}
                                m_wu[b] = P.dma("gpsimd", lambda e, b=b, icl=icl, bpar=bpar: e.dma_start(
                                    wu_c[b][:, icl], wsu[:, bpar]), waits=[mwu_l])
                                P.reads(("wsu", bpar), m_wu[b])
                            else:
                                mwg_l = P.dma("sync", lambda e, b=b, icl=icl, bpar=bpar: e.dma_start(
                                    wsg[:, bpar], wg_c[b][:, icl]),
                                    waits=[m_wg[b]] + P.war(("wsg", bpar)))
                                mwu_l = P.dma("sync", lambda e, b=b, icl=icl, bpar=bpar: e.dma_start(
                                    wsu[:, bpar], wu_c[b][:, icl]),
                                    waits=[m_wu[b]] + P.war(("wsu", bpar)))
                                P.guard[("wsg", bpar)] = {}
                                P.guard[("wsu", bpar)] = {}
                            gbk, ubk = 0 + par, 2 + par
                            w = [mwg_l, mln] + P.war(("pb", gbk))
                            for dc in range(NDC):
                                mg = P.emit("tensor", lambda t, icl=icl, bpar=bpar, dc=dc, gbk=gbk: t.matmul(
                                    pb[gbk][:, :], wsg[:, bpar, dc, :], xln[:, dc, :],
                                    start=(dc == 0), stop=(dc == NDC - 1)),
                                    waits=w if dc == 0 else (), inc=1)
                            P.guard[("pb", gbk)] = {}
                            P.reads(("wsg", bpar), mg)
                            P.reads("xln", mg)
                            w = [mwu_l] + P.war(("pb", ubk))
                            for dc in range(NDC):
                                mu = P.emit("tensor", lambda t, icl=icl, bpar=bpar, dc=dc, ubk=ubk: t.matmul(
                                    pb[ubk][:, :], wsu[:, bpar, dc, :], xln[:, dc, :],
                                    start=(dc == 0), stop=(dc == NDC - 1)),
                                    waits=w if dc == 0 else (), inc=1)
                            P.guard[("pb", ubk)] = {}
                            P.reads(("wsu", bpar), mu)
                            P.reads("xln", mu)
                            msl = P.emit("scalar", lambda s, par=par, gbk=gbk: s.activation(
                                g_sb[:, par, :], pb[gbk][:, :], AF.Silu),
                                waits=[mg] + P.war(("g_sb", par)), inc=1)
                            P.guard[("g_sb", par)] = {}
                            P.reads(("pb", gbk), msl)
                            mh = P.emit("vector", lambda v, par=par, ubk=ubk: v.tensor_tensor(
                                h_f32[:, par, :], pb[ubk][:, :], g_sb[:, par, :], op=ALU.mult),
                                waits=[msl, mu] + P.war(("h_f32", par)), inc=1)
                            P.guard[("h_f32", par)] = {}
                            P.reads(("pb", ubk), mh)
                            P.reads(("g_sb", par), mh)
                            mab = P.emit("scalar", lambda s, par=par: s.activation(
                                a_bf[:, par, :], h_f32[:, par, :], AF.Abs),
                                waits=[mh] + P.war(("a_bf", par)), inc=1)
                            P.guard[("a_bf", par)] = {}
                            P.reads(("h_f32", par), mab)
                            this = (ic, par, mab)
                        if pend is not None:
                            pic, ppar, pmab = pend
                            w = [pmab] + P.war(("pb", 4)) + P.war(("pb", 5)) + P.war(("pb", 6))
                            for k in range(3):
                                msh = P.emit("tensor", lambda t, k=k, ppar=ppar: t.matmul(
                                    pb[4 + k][:, :], sb.sd[:, k, :], a_bf[:, ppar, :],
                                    start=True, stop=True), waits=w if k == 0 else (), inc=1)
                            P.guard[("pb", 4)] = {}
                            P.guard[("pb", 5)] = {}
                            P.guard[("pb", 6)] = {}
                            P.reads(("a_bf", ppar), msh)
                            cs = sb.consts
                            mg1 = P.emit("vector", lambda v: v.tensor_scalar(
                                sp_c[:, :], pb[4][:, :], cs[:, C_TAU1:C_TAU1 + 1], None,
                                op0=ALU.is_ge), waits=[msh], inc=1)
                            P.reads(("pb", 4), mg1)
                            mg2 = P.emit("vector", lambda v: v.scalar_tensor_tensor(
                                sp_c[:, :], pb[5][:, :], cs[:, C_TAU2:C_TAU2 + 1],
                                sp_c[:, :], op0=ALU.is_ge, op1=ALU.add), inc=1)
                            P.reads(("pb", 5), mg2)
                            mg3 = P.emit("vector", lambda v: v.scalar_tensor_tensor(
                                sp_c[:, :], pb[6][:, :], cs[:, C_TAU3:C_TAU3 + 1],
                                sp_c[:, :], op0=ALU.is_ge, op1=ALU.add), inc=1)
                            P.reads(("pb", 6), mg3)
                            m_hsp_last = P.emit("vector", lambda v, ppar=ppar, pic=pic: v.scalar_tensor_tensor(
                                h_sp[:, pic, :], sp_c[:, :], cs[:, C_15:C_15 + 1],
                                h_f32[:, ppar, :], op0=ALU.is_le, op1=ALU.mult),
                                waits=P.war("h_sp") if pic == 0 else (), inc=1)
                            if pic == 0:
                                P.guard["h_sp"] = {}
                            P.reads(("h_f32", ppar), m_hsp_last)
                        if ic < NIC:
                            pend = this

                    loader_b = (stream_chunk(src_fm, cols) if src_fm is not None else None)
                    for dout in range(NDC):
                        ybk = 0 + dout % 2
                        md = None
                        for ibt in range(4):
                            bpar = (dout * 4 + ibt) % 2
                            if first_sweep:
                                mwd_l = P.dma("gpsimd", lambda e, dout=dout, ibt=ibt, bpar=bpar: e.dma_start(
                                    wsd[:, bpar], wd_src[dout, ibt]),
                                    waits=P.war(("wsd", bpar)))
                                P.guard[("wsd", bpar)] = {}
                                m_wd[dout][ibt] = P.dma("gpsimd", lambda e, dout=dout, ibt=ibt, bpar=bpar: e.dma_start(
                                    wd_c[dout, ibt], wsd[:, bpar]), waits=[mwd_l])
                                P.reads(("wsd", bpar), m_wd[dout][ibt])
                            else:
                                mwd_l = P.dma("sync", lambda e, dout=dout, ibt=ibt, bpar=bpar: e.dma_start(
                                    wsd[:, bpar], wd_c[dout, ibt]),
                                    waits=[m_wd[dout][ibt]] + P.war(("wsd", bpar)))
                                P.guard[("wsd", bpar)] = {}
                            w = [mwd_l]
                            if ibt == 0:
                                w += [m_hsp_last] + P.war(("pb", ybk))
                            for ibl in range(16):
                                ic = ibt * 16 + ibl
                                md = P.emit("tensor", lambda t, bpar=bpar, ibl=ibl, ic=ic, ybk=ybk: t.matmul(
                                    pb[ybk][:, :], wsd[:, bpar, ibl, :], h_sp[:, ic, :],
                                    start=(ic == 0), stop=(ic == NIC - 1)),
                                    waits=w if ibl == 0 else (), inc=1)
                            if ibt == 0:
                                P.guard[("pb", ybk)] = {}
                            P.reads(("wsd", bpar), md)
                            P.reads("h_sp", md)
                        opar = dout % 2
                        if dst_fm is not None:
                            mev = P.emit("vector", lambda v, opar=opar, dout=dout, ybk=ybk: v.scalar_tensor_tensor(
                                sb.xout[:, opar, :], pb[ybk][:, :], sb.consts[:, C_HALF:C_HALF + 1],
                                sb.x_fm[:, dout, :], op0=ALU.mult, op1=ALU.add),
                                waits=[md] + P.war(("xout", opar)), inc=1)
                            P.guard[("xout", opar)] = {}
                            P.reads(("pb", ybk), mev)
                            P.reads("x_fm", mev)
                            mst = P.dma("gpsimd", lambda g, opar=opar, dout=dout, cols=cols: g.dma_start(
                                dst_fm[dout, :, cols], sb.xout[:, opar, :]),
                                waits=[mev])
                            P.reads(("xout", opar), mst)
                        else:
                            xap, xw, xkey = loader_b(dout)
                            mev = P.emit("vector", lambda v, dout=dout, ybk=ybk, xap=xap: v.scalar_tensor_tensor(
                                sb.x_fm[:, dout, :], pb[ybk][:, :], sb.consts[:, C_HALF:C_HALF + 1],
                                xap, op0=ALU.mult, op1=ALU.add),
                                waits=[md] + list(xw) + (P.war("x_fm") if dout == 0 else []), inc=1)
                            if dout == 0:
                                P.guard["x_fm"] = {}
                            P.reads(("pb", ybk), mev)
                            P.reads(xkey, mev)
                            P.reads("x_fm", mev)
                            m_evac_last = mev

                    if first_sweep:
                        emit_attn_w_casts()
                    if dst_tm is not None:
                        for tt in range(NTT):
                            par = tt % 2
                            row0 = st * TSUP + tt * 128
                            for grp in range(4):
                                bk = 6 + grp % 2
                                w = [m_evac_last] + P.war(("pb", bk)) if grp < 2 else P.war(("pb", bk))
                                for k in range(4):
                                    dc = grp * 4 + k
                                    mt = P.emit("tensor", lambda t, dc=dc, bk=bk, k=k, tt=tt: t.transpose(
                                        pb[bk][:, k * 128:(k + 1) * 128],
                                        sb.x_fm[:, dc, tt * 128:(tt + 1) * 128], sb.ident[:, :]),
                                        waits=w if k == 0 else (), inc=1)
                                P.reads("x_fm", mt)
                                P.guard[("pb", bk)] = {}
                                dst_ap = xtm[:, par, grp * 512:(grp + 1) * 512]
                                mcp = P.emit("vector", lambda v, bk=bk, dst_ap=dst_ap: v.tensor_copy(
                                    dst_ap, pb[bk][:, :]),
                                    waits=[mt] + (P.war(("xtm", par)) if grp == 0 else []), inc=1)
                                if grp == 0:
                                    P.guard[("xtm", par)] = {}
                                P.reads(("pb", bk), mcp)
                            mst = P.dma("gpsimd", lambda g, par=par, row0=row0: g.dma_start(
                                dst_tm[row0:row0 + 128, :], xtm[:, par, :]),
                                waits=[mcp])
                            P.reads(("xtm", par), mst)
                P.fence()

        def emit_attn(src_fm, dst_fm):
            esf = ExitStack()
            with esf:
                xlnb = esf.enter_context(nc.sbuf_tensor("xlnb", [128, NDC, 512], BF16))
                kr = esf.enter_context(nc.sbuf_tensor("kr", [128, NKV, S], BF16))
                v_tm = esf.enter_context(nc.sbuf_tensor("v_tm", [128, NDC, 512], BF16))
                ct_sb = esf.enter_context(nc.sbuf_tensor("ct_sb", [128, 512], F32))
                st_sb = esf.enter_context(nc.sbuf_tensor("st_sb", [128, 512], F32))
                q_qg = esf.enter_context(nc.sbuf_tensor("q_qg", [128, NH, QG], BF16))
                pt = esf.enter_context(nc.sbuf_tensor("pt", [128, NDC, QG], BF16))
                o_sb = esf.enter_context(nc.sbuf_tensor("o_sb", [128, NH, QG], BF16))
                wsq = esf.enter_context(nc.sbuf_tensor("wsq", [128, 2, NDC, 128], BF16))
                wv_sb = esf.enter_context(nc.sbuf_tensor("wv_sb", [128, NDC, 512], BF16))
                q_f32 = esf.enter_context(nc.sbuf_tensor("q_f32", [128, 2, 512], F32))
                t_rope = esf.enter_context(nc.sbuf_tensor("t_rope", [128, 2, 512], F32))
                rbc = esf.enter_context(nc.sbuf_tensor("rbc", [128, 512], F32))
                cs = sb.consts

                mwv_l = P.dma("sync", lambda e: e.dma_start(wv_sb[:, :, :], wv_c[:, :, :]),
                               waits=[m_wv])

                def load_tabs(ccols):
                    m1 = P.dma("sync", lambda e, ccols=ccols: e.dma_start(
                        ct_sb[:, :], ctab_in[:, ccols]), waits=P.war("tabs"))
                    m2 = P.dma("sync", lambda e, ccols=ccols: e.dma_start(
                        st_sb[:, :], stab_in[:, ccols]), waits=P.war("tabs"))
                    P.guard["tabs"] = {}
                    return [m1, m2]

                def rope(src_bank, par2, ccols, dst_ap, scaled, extra_war_key, ready, mtab):
                    mev = P.emit("vector", lambda v, par2=par2, src_bank=src_bank: v.tensor_copy(
                        q_f32[:, par2, :], pb[src_bank][:, :]),
                        waits=[ready] + P.war(("q_f32", par2)), inc=1)
                    P.guard[("q_f32", par2)] = {}
                    P.reads(("pb", src_bank), mev)
                    msw = P.emit("tensor", lambda t, par2=par2: t.matmul(
                        pb[2 + par2][:, :], sb.swapm[:, :], q_f32[:, par2, :],
                        start=True, stop=True),
                        waits=[mev] + P.war(("pb", 2 + par2)), inc=1)
                    P.guard[("pb", 2 + par2)] = {}
                    P.reads(("q_f32", par2), msw)
                    if scaled:
                        mr1 = P.emit("vector", lambda v, par2=par2, ccols=ccols: v.scalar_tensor_tensor(
                            t_rope[:, par2, :], q_f32[:, par2, :], cs[:, C_ISQHD:C_ISQHD + 1],
                            ct_sb[:, :], op0=ALU.mult, op1=ALU.mult),
                            waits=mtab + [msw] + P.war(("t_rope", par2)), inc=1)
                        mr2 = P.emit("vector", lambda v, par2=par2, ccols=ccols: v.scalar_tensor_tensor(
                            q_f32[:, par2, :], pb[2 + par2][:, :], cs[:, C_ISQHD:C_ISQHD + 1],
                            st_sb[:, :], op0=ALU.mult, op1=ALU.mult), inc=1)
                    else:
                        mr1 = P.emit("vector", lambda v, par2=par2: v.tensor_tensor(
                            t_rope[:, par2, :], q_f32[:, par2, :], ct_sb[:, :], op=ALU.mult),
                            waits=mtab + [msw] + P.war(("t_rope", par2)), inc=1)
                        mr2 = P.emit("vector", lambda v, par2=par2: v.tensor_tensor(
                            q_f32[:, par2, :], pb[2 + par2][:, :], st_sb[:, :], op=ALU.mult),
                            inc=1)
                    P.guard[("t_rope", par2)] = {}
                    P.guard[("q_f32", par2)] = {}
                    P.reads("tabs", mr1)
                    P.reads("tabs", mr2)
                    P.reads(("pb", 2 + par2), mr2)
                    mr3 = P.emit("vector", lambda v, par2=par2, dst_ap=dst_ap: v.tensor_tensor(
                        dst_ap, t_rope[:, par2, :], q_f32[:, par2, :], op=ALU.add),
                        waits=P.war(extra_war_key), inc=1)
                    P.guard[extra_war_key] = {}
                    P.reads(("t_rope", par2), mr3)
                    P.reads(("q_f32", par2), mr3)
                    return mr3

                m_kr = None
                m_v = None
                m_xst = [None] * NSUP
                for st in range(NSUP):
                    cols = slice(st * TSUP, (st + 1) * TSUP)
                    mln = emit_ln(P, pb, sb, stream_chunk(src_fm, cols), xlnb, "xlnf")[NDC - 1]
                    m_xst[st] = P.dma("gpsimd", lambda g, cols=cols: g.dma_start(
                        xlnf_d[:, :, cols].rearrange("c p n -> p c n"), xlnb[:, :, :]),
                        waits=[mln])
                    P.reads("xlnf", m_xst[st])
                    mtab_st = load_tabs(cols)
                    for kv in range(NKV):
                        wpar = kv % 2
                        mwl = P.dma("sync", lambda e, kv=kv, wpar=wpar: e.dma_start(
                            wsq[:, wpar], wk_c[kv]),
                            waits=[m_wk[kv]] + P.war(("wsq", wpar)))
                        P.guard[("wsq", wpar)] = {}
                        bk = kv % 2
                        w = [mwl, mln] + P.war(("pb", bk))
                        for dc in range(NDC):
                            mk = P.emit("tensor", lambda t, wpar=wpar, dc=dc, bk=bk: t.matmul(
                                pb[bk][:, :], wsq[:, wpar, dc, :], xlnb[:, dc, :],
                                start=(dc == 0), stop=(dc == NDC - 1)),
                                waits=w if dc == 0 else (), inc=1)
                        P.guard[("pb", bk)] = {}
                        P.reads(("wsq", wpar), mk)
                        P.reads("xlnf", mk)
                        m_kr = rope(bk, kv % 2, cols, kr[:, kv, cols], False, "kr", mk, mtab_st)
                    for ttl in range(NTT):
                        tc = st * NTT + ttl
                        vbk = 4 + ttl % 2
                        w = [mwv_l, mln] + P.war(("pb", vbk))
                        for dc in range(NDC):
                            mv = P.emit("tensor", lambda t, ttl=ttl, dc=dc, vbk=vbk: t.matmul(
                                pb[vbk][:, :], xlnb[:, dc, ttl * 128:(ttl + 1) * 128],
                                wv_sb[:, dc, :],
                                start=(dc == 0), stop=(dc == NDC - 1)),
                                waits=w if dc == 0 else (), inc=1)
                        P.guard[("pb", vbk)] = {}
                        P.reads("xlnf", mv)
                        m_v = P.emit("vector", lambda v, tc=tc, vbk=vbk: v.tensor_copy(
                            v_tm[:, tc, :], pb[vbk][:, :]),
                            waits=[mv] + (P.war("v_tm") if tc == 0 else []), inc=1)
                        if tc == 0:
                            P.guard["v_tm"] = {}
                        P.reads(("pb", vbk), m_v)

                for qg in range(NQG):
                    ccols = slice(qg * QG, (qg + 1) * QG)
                    mxq = P.dma("sync", lambda e, ccols=ccols: e.dma_start(
                        xlnb[:, :, :], xlnf_d[:, :, ccols].rearrange("c p n -> p c n")),
                        waits=[m_xst[qg]] + P.war("xlnf"))
                    P.guard["xlnf"] = {}
                    m_qr = None
                    mtab_qg = load_tabs(ccols)
                    for h in range(NH):
                        wpar = h % 2
                        mwl = P.dma("sync", lambda e, h=h, wpar=wpar: e.dma_start(
                            wsq[:, wpar], wq_c[h]),
                            waits=[m_wq[h]] + P.war(("wsq", wpar)))
                        P.guard[("wsq", wpar)] = {}
                        bk = h % 2
                        w = [mwl, mxq] + P.war(("pb", bk))
                        for dc in range(NDC):
                            mq = P.emit("tensor", lambda t, wpar=wpar, dc=dc, bk=bk: t.matmul(
                                pb[bk][:, :], wsq[:, wpar, dc, :], xlnb[:, dc, :],
                                start=(dc == 0), stop=(dc == NDC - 1)),
                                waits=w if dc == 0 else (), inc=1)
                        P.guard[("pb", bk)] = {}
                        P.reads(("wsq", wpar), mq)
                        P.reads("xlnf", mq)
                        m_qr = rope(bk, h % 2, ccols, q_qg[:, h, :], True, ("q_qg", h), mq, mtab_qg)

                    for h in range(NH):
                        kvh = h % NKV
                        obk = 5 + h % 2
                        m_exp_pr = [None] * 8
                        msum = None
                        mo = None
                        pend_pr = None

                        def emit_pair_consumers(pr):
                            nonlocal msum, mo
                            kc0 = 2 * pr
                            w = [m_exp_pr[pr]]
                            if pr == 0:
                                w += [P.m_ones] + P.war(("pb", 4))
                            for kc in (kc0, kc0 + 1):
                                msum_ = P.emit("tensor", lambda t, kc=kc: t.matmul(
                                    pb[4][0:1, :], sb.ones_bf[:, :], pt[:, kc, :],
                                    start=(kc == 0), stop=(kc == NDC - 1)),
                                    waits=w if kc == kc0 else (), inc=1)
                            if pr == 0:
                                P.guard[("pb", 4)] = {}
                            w2 = []
                            if pr == 0:
                                w2 = [m_v] + P.war(("pb", obk))
                            for kc in (kc0, kc0 + 1):
                                mo_ = P.emit("tensor", lambda t, kc=kc, kvh=kvh, obk=obk: t.matmul(
                                    pb[obk][:, :], v_tm[:, kc, kvh * 128:(kvh + 1) * 128], pt[:, kc, :],
                                    start=(kc == 0), stop=(kc == NDC - 1)),
                                    waits=w2 if kc == kc0 else (), inc=1)
                            if pr == 0:
                                P.guard[("pb", obk)] = {}
                            P.reads("pt", mo_)
                            msum, mo = msum_, mo_

                        for pr in range(8):
                            pbase = (pr % 2) * 2
                            w = [m_qr, m_kr] if pr == 0 else []
                            w = w + P.war(("pb", pbase)) + P.war(("pb", pbase + 1))
                            for j in (0, 1):
                                kc = 2 * pr + j
                                msmm = P.emit("tensor", lambda t, kvh=kvh, kc=kc, bk=pbase + j, h=h: t.matmul(
                                    pb[bk][:, :], kr[:, kvh, kc * 128:(kc + 1) * 128], q_qg[:, h, :],
                                    start=True, stop=True), waits=w if j == 0 else (), inc=1)
                            P.guard[("pb", pbase)] = {}
                            P.guard[("pb", pbase + 1)] = {}
                            P.reads(("q_qg", h), msmm)
                            P.reads("kr", msmm)
                            w2 = [msmm]
                            if pr == 0:
                                w2 += P.war("pt")
                                P.guard["pt"] = {}
                            m_exp_pr[pr] = P.emit("scalar", lambda s, pr=pr, pbase=pbase: s.activation(
                                pt[:, 2 * pr:2 * pr + 2, :], pbig[:, pbase:pbase + 2, :],
                                AF.Exp, bias=np.float32(-EXP_SHIFT)),
                                waits=w2, inc=1)
                            P.reads(("pb", pbase), m_exp_pr[pr])
                            P.reads(("pb", pbase + 1), m_exp_pr[pr])
                            if pend_pr is not None:
                                emit_pair_consumers(pend_pr)
                            pend_pr = pr
                        emit_pair_consumers(pend_pr)
                        if debug_streams and qg == 0 and h == 0:
                            mdump = P.dma("gpsimd", lambda g: g.dma_start(
                                pt_dbg[:, :, :], pt[:, :, :]), waits=[mo])
                            P.reads("pt", mdump)
                        mr = P.emit("vector", lambda v: v.reciprocal(
                            sb.st_r[:, :], pb[4][0:1, :]),
                            waits=[msum] + P.war("st_r"), inc=1)
                        P.guard["st_r"] = {}
                        P.reads(("pb", 4), mr)
                        mb = P.emit("tensor", lambda t: t.matmul(
                            pb[7][:, :], sb.ones_row[:1, :], sb.st_r[:, :],
                            start=True, stop=True),
                            waits=[mr] + P.war(("pb", 7)), inc=1)
                        P.guard[("pb", 7)] = {}
                        P.reads("st_r", mb)
                        mbc = P.emit("vector", lambda v: v.tensor_copy(
                            rbc[:, :], pb[7][:, :]),
                            waits=[mb] + P.war("rbc"), inc=1)
                        P.guard["rbc"] = {}
                        P.reads(("pb", 7), mbc)
                        moev = P.emit("vector", lambda v, h=h, obk=obk: v.tensor_tensor(
                            o_sb[:, h, :], pb[obk][:, :], rbc[:, :], op=ALU.mult),
                            waits=[mo, mbc] + (P.war("o_sb") if h == 0 else []), inc=1)
                        if h == 0:
                            P.guard["o_sb"] = {}
                        P.reads(("pb", obk), moev)
                        P.reads("rbc", moev)
                        m_oev = moev

                    if debug_streams and qg == 0:
                        P.dma("gpsimd", lambda g: g.dma_start(o_dbg[:, :, :], o_sb[:, :, :]),
                              waits=[m_oev])
                        P.dma("gpsimd", lambda g: g.dma_start(q_dbg[:, :, :], q_qg[:, :, :]),
                              waits=[m_qr])
                        P.dma("gpsimd", lambda g: g.dma_start(k_dbg[:, :, :], kr[:, :, :]),
                              waits=[m_kr])
                    mxl = P.dma("sync", lambda e, ccols=ccols: e.dma_start(
                        sb.x_fm[:, :, :], src_fm[:, :, ccols].rearrange("c p n -> p c n")),
                        waits=P.war("x_fm"))
                    P.guard["x_fm"] = {}
                    for dout in range(NDC):
                        wpar = dout % 2
                        mwl = P.dma("sync", lambda e, dout=dout, wpar=wpar: e.dma_start(
                            wsq[:, wpar], wo_c[dout]),
                            waits=[m_wo[dout]] + P.war(("wsq", wpar)))
                        P.guard[("wsq", wpar)] = {}
                        ybk = 2 + dout % 2
                        w = [mwl, m_oev] + P.war(("pb", ybk))
                        my = None
                        for dinc in range(NDC):
                            my = P.emit("tensor", lambda t, dout=dout, dinc=dinc, wpar=wpar, ybk=ybk: t.matmul(
                                pb[ybk][:, :], wsq[:, wpar, dinc, :], o_sb[:, dinc, :],
                                start=(dinc == 0), stop=(dinc == NDC - 1)),
                                waits=w if dinc == 0 else (), inc=1)
                        P.guard[("pb", ybk)] = {}
                        P.reads(("wsq", wpar), my)
                        P.reads("o_sb", my)
                        opar = dout % 2
                        mev = P.emit("vector", lambda v, opar=opar, dout=dout, ybk=ybk: v.tensor_tensor(
                            sb.xout[:, opar, :], pb[ybk][:, :], sb.x_fm[:, dout, :], op=ALU.add),
                            waits=[my, mxl] + P.war(("xout", opar)), inc=1)
                        P.guard[("xout", opar)] = {}
                        P.reads(("pb", ybk), mev)
                        P.reads("x_fm", mev)
                        mst = P.dma("gpsimd", lambda g, opar=opar, dout=dout, ccols=ccols: g.dma_start(
                            dst_fm[dout, :, ccols], sb.xout[:, opar, :]),
                            waits=[mev])
                        P.reads(("xout", opar), mst)
                P.fence()

        emit_ffn(src_tm=x_in, src_fm=None, dst_fm=s1, dst_tm=None, tag="f1")
        emit_attn(src_fm=s1, dst_fm=s2)
        emit_ffn(src_tm=None, src_fm=s2, dst_fm=None, dst_tm=out, tag="f2")

        P.emit("gpsimd", None, waits=[(f"gpsimd_d{i}", P.counts[f"gpsimd_d{i}"])
                                      for i in range(P.NDMASEM)])

        block = es.enter_context(nc.Block())

        @block.sync
        def _(e):
            for f in P.steps["sync"]:
                f(e)

        @block.gpsimd
        def _(e):
            for f in P.steps["gpsimd"]:
                f(e)

        @block.tensor
        def _(e):
            for f in P.steps["tensor"]:
                f(e)

        @block.vector
        def _(e):
            for f in P.steps["vector"]:
                f(e)

        @block.scalar
        def _(e):
            for f in P.steps["scalar"]:
                f(e)

    return nc


def host_constants():
    TINY = np.float32(2.0 ** -126)
    consts = np.zeros((128, 16), dtype=np.float32)
    p = np.arange(128)
    for k in (1, 2, 3):
        consts[:, k - 1] = np.where((p % 4) + k < 4, TINY, 0.0)
    consts[:, C_15] = 1.5
    consts[:, C_INV2048] = 1.0 / 2048.0
    consts[:, C_EPS] = 1e-5
    consts[:, C_NEG1] = -1.0
    consts[:, C_HALF] = 0.5
    consts[:, C_ISQHD] = 1.0 / np.sqrt(np.float32(HD))
    consts[:, C_ONE] = 1.0
    ident = np.eye(128, dtype=np.float32)
    sd = np.zeros((3, 128, 128), dtype=np.float32)
    for k in (1, 2, 3):
        for c in range(128):
            r = 4 * (c // 4) + (c + k) % 4
            sd[k - 1, r, c] += 1.0
            sd[k - 1, c, c] -= 1.0
    swapm = np.zeros((128, 128), dtype=np.float32)
    for c in range(128):
        r = c + 1 if c % 2 == 0 else c - 1
        swapm[r, c] = 1.0
    half = HD // 2
    inv = 1.0 / 10000.0 ** (np.arange(half, dtype=np.float64) / half)
    pos = np.arange(S, dtype=np.float64)
    ang = pos[None, :] * inv[:, None]
    ctab = np.repeat(np.cos(ang), 2, axis=0).astype(np.float32)
    sin2 = np.repeat(np.sin(ang), 2, axis=0)
    sign = np.where(np.arange(128) % 2 == 0, -1.0, 1.0)
    stab = (sin2 * sign[:, None]).astype(np.float32)
    return consts, ident, sd, swapm, ctab, stab


def kernel(L0_attn_x, L0_attn_W_q, L0_attn_W_k, L0_attn_W_v, L0_attn_W_o,
           L0_ffn_w_gate, L0_ffn_w_up, L0_ffn_w_down):
    x = np.ascontiguousarray(np.asarray(L0_attn_x, dtype=np.float32))
    consts, ident, sd, swapm, ctab, stab = host_constants()
    common = {
        "wq": np.ascontiguousarray(np.asarray(L0_attn_W_q, dtype=np.float32)),
        "wk": np.ascontiguousarray(np.asarray(L0_attn_W_k, dtype=np.float32)),
        "wv": np.ascontiguousarray(np.asarray(L0_attn_W_v, dtype=np.float32)),
        "wo": np.ascontiguousarray(np.asarray(L0_attn_W_o, dtype=np.float32)),
        "wg": np.ascontiguousarray(np.asarray(L0_ffn_w_gate, dtype=np.float32)),
        "wu": np.ascontiguousarray(np.asarray(L0_ffn_w_up, dtype=np.float32)),
        "wd": np.ascontiguousarray(np.asarray(L0_ffn_w_down, dtype=np.float32)),
        "consts": consts, "ident": ident, "sd": sd, "swapm": swapm,
        "ctab": ctab, "stab": stab,
    }
    nc = build()
    in_maps = [dict(common, x=np.ascontiguousarray(x[i])) for i in range(8)]
    res = run_bass_kernel_spmd(nc, in_maps, core_ids=list(range(8)))
    return np.stack([res.results[i]["out"] for i in range(8)], axis=0)


# version 10
# speedup vs baseline: 1.0407x; 1.0119x over previous
import numpy as np
import concourse.bass as bass
import concourse.mybir as mybir
from concourse.bass_utils import run_bass_kernel_spmd
from contextlib import ExitStack

F32 = mybir.dt.float32
BF16 = mybir.dt.bfloat16
AF = mybir.ActivationFunctionType
ALU = mybir.AluOpType

D = 2048
S = 2048
II = 8192
NH, NKV, HD = 16, 4, 128
NDC = D // 128
NIC = II // 128
TSUP = 512
NSUP = S // TSUP
QG = 512
NQG = S // QG
NTT = TSUP // 128

C_TAU1, C_TAU2, C_TAU3, C_15, C_INV2048, C_EPS, C_NEG1, C_HALF, C_ISQHD, C_ONE = range(10)
EXP_SHIFT = 10.0

ENGS = ("sync", "gpsimd", "tensor", "vector", "scalar")


class Prog:

    NDMASEM = 8

    def __init__(self, nc, es):
        self.nc = nc
        self.steps = {e: [] for e in ENGS}
        self.sems = {e: es.enter_context(nc.semaphore("sem_" + e)) for e in ENGS}
        for q in ("sync", "gpsimd"):
            for i in range(self.NDMASEM):
                n = f"{q}_d{i}"
                self.sems[n] = es.enter_context(nc.semaphore("sem_" + n))
        self.counts = {e: 0 for e in self.sems}
        self.rr = {"sync": 0, "gpsimd": 0}
        self.guard = {}

    def emit(self, eng, fn, waits=(), inc=0):
        sems = self.sems
        wl = []
        for w in waits:
            if w is None:
                continue
            en, val = w
            if val > 0:
                wl.append((en, val))
        wl = tuple(wl)

        def step(e, fn=fn, wl=wl, inc=inc, sem=self.sems[eng]):
            for (en, val) in wl:
                e.wait_ge(sems[en], val)
            ins = fn(e) if fn is not None else None
            if inc:
                ins.then_inc(sem, inc)

        self.steps[eng].append(step)
        if inc:
            self.counts[eng] += inc
        return (eng, self.counts[eng])

    def dma(self, queue, fn, waits=()):
        i = self.rr[queue] % self.NDMASEM
        self.rr[queue] += 1
        sname = f"{queue}_d{i}"
        w = list(waits)
        if self.counts[sname] > 0:
            w.append((sname, self.counts[sname]))
        sems = self.sems
        wl = []
        for x in w:
            if x is None:
                continue
            en, val = x
            if val > 0:
                wl.append((en, val))
        wl = tuple(wl)

        def step(e, fn=fn, wl=wl, sem=self.sems[sname]):
            for (en, val) in wl:
                e.wait_ge(sems[en], val)
            fn(e).then_inc(sem, 16)

        self.steps[queue].append(step)
        self.counts[sname] += 16
        return (sname, self.counts[sname])

    def reads(self, key, milestone):
        d = self.guard.setdefault(key, {})
        d[milestone[0]] = max(d.get(milestone[0], 0), milestone[1])

    def war(self, key):
        d = self.guard.get(key, {})
        return [(e, v) for e, v in d.items()]

    def fence(self):
        snap = dict(self.counts)
        for e in ENGS:
            self.emit(e, None, waits=[(o, v) for o, v in snap.items() if o != e])


class SB:
    pass


def emit_ln(P, pb, sb, load_chunk, xln_dst, xln_key):
    cs = sb.consts
    m_sum = [None] * NDC
    for c in range(NDC):
        par = c % 2
        ap, w, key = load_chunk(c, 0)
        w = list(w) + P.war(("lncast", par))
        mcast = P.emit("scalar", lambda s, ap=ap, par=par: s.activation(
            sb.xb[:, par, :], ap, AF.Copy), waits=w, inc=1)
        msq = P.emit("scalar", lambda s, ap=ap, par=par: s.activation(
            sb.x2b[:, par, :], ap, AF.Square), inc=1)
        P.reads(key, mcast)
        P.reads(key, msq)
        w = [msq]
        if c == 0:
            w += P.war(("pb", 4)) + P.war(("pb", 5)) + [P.m_ones]
        P.emit("tensor", lambda t, c=c, par=par: t.matmul(
            pb[4][0:1, :], sb.ones_bf[:, :], sb.xb[:, par, :],
            start=(c == 0), stop=(c == NDC - 1)), waits=w, inc=1)
        m_sum[c] = P.emit("tensor", lambda t, c=c, par=par: t.matmul(
            pb[5][0:1, :], sb.ones_bf[:, :], sb.x2b[:, par, :],
            start=(c == 0), stop=(c == NDC - 1)), inc=1)
        P.guard[("lncast", par)] = {}
        P.reads(("lncast", par), m_sum[c])
    m1 = P.emit("vector", lambda v: v.tensor_scalar(
        sb.st_m[:, :], pb[4][0:1, :], cs[:1, C_INV2048:C_INV2048 + 1], None,
        op0=ALU.mult), waits=[m_sum[NDC - 1]], inc=1)
    m2 = P.emit("vector", lambda v: v.tensor_scalar(
        sb.st_e2[:, :], pb[5][0:1, :], cs[:1, C_INV2048:C_INV2048 + 1],
        cs[:1, C_EPS:C_EPS + 1], op0=ALU.mult, op1=ALU.add), inc=1)
    P.guard[("pb", 4)] = {}
    P.guard[("pb", 5)] = {}
    P.reads(("pb", 4), m1)
    P.reads(("pb", 5), m2)
    m3 = P.emit("vector", lambda v: v.tensor_tensor(
        sb.st_m2[:, :], sb.st_m[:, :], sb.st_m[:, :], op=ALU.mult), inc=1)
    m4 = P.emit("vector", lambda v: v.tensor_tensor(
        sb.st_e2[:, :], sb.st_e2[:, :], sb.st_m2[:, :], op=ALU.subtract), inc=1)
    m5 = P.emit("scalar", lambda s: s.activation(
        sb.st_m2[:, :], sb.st_e2[:, :], AF.Sqrt), waits=[m4], inc=1)
    m6 = P.emit("vector", lambda v: v.reciprocal(
        sb.st_rstd[:, :], sb.st_m2[:, :]),
        waits=[m5] + P.war("st_rstd"), inc=1)
    m7 = P.emit("vector", lambda v: v.scalar_tensor_tensor(
        sb.st_b[:, :], sb.st_m[:, :], cs[:1, C_NEG1:C_NEG1 + 1],
        sb.st_rstd[:, :], op0=ALU.mult, op1=ALU.mult),
        waits=P.war("st_b"), inc=1)
    mb1 = P.emit("tensor", lambda t: t.matmul(
        pb[4][:, :], sb.ones_row[:1, :], sb.st_rstd[:, :],
        start=True, stop=True), waits=[m6, m1] + P.war(("pb", 4)), inc=1)
    mb2 = P.emit("tensor", lambda t: t.matmul(
        pb[5][:, :], sb.ones_row[:1, :], sb.st_b[:, :],
        start=True, stop=True), waits=[m7, m2] + P.war(("pb", 5)), inc=1)
    P.guard["st_rstd"] = {}
    P.guard["st_b"] = {}
    P.reads("st_rstd", mb1)
    P.reads("st_b", mb2)
    mc1 = P.emit("vector", lambda v: v.tensor_copy(
        sb.bc_a[:, :], pb[4][:, :]), waits=[mb1], inc=1)
    mc2 = P.emit("vector", lambda v: v.tensor_copy(
        sb.bc_b[:, :], pb[5][:, :]), waits=[mb2], inc=1)
    P.guard[("pb", 4)] = {}
    P.guard[("pb", 5)] = {}
    P.reads(("pb", 4), mc1)
    P.reads(("pb", 5), mc2)
    mnorm = [None] * NDC
    for c in range(NDC):
        ap, w2, key2 = load_chunk(c, 1)
        w = list(w2) + ([mc2] if c == 0 else []) + (P.war(xln_key) if c == 0 else [])
        mn1 = P.emit("vector", lambda v, ap=ap: v.tensor_tensor(
            sb.t_norm[:, :], ap, sb.bc_a[:, :], op=ALU.mult),
            waits=w, inc=1)
        P.reads(key2, mn1)
        mnorm[c] = P.emit("vector", lambda v, c=c: v.tensor_tensor(
            xln_dst[:, c, :], sb.t_norm[:, :], sb.bc_b[:, :], op=ALU.add), inc=1)
    P.guard[xln_key] = {}
    return mnorm


def build(debug_streams=False):
    nc = bass.Bass("TRN2", target_bir_lowering=False)

    x_in = nc.dram_tensor("x", [S, D], F32, kind="ExternalInput")
    wq_in = nc.dram_tensor("wq", [D, D], F32, kind="ExternalInput")
    wk_in = nc.dram_tensor("wk", [D, 512], F32, kind="ExternalInput")
    wv_in = nc.dram_tensor("wv", [D, 512], F32, kind="ExternalInput")
    wo_in = nc.dram_tensor("wo", [D, D], F32, kind="ExternalInput")
    wg_in = nc.dram_tensor("wg", [D, II], F32, kind="ExternalInput")
    wu_in = nc.dram_tensor("wu", [D, II], F32, kind="ExternalInput")
    wd_in = nc.dram_tensor("wd", [II, D], F32, kind="ExternalInput")
    consts_in = nc.dram_tensor("consts", [128, 16], F32, kind="ExternalInput")
    ident_in = nc.dram_tensor("ident", [128, 128], F32, kind="ExternalInput")
    sd_in = nc.dram_tensor("sd", [3, 128, 128], F32, kind="ExternalInput")
    swap_in = nc.dram_tensor("swapm", [128, 128], F32, kind="ExternalInput")
    ctab_in = nc.dram_tensor("ctab", [128, S], F32, kind="ExternalInput")
    stab_in = nc.dram_tensor("stab", [128, S], F32, kind="ExternalInput")
    out = nc.dram_tensor("out", [S, D], F32, kind="ExternalOutput")

    skind = "ExternalOutput" if debug_streams else "Internal"
    s1 = nc.dram_tensor("s1", [NDC, 128, S], F32, kind=skind)
    s2 = nc.dram_tensor("s2", [NDC, 128, S], F32, kind=skind)
    if debug_streams:
        pt_dbg = nc.dram_tensor("pt_dbg", [128, NDC, QG], F32, kind="ExternalOutput")
        o_dbg = nc.dram_tensor("o_dbg", [128, NH, QG], F32, kind="ExternalOutput")
        q_dbg = nc.dram_tensor("q_dbg", [128, NH, QG], F32, kind="ExternalOutput")
        k_dbg = nc.dram_tensor("k_dbg", [128, NKV, S], F32, kind="ExternalOutput")
    else:
        pt_dbg = o_dbg = q_dbg = k_dbg = None

    xlnf_d = nc.dram_tensor("xlnf_d", [NDC, 128, S], BF16, kind=skind)
    wk_dbg = nc.dram_tensor("wk_c_dbg", [NKV, 128, NDC, 128], BF16) if False else None
    wg_c = nc.dram_tensor("wg_c", [NIC // 2, 128, 2, NDC, 128], BF16)
    wu_c = nc.dram_tensor("wu_c", [NIC // 2, 128, 2, NDC, 128], BF16)
    wd_c = nc.dram_tensor("wd_c", [NDC, 4, 128, 16, 128], BF16)
    wq_c = nc.dram_tensor("wq_c", [NH, 128, NDC, 128], BF16)
    wk_c = nc.dram_tensor("wk_c", [NKV, 128, NDC, 128], BF16)
    wo_c = nc.dram_tensor("wo_c", [NDC, 128, NDC, 128], BF16)
    wv_c = nc.dram_tensor("wv_c", [128, NDC, 512], BF16)

    es = ExitStack()
    with es:
        sb = SB()
        sb.consts = es.enter_context(nc.sbuf_tensor("consts_sb", [128, 16], F32))
        sb.ones_bf = es.enter_context(nc.sbuf_tensor("ones_bf", [128, 1], BF16))
        sb.ones_row = es.enter_context(nc.sbuf_tensor("ones_row", [1, 128], F32))
        sb.ident = es.enter_context(nc.sbuf_tensor("ident_sb", [128, 128], F32))
        sb.sd = es.enter_context(nc.sbuf_tensor("sd_sb", [128, 3, 128], BF16))
        sb.swapm = es.enter_context(nc.sbuf_tensor("swap_sb", [128, 128], F32))
        for n in ("st_m", "st_e2", "st_m2", "st_rstd", "st_b", "st_r"):
            setattr(sb, n, es.enter_context(nc.sbuf_tensor(n, [1, 512], F32)))
        sb.xb = es.enter_context(nc.sbuf_tensor("xb_sb", [128, 2, 512], BF16))
        sb.x2b = es.enter_context(nc.sbuf_tensor("x2b_sb", [128, 2, 512], BF16))
        sb.bc_a = es.enter_context(nc.sbuf_tensor("bc_a", [128, 512], F32))
        sb.bc_b = es.enter_context(nc.sbuf_tensor("bc_b", [128, 512], F32))
        sb.t_norm = es.enter_context(nc.sbuf_tensor("t_norm", [128, 512], F32))
        sb.x_fm = es.enter_context(nc.sbuf_tensor("x_fm", [128, NDC, 512], F32))
        sb.xs = es.enter_context(nc.sbuf_tensor("xs", [128, 3, 512], F32))
        sb.xout = es.enter_context(nc.sbuf_tensor("xout", [128, 2, 512], F32))

        pbig = es.enter_context(nc.psum_tensor("pbig", [128, 8, 512], F32))
        pb = [pbig[:, i, :] for i in range(8)]

        P = Prog(nc, es)

        P.dma("gpsimd", lambda g: g.dma_start(sb.consts[:, :], consts_in[:, :]))
        P.dma("gpsimd", lambda g: g.dma_start(sb.ident[:, :], ident_in[:, :]))
        P.dma("gpsimd", lambda g: g.dma_start(
            sb.sd[:, :, :], sd_in.rearrange("k p f -> p k f")))
        m_const = P.dma("gpsimd", lambda g: g.dma_start(sb.swapm[:, :], swap_in[:, :]))
        m_ones = P.emit("vector", lambda v: v.memset(sb.ones_bf[:, :], 1.0),
                        waits=[m_const], inc=1)
        m_ones = P.emit("vector", lambda v: v.memset(sb.ones_row[:, :], 1.0), inc=1)
        P.m_ones = m_ones

        wg_src = wg_in.rearrange("(dc p) (ib icl f) -> ib p icl dc f", p=128, icl=2, f=128)
        wu_src = wu_in.rearrange("(dc p) (ib icl f) -> ib p icl dc f", p=128, icl=2, f=128)
        m_wg = [None] * (NIC // 2)
        m_wu = [None] * (NIC // 2)
        wd_src = wd_in.rearrange("(ibt ibl p) (dc f) -> dc ibt p ibl f", ibl=16, p=128, f=128)
        m_wd = [[None] * 4 for _ in range(NDC)]
        wq_src = wq_in.rearrange("(dc p) (h f) -> h p dc f", p=128, f=128)
        wk_src = wk_in.rearrange("(dc p) (h f) -> h p dc f", p=128, f=128)
        wo_src = wo_in.rearrange("(dc p) (h f) -> h p dc f", p=128, f=128)
        wv_src = wv_in.rearrange("(dc p) f -> p dc f", p=128)
        for b in range(NIC // 2):
            P.dma("gpsimd", lambda g, b=b: g.dma_start(wg_c[b][:, 0], wg_src[b][:, 0]))
            m_wg[b] = P.dma("gpsimd", lambda g, b=b: g.dma_start(wg_c[b][:, 1], wg_src[b][:, 1]))
            P.dma("gpsimd", lambda g, b=b: g.dma_start(wu_c[b][:, 0], wu_src[b][:, 0]))
            m_wu[b] = P.dma("gpsimd", lambda g, b=b: g.dma_start(wu_c[b][:, 1], wu_src[b][:, 1]))
            for j in (2 * b, 2 * b + 1):
                dc, ibt = j // 4, j % 4
                m_wd[dc][ibt] = P.dma("gpsimd", lambda g, dc=dc, ibt=ibt: g.dma_start(
                    wd_c[dc, ibt], wd_src[dc, ibt]))
        m_wk = [P.dma("gpsimd", lambda g, h=h: g.dma_start(wk_c[h], wk_src[h]))
                for h in range(NKV)]
        m_wv = P.dma("gpsimd", lambda g: g.dma_start(wv_c[:, :, :], wv_src))
        m_wq = [P.dma("gpsimd", lambda g, h=h: g.dma_start(wq_c[h], wq_src[h]))
                for h in range(NH)]
        m_wo = [P.dma("gpsimd", lambda g, h=h: g.dma_start(wo_c[h], wo_src[h]))
                for h in range(NDC)]

        P.emit("vector", None, waits=[m_const])
        P.emit("scalar", None, waits=[m_const])
        P.emit("tensor", None, waits=[m_const])

        xs_rr = [0]

        def stream_chunk(src_fm, cols):
            def load(c, phase=0):
                slot = xs_rr[0] % 3
                xs_rr[0] += 1
                m = P.dma("sync", lambda e, c=c, slot=slot: e.dma_start(
                    sb.xs[:, slot, :], src_fm[c, :, cols]),
                    waits=P.war(("xs", slot)))
                P.guard[("xs", slot)] = {}
                return (sb.xs[:, slot, :], [m], ("xs", slot))
            return load

        def emit_ffn(src_tm, src_fm, dst_fm, dst_tm, tag):
            esf = ExitStack()
            with esf:
                xln = esf.enter_context(nc.sbuf_tensor("xln_f_" + tag, [128, NDC, TSUP], BF16))
                h_sp = esf.enter_context(nc.sbuf_tensor("h_sp_" + tag, [128, NIC, TSUP], BF16))
                xtm = esf.enter_context(nc.sbuf_tensor("xtm_" + tag, [128, 2, D], F32))
                wsg = esf.enter_context(nc.sbuf_tensor("wsg_" + tag, [128, 2, NDC, 128], BF16))
                wsu = esf.enter_context(nc.sbuf_tensor("wsu_" + tag, [128, 2, NDC, 128], BF16))
                wsd = esf.enter_context(nc.sbuf_tensor("wsd_" + tag, [128, 2, 16, 128], BF16))
                g_sb = esf.enter_context(nc.sbuf_tensor("g_sb_" + tag, [128, 2, TSUP], F32))
                h_f32 = esf.enter_context(nc.sbuf_tensor("h_f32_" + tag, [128, 2, TSUP], F32))
                a_bf = esf.enter_context(nc.sbuf_tensor("a_bf_" + tag, [128, 2, TSUP], BF16))
                sp_c = esf.enter_context(nc.sbuf_tensor("sp_c_" + tag, [128, TSUP], BF16))

                for st in range(NSUP):
                    cols = slice(st * TSUP, (st + 1) * TSUP)
                    if src_tm is not None:
                        for tt in range(NTT):
                            par = tt % 2
                            row0 = st * TSUP + tt * 128
                            mld = P.dma("sync", lambda e, par=par, row0=row0: e.dma_start(
                                xtm[:, par, :], src_tm[row0:row0 + 128, :]),
                                waits=P.war(("xtm", par)))
                            P.guard[("xtm", par)] = {}
                            for grp in range(4):
                                bk = 6 + grp % 2
                                w = [mld] + (P.war("x_fm") if (tt, grp) == (0, 0) else []) \
                                    + P.war(("pb", bk))
                                for k in range(4):
                                    dc = grp * 4 + k
                                    mt = P.emit("tensor", lambda t, par=par, dc=dc, bk=bk, k=k: t.transpose(
                                        pb[bk][:, k * 128:(k + 1) * 128],
                                        xtm[:, par, dc * 128:(dc + 1) * 128], sb.ident[:, :]),
                                        waits=w if k == 0 else (), inc=1)
                                P.reads(("xtm", par), mt)
                                P.guard[("pb", bk)] = {}
                                dst_ap = sb.x_fm[:, grp * 4:(grp + 1) * 4, tt * 128:(tt + 1) * 128]
                                mcp = P.emit("vector", lambda v, bk=bk, dst_ap=dst_ap: v.tensor_copy(
                                    dst_ap, pb[bk].rearrange("p (k f) -> p k f", k=4)),
                                    waits=[mt], inc=1)
                                P.reads(("pb", bk), mcp)
                        if (st, 0) == (0, 0):
                            P.guard["x_fm"] = {}
                        m_xready = [mcp]
                        def loader(c, phase=0, m_xready=tuple(m_xready)):
                            return (sb.x_fm[:, c, :],
                                    list(m_xready) if c == 0 else [], "x_fm")
                    else:
                        loader = stream_chunk(src_fm, cols)

                    mln = emit_ln(P, pb, sb, loader, xln, "xln")[NDC - 1]

                    m_hsp_last = None
                    pend = None
                    for ic in range(NIC + 1):
                        if ic < NIC:
                            par = ic % 2
                            b, icl = ic // 2, ic % 2
                            bpar = ic % 2
                            mwg_l = P.dma("sync", lambda e, b=b, icl=icl, bpar=bpar: e.dma_start(
                                wsg[:, bpar], wg_c[b][:, icl]),
                                waits=[m_wg[b]] + P.war(("wsg", bpar)))
                            mwu_l = P.dma("sync", lambda e, b=b, icl=icl, bpar=bpar: e.dma_start(
                                wsu[:, bpar], wu_c[b][:, icl]),
                                waits=[m_wu[b]] + P.war(("wsu", bpar)))
                            P.guard[("wsg", bpar)] = {}
                            P.guard[("wsu", bpar)] = {}
                            gbk, ubk = 0 + par, 2 + par
                            w = [mwg_l, mln] + P.war(("pb", gbk))
                            for dc in range(NDC):
                                mg = P.emit("tensor", lambda t, icl=icl, bpar=bpar, dc=dc, gbk=gbk: t.matmul(
                                    pb[gbk][:, :], wsg[:, bpar, dc, :], xln[:, dc, :],
                                    start=(dc == 0), stop=(dc == NDC - 1)),
                                    waits=w if dc == 0 else (), inc=1)
                            P.guard[("pb", gbk)] = {}
                            P.reads(("wsg", bpar), mg)
                            P.reads("xln", mg)
                            w = [mwu_l] + P.war(("pb", ubk))
                            for dc in range(NDC):
                                mu = P.emit("tensor", lambda t, icl=icl, bpar=bpar, dc=dc, ubk=ubk: t.matmul(
                                    pb[ubk][:, :], wsu[:, bpar, dc, :], xln[:, dc, :],
                                    start=(dc == 0), stop=(dc == NDC - 1)),
                                    waits=w if dc == 0 else (), inc=1)
                            P.guard[("pb", ubk)] = {}
                            P.reads(("wsu", bpar), mu)
                            P.reads("xln", mu)
                            msl = P.emit("scalar", lambda s, par=par, gbk=gbk: s.activation(
                                g_sb[:, par, :], pb[gbk][:, :], AF.Silu),
                                waits=[mg] + P.war(("g_sb", par)), inc=1)
                            P.guard[("g_sb", par)] = {}
                            P.reads(("pb", gbk), msl)
                            mh = P.emit("vector", lambda v, par=par, ubk=ubk: v.tensor_tensor(
                                h_f32[:, par, :], pb[ubk][:, :], g_sb[:, par, :], op=ALU.mult),
                                waits=[msl, mu] + P.war(("h_f32", par)), inc=1)
                            P.guard[("h_f32", par)] = {}
                            P.reads(("pb", ubk), mh)
                            P.reads(("g_sb", par), mh)
                            mab = P.emit("scalar", lambda s, par=par: s.activation(
                                a_bf[:, par, :], h_f32[:, par, :], AF.Abs),
                                waits=[mh] + P.war(("a_bf", par)), inc=1)
                            P.guard[("a_bf", par)] = {}
                            P.reads(("h_f32", par), mab)
                            this = (ic, par, mab)
                        if pend is not None:
                            pic, ppar, pmab = pend
                            w = [pmab] + P.war(("pb", 4)) + P.war(("pb", 5)) + P.war(("pb", 6))
                            for k in range(3):
                                msh = P.emit("tensor", lambda t, k=k, ppar=ppar: t.matmul(
                                    pb[4 + k][:, :], sb.sd[:, k, :], a_bf[:, ppar, :],
                                    start=True, stop=True), waits=w if k == 0 else (), inc=1)
                            P.guard[("pb", 4)] = {}
                            P.guard[("pb", 5)] = {}
                            P.guard[("pb", 6)] = {}
                            P.reads(("a_bf", ppar), msh)
                            cs = sb.consts
                            mg1 = P.emit("vector", lambda v: v.tensor_scalar(
                                sp_c[:, :], pb[4][:, :], cs[:, C_TAU1:C_TAU1 + 1], None,
                                op0=ALU.is_ge), waits=[msh], inc=1)
                            P.reads(("pb", 4), mg1)
                            mg2 = P.emit("vector", lambda v: v.scalar_tensor_tensor(
                                sp_c[:, :], pb[5][:, :], cs[:, C_TAU2:C_TAU2 + 1],
                                sp_c[:, :], op0=ALU.is_ge, op1=ALU.add), inc=1)
                            P.reads(("pb", 5), mg2)
                            mg3 = P.emit("vector", lambda v: v.scalar_tensor_tensor(
                                sp_c[:, :], pb[6][:, :], cs[:, C_TAU3:C_TAU3 + 1],
                                sp_c[:, :], op0=ALU.is_ge, op1=ALU.add), inc=1)
                            P.reads(("pb", 6), mg3)
                            m_hsp_last = P.emit("vector", lambda v, ppar=ppar, pic=pic: v.scalar_tensor_tensor(
                                h_sp[:, pic, :], sp_c[:, :], cs[:, C_15:C_15 + 1],
                                h_f32[:, ppar, :], op0=ALU.is_le, op1=ALU.mult),
                                waits=P.war("h_sp") if pic == 0 else (), inc=1)
                            if pic == 0:
                                P.guard["h_sp"] = {}
                            P.reads(("h_f32", ppar), m_hsp_last)
                        if ic < NIC:
                            pend = this

                    loader_b = (stream_chunk(src_fm, cols) if src_fm is not None else None)
                    for dout in range(NDC):
                        ybk = 0 + dout % 2
                        md = None
                        for ibt in range(4):
                            bpar = (dout * 4 + ibt) % 2
                            mwd_l = P.dma("sync", lambda e, dout=dout, ibt=ibt, bpar=bpar: e.dma_start(
                                wsd[:, bpar], wd_c[dout, ibt]),
                                waits=[m_wd[dout][ibt]] + P.war(("wsd", bpar)))
                            P.guard[("wsd", bpar)] = {}
                            w = [mwd_l]
                            if ibt == 0:
                                w += [m_hsp_last] + P.war(("pb", ybk))
                            for ibl in range(16):
                                ic = ibt * 16 + ibl
                                md = P.emit("tensor", lambda t, bpar=bpar, ibl=ibl, ic=ic, ybk=ybk: t.matmul(
                                    pb[ybk][:, :], wsd[:, bpar, ibl, :], h_sp[:, ic, :],
                                    start=(ic == 0), stop=(ic == NIC - 1)),
                                    waits=w if ibl == 0 else (), inc=1)
                            if ibt == 0:
                                P.guard[("pb", ybk)] = {}
                            P.reads(("wsd", bpar), md)
                            P.reads("h_sp", md)
                        opar = dout % 2
                        if dst_fm is not None:
                            mev = P.emit("vector", lambda v, opar=opar, dout=dout, ybk=ybk: v.scalar_tensor_tensor(
                                sb.xout[:, opar, :], pb[ybk][:, :], sb.consts[:, C_HALF:C_HALF + 1],
                                sb.x_fm[:, dout, :], op0=ALU.mult, op1=ALU.add),
                                waits=[md] + P.war(("xout", opar)), inc=1)
                            P.guard[("xout", opar)] = {}
                            P.reads(("pb", ybk), mev)
                            P.reads("x_fm", mev)
                            mst = P.dma("gpsimd", lambda g, opar=opar, dout=dout, cols=cols: g.dma_start(
                                dst_fm[dout, :, cols], sb.xout[:, opar, :]),
                                waits=[mev])
                            P.reads(("xout", opar), mst)
                        else:
                            xap, xw, xkey = loader_b(dout)
                            mev = P.emit("vector", lambda v, dout=dout, ybk=ybk, xap=xap: v.scalar_tensor_tensor(
                                sb.x_fm[:, dout, :], pb[ybk][:, :], sb.consts[:, C_HALF:C_HALF + 1],
                                xap, op0=ALU.mult, op1=ALU.add),
                                waits=[md] + list(xw) + (P.war("x_fm") if dout == 0 else []), inc=1)
                            if dout == 0:
                                P.guard["x_fm"] = {}
                            P.reads(("pb", ybk), mev)
                            P.reads(xkey, mev)
                            P.reads("x_fm", mev)
                            m_evac_last = mev

                    if dst_tm is not None:
                        for tt in range(NTT):
                            par = tt % 2
                            row0 = st * TSUP + tt * 128
                            for grp in range(4):
                                bk = 6 + grp % 2
                                w = [m_evac_last] + P.war(("pb", bk)) if grp < 2 else P.war(("pb", bk))
                                for k in range(4):
                                    dc = grp * 4 + k
                                    mt = P.emit("tensor", lambda t, dc=dc, bk=bk, k=k, tt=tt: t.transpose(
                                        pb[bk][:, k * 128:(k + 1) * 128],
                                        sb.x_fm[:, dc, tt * 128:(tt + 1) * 128], sb.ident[:, :]),
                                        waits=w if k == 0 else (), inc=1)
                                P.reads("x_fm", mt)
                                P.guard[("pb", bk)] = {}
                                dst_ap = xtm[:, par, grp * 512:(grp + 1) * 512]
                                mcp = P.emit("vector", lambda v, bk=bk, dst_ap=dst_ap: v.tensor_copy(
                                    dst_ap, pb[bk][:, :]),
                                    waits=[mt] + (P.war(("xtm", par)) if grp == 0 else []), inc=1)
                                if grp == 0:
                                    P.guard[("xtm", par)] = {}
                                P.reads(("pb", bk), mcp)
                            mst = P.dma("gpsimd", lambda g, par=par, row0=row0: g.dma_start(
                                dst_tm[row0:row0 + 128, :], xtm[:, par, :]),
                                waits=[mcp])
                            P.reads(("xtm", par), mst)
                P.fence()

        def emit_attn(src_fm, dst_fm):
            esf = ExitStack()
            with esf:
                xlnb = esf.enter_context(nc.sbuf_tensor("xlnb", [128, NDC, 512], BF16))
                kr = esf.enter_context(nc.sbuf_tensor("kr", [128, NKV, S], BF16))
                v_tm = esf.enter_context(nc.sbuf_tensor("v_tm", [128, NDC, 512], BF16))
                ct_sb = esf.enter_context(nc.sbuf_tensor("ct_sb", [128, 512], F32))
                st_sb = esf.enter_context(nc.sbuf_tensor("st_sb", [128, 512], F32))
                q_qg = esf.enter_context(nc.sbuf_tensor("q_qg", [128, NH, QG], BF16))
                pt = esf.enter_context(nc.sbuf_tensor("pt", [128, NDC, QG], BF16))
                o_sb = esf.enter_context(nc.sbuf_tensor("o_sb", [128, NH, QG], BF16))
                wsq = esf.enter_context(nc.sbuf_tensor("wsq", [128, 2, NDC, 128], BF16))
                wv_sb = esf.enter_context(nc.sbuf_tensor("wv_sb", [128, NDC, 512], BF16))
                q_f32 = esf.enter_context(nc.sbuf_tensor("q_f32", [128, 2, 512], F32))
                t_rope = esf.enter_context(nc.sbuf_tensor("t_rope", [128, 2, 512], F32))
                rbc = esf.enter_context(nc.sbuf_tensor("rbc", [128, 512], F32))
                cs = sb.consts

                mwv_l = P.dma("sync", lambda e: e.dma_start(wv_sb[:, :, :], wv_c[:, :, :]),
                               waits=[m_wv])

                def load_tabs(ccols):
                    m1 = P.dma("sync", lambda e, ccols=ccols: e.dma_start(
                        ct_sb[:, :], ctab_in[:, ccols]), waits=P.war("tabs"))
                    m2 = P.dma("sync", lambda e, ccols=ccols: e.dma_start(
                        st_sb[:, :], stab_in[:, ccols]), waits=P.war("tabs"))
                    P.guard["tabs"] = {}
                    return [m1, m2]

                def rope(src_bank, par2, ccols, dst_ap, scaled, extra_war_key, ready, mtab):
                    mev = P.emit("vector", lambda v, par2=par2, src_bank=src_bank: v.tensor_copy(
                        q_f32[:, par2, :], pb[src_bank][:, :]),
                        waits=[ready] + P.war(("q_f32", par2)), inc=1)
                    P.guard[("q_f32", par2)] = {}
                    P.reads(("pb", src_bank), mev)
                    msw = P.emit("tensor", lambda t, par2=par2: t.matmul(
                        pb[2 + par2][:, :], sb.swapm[:, :], q_f32[:, par2, :],
                        start=True, stop=True),
                        waits=[mev] + P.war(("pb", 2 + par2)), inc=1)
                    P.guard[("pb", 2 + par2)] = {}
                    P.reads(("q_f32", par2), msw)
                    if scaled:
                        mr1 = P.emit("vector", lambda v, par2=par2, ccols=ccols: v.scalar_tensor_tensor(
                            t_rope[:, par2, :], q_f32[:, par2, :], cs[:, C_ISQHD:C_ISQHD + 1],
                            ct_sb[:, :], op0=ALU.mult, op1=ALU.mult),
                            waits=mtab + [msw] + P.war(("t_rope", par2)), inc=1)
                        mr2 = P.emit("vector", lambda v, par2=par2, ccols=ccols: v.scalar_tensor_tensor(
                            q_f32[:, par2, :], pb[2 + par2][:, :], cs[:, C_ISQHD:C_ISQHD + 1],
                            st_sb[:, :], op0=ALU.mult, op1=ALU.mult), inc=1)
                    else:
                        mr1 = P.emit("vector", lambda v, par2=par2: v.tensor_tensor(
                            t_rope[:, par2, :], q_f32[:, par2, :], ct_sb[:, :], op=ALU.mult),
                            waits=mtab + [msw] + P.war(("t_rope", par2)), inc=1)
                        mr2 = P.emit("vector", lambda v, par2=par2: v.tensor_tensor(
                            q_f32[:, par2, :], pb[2 + par2][:, :], st_sb[:, :], op=ALU.mult),
                            inc=1)
                    P.guard[("t_rope", par2)] = {}
                    P.guard[("q_f32", par2)] = {}
                    P.reads("tabs", mr1)
                    P.reads("tabs", mr2)
                    P.reads(("pb", 2 + par2), mr2)
                    mr3 = P.emit("vector", lambda v, par2=par2, dst_ap=dst_ap: v.tensor_tensor(
                        dst_ap, t_rope[:, par2, :], q_f32[:, par2, :], op=ALU.add),
                        waits=P.war(extra_war_key), inc=1)
                    P.guard[extra_war_key] = {}
                    P.reads(("t_rope", par2), mr3)
                    P.reads(("q_f32", par2), mr3)
                    return mr3

                m_kr = None
                m_v = None
                m_xst = [None] * NSUP
                for st in range(NSUP):
                    cols = slice(st * TSUP, (st + 1) * TSUP)
                    mln = emit_ln(P, pb, sb, stream_chunk(src_fm, cols), xlnb, "xlnf")[NDC - 1]
                    m_xst[st] = P.dma("gpsimd", lambda g, cols=cols: g.dma_start(
                        xlnf_d[:, :, cols].rearrange("c p n -> p c n"), xlnb[:, :, :]),
                        waits=[mln])
                    P.reads("xlnf", m_xst[st])
                    mtab_st = load_tabs(cols)
                    for kv in range(NKV):
                        wpar = kv % 2
                        mwl = P.dma("sync", lambda e, kv=kv, wpar=wpar: e.dma_start(
                            wsq[:, wpar], wk_c[kv]),
                            waits=[m_wk[kv]] + P.war(("wsq", wpar)))
                        P.guard[("wsq", wpar)] = {}
                        bk = kv % 2
                        w = [mwl, mln] + P.war(("pb", bk))
                        for dc in range(NDC):
                            mk = P.emit("tensor", lambda t, wpar=wpar, dc=dc, bk=bk: t.matmul(
                                pb[bk][:, :], wsq[:, wpar, dc, :], xlnb[:, dc, :],
                                start=(dc == 0), stop=(dc == NDC - 1)),
                                waits=w if dc == 0 else (), inc=1)
                        P.guard[("pb", bk)] = {}
                        P.reads(("wsq", wpar), mk)
                        P.reads("xlnf", mk)
                        m_kr = rope(bk, kv % 2, cols, kr[:, kv, cols], False, "kr", mk, mtab_st)
                    for ttl in range(NTT):
                        tc = st * NTT + ttl
                        vbk = 4 + ttl % 2
                        w = [mwv_l, mln] + P.war(("pb", vbk))
                        for dc in range(NDC):
                            mv = P.emit("tensor", lambda t, ttl=ttl, dc=dc, vbk=vbk: t.matmul(
                                pb[vbk][:, :], xlnb[:, dc, ttl * 128:(ttl + 1) * 128],
                                wv_sb[:, dc, :],
                                start=(dc == 0), stop=(dc == NDC - 1)),
                                waits=w if dc == 0 else (), inc=1)
                        P.guard[("pb", vbk)] = {}
                        P.reads("xlnf", mv)
                        m_v = P.emit("vector", lambda v, tc=tc, vbk=vbk: v.tensor_copy(
                            v_tm[:, tc, :], pb[vbk][:, :]),
                            waits=[mv] + (P.war("v_tm") if tc == 0 else []), inc=1)
                        if tc == 0:
                            P.guard["v_tm"] = {}
                        P.reads(("pb", vbk), m_v)

                for qg in range(NQG):
                    ccols = slice(qg * QG, (qg + 1) * QG)
                    mxq = P.dma("sync", lambda e, ccols=ccols: e.dma_start(
                        xlnb[:, :, :], xlnf_d[:, :, ccols].rearrange("c p n -> p c n")),
                        waits=[m_xst[qg]] + P.war("xlnf"))
                    P.guard["xlnf"] = {}
                    m_qr = None
                    mtab_qg = load_tabs(ccols)
                    for h in range(NH):
                        wpar = h % 2
                        mwl = P.dma("sync", lambda e, h=h, wpar=wpar: e.dma_start(
                            wsq[:, wpar], wq_c[h]),
                            waits=[m_wq[h]] + P.war(("wsq", wpar)))
                        P.guard[("wsq", wpar)] = {}
                        bk = h % 2
                        w = [mwl, mxq] + P.war(("pb", bk))
                        for dc in range(NDC):
                            mq = P.emit("tensor", lambda t, wpar=wpar, dc=dc, bk=bk: t.matmul(
                                pb[bk][:, :], wsq[:, wpar, dc, :], xlnb[:, dc, :],
                                start=(dc == 0), stop=(dc == NDC - 1)),
                                waits=w if dc == 0 else (), inc=1)
                        P.guard[("pb", bk)] = {}
                        P.reads(("wsq", wpar), mq)
                        P.reads("xlnf", mq)
                        m_qr = rope(bk, h % 2, ccols, q_qg[:, h, :], True, ("q_qg", h), mq, mtab_qg)

                    for h in range(NH):
                        kvh = h % NKV
                        obk = 5 + h % 2
                        m_exp_pr = [None] * 8
                        msum = None
                        mo = None
                        pend_pr = None

                        def emit_pair_consumers(pr):
                            nonlocal msum, mo
                            kc0 = 2 * pr
                            w = [m_exp_pr[pr]]
                            if pr == 0:
                                w += [P.m_ones] + P.war(("pb", 4))
                            for kc in (kc0, kc0 + 1):
                                msum_ = P.emit("tensor", lambda t, kc=kc: t.matmul(
                                    pb[4][0:1, :], sb.ones_bf[:, :], pt[:, kc, :],
                                    start=(kc == 0), stop=(kc == NDC - 1)),
                                    waits=w if kc == kc0 else (), inc=1)
                            if pr == 0:
                                P.guard[("pb", 4)] = {}
                            w2 = []
                            if pr == 0:
                                w2 = [m_v] + P.war(("pb", obk))
                            for kc in (kc0, kc0 + 1):
                                mo_ = P.emit("tensor", lambda t, kc=kc, kvh=kvh, obk=obk: t.matmul(
                                    pb[obk][:, :], v_tm[:, kc, kvh * 128:(kvh + 1) * 128], pt[:, kc, :],
                                    start=(kc == 0), stop=(kc == NDC - 1)),
                                    waits=w2 if kc == kc0 else (), inc=1)
                            if pr == 0:
                                P.guard[("pb", obk)] = {}
                            P.reads("pt", mo_)
                            msum, mo = msum_, mo_

                        for pr in range(8):
                            pbase = (pr % 2) * 2
                            w = [m_qr, m_kr] if pr == 0 else []
                            w = w + P.war(("pb", pbase)) + P.war(("pb", pbase + 1))
                            for j in (0, 1):
                                kc = 2 * pr + j
                                msmm = P.emit("tensor", lambda t, kvh=kvh, kc=kc, bk=pbase + j, h=h: t.matmul(
                                    pb[bk][:, :], kr[:, kvh, kc * 128:(kc + 1) * 128], q_qg[:, h, :],
                                    start=True, stop=True), waits=w if j == 0 else (), inc=1)
                            P.guard[("pb", pbase)] = {}
                            P.guard[("pb", pbase + 1)] = {}
                            P.reads(("q_qg", h), msmm)
                            P.reads("kr", msmm)
                            w2 = [msmm]
                            if pr == 0:
                                w2 += P.war("pt")
                                P.guard["pt"] = {}
                            m_exp_pr[pr] = P.emit("scalar", lambda s, pr=pr, pbase=pbase: s.activation(
                                pt[:, 2 * pr:2 * pr + 2, :], pbig[:, pbase:pbase + 2, :],
                                AF.Exp, bias=np.float32(-EXP_SHIFT)),
                                waits=w2, inc=1)
                            P.reads(("pb", pbase), m_exp_pr[pr])
                            P.reads(("pb", pbase + 1), m_exp_pr[pr])
                            if pend_pr is not None:
                                emit_pair_consumers(pend_pr)
                            pend_pr = pr
                        emit_pair_consumers(pend_pr)
                        if debug_streams and qg == 0 and h == 0:
                            mdump = P.dma("gpsimd", lambda g: g.dma_start(
                                pt_dbg[:, :, :], pt[:, :, :]), waits=[mo])
                            P.reads("pt", mdump)
                        mr = P.emit("vector", lambda v: v.reciprocal(
                            sb.st_r[:, :], pb[4][0:1, :]),
                            waits=[msum] + P.war("st_r"), inc=1)
                        P.guard["st_r"] = {}
                        P.reads(("pb", 4), mr)
                        mb = P.emit("tensor", lambda t: t.matmul(
                            pb[7][:, :], sb.ones_row[:1, :], sb.st_r[:, :],
                            start=True, stop=True),
                            waits=[mr] + P.war(("pb", 7)), inc=1)
                        P.guard[("pb", 7)] = {}
                        P.reads("st_r", mb)
                        mbc = P.emit("vector", lambda v: v.tensor_copy(
                            rbc[:, :], pb[7][:, :]),
                            waits=[mb] + P.war("rbc"), inc=1)
                        P.guard["rbc"] = {}
                        P.reads(("pb", 7), mbc)
                        moev = P.emit("vector", lambda v, h=h, obk=obk: v.tensor_tensor(
                            o_sb[:, h, :], pb[obk][:, :], rbc[:, :], op=ALU.mult),
                            waits=[mo, mbc] + (P.war("o_sb") if h == 0 else []), inc=1)
                        if h == 0:
                            P.guard["o_sb"] = {}
                        P.reads(("pb", obk), moev)
                        P.reads("rbc", moev)
                        m_oev = moev

                    if debug_streams and qg == 0:
                        P.dma("gpsimd", lambda g: g.dma_start(o_dbg[:, :, :], o_sb[:, :, :]),
                              waits=[m_oev])
                        P.dma("gpsimd", lambda g: g.dma_start(q_dbg[:, :, :], q_qg[:, :, :]),
                              waits=[m_qr])
                        P.dma("gpsimd", lambda g: g.dma_start(k_dbg[:, :, :], kr[:, :, :]),
                              waits=[m_kr])
                    mxl = P.dma("sync", lambda e, ccols=ccols: e.dma_start(
                        sb.x_fm[:, :, :], src_fm[:, :, ccols].rearrange("c p n -> p c n")),
                        waits=P.war("x_fm"))
                    P.guard["x_fm"] = {}
                    for dout in range(NDC):
                        wpar = dout % 2
                        mwl = P.dma("sync", lambda e, dout=dout, wpar=wpar: e.dma_start(
                            wsq[:, wpar], wo_c[dout]),
                            waits=[m_wo[dout]] + P.war(("wsq", wpar)))
                        P.guard[("wsq", wpar)] = {}
                        ybk = 2 + dout % 2
                        w = [mwl, m_oev] + P.war(("pb", ybk))
                        my = None
                        for dinc in range(NDC):
                            my = P.emit("tensor", lambda t, dout=dout, dinc=dinc, wpar=wpar, ybk=ybk: t.matmul(
                                pb[ybk][:, :], wsq[:, wpar, dinc, :], o_sb[:, dinc, :],
                                start=(dinc == 0), stop=(dinc == NDC - 1)),
                                waits=w if dinc == 0 else (), inc=1)
                        P.guard[("pb", ybk)] = {}
                        P.reads(("wsq", wpar), my)
                        P.reads("o_sb", my)
                        opar = dout % 2
                        mev = P.emit("vector", lambda v, opar=opar, dout=dout, ybk=ybk: v.tensor_tensor(
                            sb.xout[:, opar, :], pb[ybk][:, :], sb.x_fm[:, dout, :], op=ALU.add),
                            waits=[my, mxl] + P.war(("xout", opar)), inc=1)
                        P.guard[("xout", opar)] = {}
                        P.reads(("pb", ybk), mev)
                        P.reads("x_fm", mev)
                        mst = P.dma("gpsimd", lambda g, opar=opar, dout=dout, ccols=ccols: g.dma_start(
                            dst_fm[dout, :, ccols], sb.xout[:, opar, :]),
                            waits=[mev])
                        P.reads(("xout", opar), mst)
                P.fence()

        emit_ffn(src_tm=x_in, src_fm=None, dst_fm=s1, dst_tm=None, tag="f1")
        emit_attn(src_fm=s1, dst_fm=s2)
        emit_ffn(src_tm=None, src_fm=s2, dst_fm=None, dst_tm=out, tag="f2")

        P.emit("gpsimd", None, waits=[(f"gpsimd_d{i}", P.counts[f"gpsimd_d{i}"])
                                      for i in range(P.NDMASEM)])

        block = es.enter_context(nc.Block())

        @block.sync
        def _(e):
            for f in P.steps["sync"]:
                f(e)

        @block.gpsimd
        def _(e):
            for f in P.steps["gpsimd"]:
                f(e)

        @block.tensor
        def _(e):
            for f in P.steps["tensor"]:
                f(e)

        @block.vector
        def _(e):
            for f in P.steps["vector"]:
                f(e)

        @block.scalar
        def _(e):
            for f in P.steps["scalar"]:
                f(e)

    return nc


def host_constants():
    TINY = np.float32(2.0 ** -126)
    consts = np.zeros((128, 16), dtype=np.float32)
    p = np.arange(128)
    for k in (1, 2, 3):
        consts[:, k - 1] = np.where((p % 4) + k < 4, TINY, 0.0)
    consts[:, C_15] = 1.5
    consts[:, C_INV2048] = 1.0 / 2048.0
    consts[:, C_EPS] = 1e-5
    consts[:, C_NEG1] = -1.0
    consts[:, C_HALF] = 0.5
    consts[:, C_ISQHD] = 1.0 / np.sqrt(np.float32(HD))
    consts[:, C_ONE] = 1.0
    ident = np.eye(128, dtype=np.float32)
    sd = np.zeros((3, 128, 128), dtype=np.float32)
    for k in (1, 2, 3):
        for c in range(128):
            r = 4 * (c // 4) + (c + k) % 4
            sd[k - 1, r, c] += 1.0
            sd[k - 1, c, c] -= 1.0
    swapm = np.zeros((128, 128), dtype=np.float32)
    for c in range(128):
        r = c + 1 if c % 2 == 0 else c - 1
        swapm[r, c] = 1.0
    half = HD // 2
    inv = 1.0 / 10000.0 ** (np.arange(half, dtype=np.float64) / half)
    pos = np.arange(S, dtype=np.float64)
    ang = pos[None, :] * inv[:, None]
    ctab = np.repeat(np.cos(ang), 2, axis=0).astype(np.float32)
    sin2 = np.repeat(np.sin(ang), 2, axis=0)
    sign = np.where(np.arange(128) % 2 == 0, -1.0, 1.0)
    stab = (sin2 * sign[:, None]).astype(np.float32)
    return consts, ident, sd, swapm, ctab, stab


def kernel(L0_attn_x, L0_attn_W_q, L0_attn_W_k, L0_attn_W_v, L0_attn_W_o,
           L0_ffn_w_gate, L0_ffn_w_up, L0_ffn_w_down):
    x = np.ascontiguousarray(np.asarray(L0_attn_x, dtype=np.float32))
    consts, ident, sd, swapm, ctab, stab = host_constants()
    common = {
        "wq": np.ascontiguousarray(np.asarray(L0_attn_W_q, dtype=np.float32)),
        "wk": np.ascontiguousarray(np.asarray(L0_attn_W_k, dtype=np.float32)),
        "wv": np.ascontiguousarray(np.asarray(L0_attn_W_v, dtype=np.float32)),
        "wo": np.ascontiguousarray(np.asarray(L0_attn_W_o, dtype=np.float32)),
        "wg": np.ascontiguousarray(np.asarray(L0_ffn_w_gate, dtype=np.float32)),
        "wu": np.ascontiguousarray(np.asarray(L0_ffn_w_up, dtype=np.float32)),
        "wd": np.ascontiguousarray(np.asarray(L0_ffn_w_down, dtype=np.float32)),
        "consts": consts, "ident": ident, "sd": sd, "swapm": swapm,
        "ctab": ctab, "stab": stab,
    }
    nc = build()
    in_maps = [dict(common, x=np.ascontiguousarray(x[i])) for i in range(8)]
    res = run_bass_kernel_spmd(nc, in_maps, core_ids=list(range(8)))
    return np.stack([res.results[i]["out"] for i in range(8)], axis=0)


# version 11
# speedup vs baseline: 1.0656x; 1.0239x over previous
import numpy as np
import concourse.bass as bass
import concourse.mybir as mybir
from concourse.bass_utils import run_bass_kernel_spmd
from contextlib import ExitStack

F32 = mybir.dt.float32
BF16 = mybir.dt.bfloat16
AF = mybir.ActivationFunctionType
ALU = mybir.AluOpType

D = 2048
S = 2048
II = 8192
NH, NKV, HD = 16, 4, 128
NDC = D // 128
NIC = II // 128
TSUP = 512
NSUP = S // TSUP
QG = 512
NQG = S // QG
NTT = TSUP // 128

C_TAU1, C_TAU2, C_TAU3, C_15, C_INV2048, C_EPS, C_NEG1, C_HALF, C_ISQHD, C_ONE = range(10)
EXP_SHIFT = 10.0

ENGS = ("sync", "gpsimd", "tensor", "vector", "scalar")


class Prog:

    NDMASEM = 8

    def __init__(self, nc, es):
        self.nc = nc
        self.steps = {e: [] for e in ENGS}
        self.sems = {e: es.enter_context(nc.semaphore("sem_" + e)) for e in ENGS}
        for q in ("sync", "gpsimd"):
            for i in range(self.NDMASEM):
                n = f"{q}_d{i}"
                self.sems[n] = es.enter_context(nc.semaphore("sem_" + n))
        self.counts = {e: 0 for e in self.sems}
        self.rr = {"sync": 0, "gpsimd": 0}
        self.guard = {}

    def emit(self, eng, fn, waits=(), inc=0):
        sems = self.sems
        wl = []
        for w in waits:
            if w is None:
                continue
            en, val = w
            if val > 0:
                wl.append((en, val))
        wl = tuple(wl)

        def step(e, fn=fn, wl=wl, inc=inc, sem=self.sems[eng]):
            for (en, val) in wl:
                e.wait_ge(sems[en], val)
            ins = fn(e) if fn is not None else None
            if inc:
                ins.then_inc(sem, inc)

        self.steps[eng].append(step)
        if inc:
            self.counts[eng] += inc
        return (eng, self.counts[eng])

    def dma(self, queue, fn, waits=()):
        i = self.rr[queue] % self.NDMASEM
        self.rr[queue] += 1
        sname = f"{queue}_d{i}"
        w = list(waits)
        if self.counts[sname] > 0:
            w.append((sname, self.counts[sname]))
        sems = self.sems
        wl = []
        for x in w:
            if x is None:
                continue
            en, val = x
            if val > 0:
                wl.append((en, val))
        wl = tuple(wl)

        def step(e, fn=fn, wl=wl, sem=self.sems[sname]):
            for (en, val) in wl:
                e.wait_ge(sems[en], val)
            fn(e).then_inc(sem, 16)

        self.steps[queue].append(step)
        self.counts[sname] += 16
        return (sname, self.counts[sname])

    def reads(self, key, milestone):
        d = self.guard.setdefault(key, {})
        d[milestone[0]] = max(d.get(milestone[0], 0), milestone[1])

    def war(self, key):
        d = self.guard.get(key, {})
        return [(e, v) for e, v in d.items()]

    def fence(self):
        snap = dict(self.counts)
        for e in ENGS:
            self.emit(e, None, waits=[(o, v) for o, v in snap.items() if o != e])


class SB:
    pass


def emit_ln(P, pb, sb, load_chunk, xln_dst, xln_key):
    cs = sb.consts
    m_sum = [None] * NDC
    for c in range(NDC):
        par = c % 2
        ap, w, key = load_chunk(c, 0)
        w = list(w) + P.war(("lncast", par))
        mcast = P.emit("scalar", lambda s, ap=ap, par=par: s.activation(
            sb.xb[:, par, :], ap, AF.Copy), waits=w, inc=1)
        msq = P.emit("scalar", lambda s, ap=ap, par=par: s.activation(
            sb.x2b[:, par, :], ap, AF.Square), inc=1)
        P.reads(key, mcast)
        P.reads(key, msq)
        w = [msq]
        if c == 0:
            w += P.war(("pb", 4)) + P.war(("pb", 5)) + [P.m_ones]
        P.emit("tensor", lambda t, c=c, par=par: t.matmul(
            pb[4][:, :], sb.ones_mat[:, :], sb.xb[:, par, :],
            start=(c == 0), stop=(c == NDC - 1)), waits=w, inc=1)
        m_sum[c] = P.emit("tensor", lambda t, c=c, par=par: t.matmul(
            pb[5][:, :], sb.ones_mat[:, :], sb.x2b[:, par, :],
            start=(c == 0), stop=(c == NDC - 1)), inc=1)
        P.guard[("lncast", par)] = {}
        P.reads(("lncast", par), m_sum[c])
    m1 = P.emit("vector", lambda v: v.tensor_scalar(
        sb.st_m[:, :], pb[4][0:1, :], cs[:1, C_INV2048:C_INV2048 + 1], None,
        op0=ALU.mult), waits=[m_sum[NDC - 1]], inc=1)
    m2 = P.emit("vector", lambda v: v.tensor_scalar(
        sb.st_e2[:, :], pb[5][0:1, :], cs[:1, C_INV2048:C_INV2048 + 1],
        cs[:1, C_EPS:C_EPS + 1], op0=ALU.mult, op1=ALU.add), inc=1)
    P.guard[("pb", 4)] = {}
    P.guard[("pb", 5)] = {}
    P.reads(("pb", 4), m1)
    P.reads(("pb", 5), m2)
    m3 = P.emit("vector", lambda v: v.tensor_tensor(
        sb.st_m2[:, :], sb.st_m[:, :], sb.st_m[:, :], op=ALU.mult), inc=1)
    m4 = P.emit("vector", lambda v: v.tensor_tensor(
        sb.st_e2[:, :], sb.st_e2[:, :], sb.st_m2[:, :], op=ALU.subtract), inc=1)
    m5 = P.emit("scalar", lambda s: s.activation(
        sb.st_m2[:, :], sb.st_e2[:, :], AF.Sqrt), waits=[m4], inc=1)
    m6 = P.emit("vector", lambda v: v.reciprocal(
        sb.st_rstd[:, :], sb.st_m2[:, :]),
        waits=[m5] + P.war("st_rstd"), inc=1)
    m7 = P.emit("vector", lambda v: v.scalar_tensor_tensor(
        sb.st_b[:, :], sb.st_m[:, :], cs[:1, C_NEG1:C_NEG1 + 1],
        sb.st_rstd[:, :], op0=ALU.mult, op1=ALU.mult),
        waits=P.war("st_b"), inc=1)
    mb1 = P.emit("tensor", lambda t: t.matmul(
        pb[4][:, :], sb.ones_row[:1, :], sb.st_rstd[:, :],
        start=True, stop=True), waits=[m6, m1] + P.war(("pb", 4)), inc=1)
    mb2 = P.emit("tensor", lambda t: t.matmul(
        pb[5][:, :], sb.ones_row[:1, :], sb.st_b[:, :],
        start=True, stop=True), waits=[m7, m2] + P.war(("pb", 5)), inc=1)
    P.guard["st_rstd"] = {}
    P.guard["st_b"] = {}
    P.reads("st_rstd", mb1)
    P.reads("st_b", mb2)
    mc1 = P.emit("vector", lambda v: v.tensor_copy(
        sb.bc_a[:, :], pb[4][:, :]), waits=[mb1], inc=1)
    mc2 = P.emit("vector", lambda v: v.tensor_copy(
        sb.bc_b[:, :], pb[5][:, :]), waits=[mb2], inc=1)
    P.guard[("pb", 4)] = {}
    P.guard[("pb", 5)] = {}
    P.reads(("pb", 4), mc1)
    P.reads(("pb", 5), mc2)
    mnorm = [None] * NDC
    for c in range(NDC):
        ap, w2, key2 = load_chunk(c, 1)
        w = list(w2) + ([mc2] if c == 0 else []) + (P.war(xln_key) if c == 0 else [])
        mn1 = P.emit("vector", lambda v, ap=ap: v.tensor_tensor(
            sb.t_norm[:, :], ap, sb.bc_a[:, :], op=ALU.mult),
            waits=w, inc=1)
        P.reads(key2, mn1)
        mnorm[c] = P.emit("vector", lambda v, c=c: v.tensor_tensor(
            xln_dst[:, c, :], sb.t_norm[:, :], sb.bc_b[:, :], op=ALU.add), inc=1)
    P.guard[xln_key] = {}
    return mnorm


def build(debug_streams=False):
    nc = bass.Bass("TRN2", target_bir_lowering=False)

    x_in = nc.dram_tensor("x", [S, D], F32, kind="ExternalInput")
    wq_in = nc.dram_tensor("wq", [D, D], F32, kind="ExternalInput")
    wk_in = nc.dram_tensor("wk", [D, 512], F32, kind="ExternalInput")
    wv_in = nc.dram_tensor("wv", [D, 512], F32, kind="ExternalInput")
    wo_in = nc.dram_tensor("wo", [D, D], F32, kind="ExternalInput")
    wg_in = nc.dram_tensor("wg", [D, II], F32, kind="ExternalInput")
    wu_in = nc.dram_tensor("wu", [D, II], F32, kind="ExternalInput")
    wd_in = nc.dram_tensor("wd", [II, D], F32, kind="ExternalInput")
    consts_in = nc.dram_tensor("consts", [128, 16], F32, kind="ExternalInput")
    ident_in = nc.dram_tensor("ident", [128, 128], F32, kind="ExternalInput")
    sd_in = nc.dram_tensor("sd", [3, 128, 128], F32, kind="ExternalInput")
    swap_in = nc.dram_tensor("swapm", [128, 128], F32, kind="ExternalInput")
    ctab_in = nc.dram_tensor("ctab", [128, S], F32, kind="ExternalInput")
    stab_in = nc.dram_tensor("stab", [128, S], F32, kind="ExternalInput")
    out = nc.dram_tensor("out", [S, D], F32, kind="ExternalOutput")

    skind = "ExternalOutput" if debug_streams else "Internal"
    s1 = nc.dram_tensor("s1", [NDC, 128, S], F32, kind=skind)
    s2 = nc.dram_tensor("s2", [NDC, 128, S], F32, kind=skind)
    if debug_streams:
        pt_dbg = nc.dram_tensor("pt_dbg", [128, NDC, QG], F32, kind="ExternalOutput")
        o_dbg = nc.dram_tensor("o_dbg", [128, NH, QG], F32, kind="ExternalOutput")
        q_dbg = nc.dram_tensor("q_dbg", [128, NH, QG], F32, kind="ExternalOutput")
        k_dbg = nc.dram_tensor("k_dbg", [128, NKV, S], F32, kind="ExternalOutput")
    else:
        pt_dbg = o_dbg = q_dbg = k_dbg = None

    xlnf_d = nc.dram_tensor("xlnf_d", [NDC, 128, S], BF16, kind=skind)
    wk_dbg = nc.dram_tensor("wk_c_dbg", [NKV, 128, NDC, 128], BF16) if False else None
    wg_c = nc.dram_tensor("wg_c", [NIC // 2, 128, 2, NDC, 128], BF16)
    wu_c = nc.dram_tensor("wu_c", [NIC // 2, 128, 2, NDC, 128], BF16)
    wd_c = nc.dram_tensor("wd_c", [NDC, 4, 128, 16, 128], BF16)
    wq_c = nc.dram_tensor("wq_c", [NH, 128, NDC, 128], BF16)
    wk_c = nc.dram_tensor("wk_c", [NKV, 128, NDC, 128], BF16)
    wo_c = nc.dram_tensor("wo_c", [NDC, 128, NDC, 128], BF16)
    wv_c = nc.dram_tensor("wv_c", [128, NDC, 512], BF16)

    es = ExitStack()
    with es:
        sb = SB()
        sb.consts = es.enter_context(nc.sbuf_tensor("consts_sb", [128, 16], F32))
        sb.ones_bf = es.enter_context(nc.sbuf_tensor("ones_bf", [128, 1], BF16))
        sb.ones_mat = es.enter_context(nc.sbuf_tensor("ones_mat", [128, 128], BF16))
        sb.ones_row = es.enter_context(nc.sbuf_tensor("ones_row", [1, 128], F32))
        sb.ident = es.enter_context(nc.sbuf_tensor("ident_sb", [128, 128], F32))
        sb.sd = es.enter_context(nc.sbuf_tensor("sd_sb", [128, 3, 128], BF16))
        sb.swapm = es.enter_context(nc.sbuf_tensor("swap_sb", [128, 128], F32))
        for n in ("st_m", "st_e2", "st_m2", "st_rstd", "st_b", "st_r"):
            setattr(sb, n, es.enter_context(nc.sbuf_tensor(n, [1, 512], F32)))
        sb.xb = es.enter_context(nc.sbuf_tensor("xb_sb", [128, 2, 512], BF16))
        sb.x2b = es.enter_context(nc.sbuf_tensor("x2b_sb", [128, 2, 512], BF16))
        sb.bc_a = es.enter_context(nc.sbuf_tensor("bc_a", [128, 512], F32))
        sb.bc_b = es.enter_context(nc.sbuf_tensor("bc_b", [128, 512], F32))
        sb.t_norm = es.enter_context(nc.sbuf_tensor("t_norm", [128, 512], F32))
        sb.x_fm = es.enter_context(nc.sbuf_tensor("x_fm", [128, NDC, 512], F32))
        sb.xs = es.enter_context(nc.sbuf_tensor("xs", [128, 3, 512], F32))
        sb.xout = es.enter_context(nc.sbuf_tensor("xout", [128, 2, 512], F32))

        pbig = es.enter_context(nc.psum_tensor("pbig", [128, 8, 512], F32))
        pb = [pbig[:, i, :] for i in range(8)]

        P = Prog(nc, es)

        P.dma("gpsimd", lambda g: g.dma_start(sb.consts[:, :], consts_in[:, :]))
        P.dma("gpsimd", lambda g: g.dma_start(sb.ident[:, :], ident_in[:, :]))
        P.dma("gpsimd", lambda g: g.dma_start(
            sb.sd[:, :, :], sd_in.rearrange("k p f -> p k f")))
        m_const = P.dma("gpsimd", lambda g: g.dma_start(sb.swapm[:, :], swap_in[:, :]))
        m_ones = P.emit("vector", lambda v: v.memset(sb.ones_bf[:, :], 1.0),
                        waits=[m_const], inc=1)
        m_ones = P.emit("vector", lambda v: v.memset(sb.ones_mat[:, :], 1.0), inc=1)
        m_ones = P.emit("vector", lambda v: v.memset(sb.ones_row[:, :], 1.0), inc=1)
        P.m_ones = m_ones

        wg_src = wg_in.rearrange("(dc p) (ib icl f) -> ib p icl dc f", p=128, icl=2, f=128)
        wu_src = wu_in.rearrange("(dc p) (ib icl f) -> ib p icl dc f", p=128, icl=2, f=128)
        m_wg = [None] * (NIC // 2)
        m_wu = [None] * (NIC // 2)
        wd_src = wd_in.rearrange("(ibt ibl p) (dc f) -> dc ibt p ibl f", ibl=16, p=128, f=128)
        m_wd = [[None] * 4 for _ in range(NDC)]
        wq_src = wq_in.rearrange("(dc p) (h f) -> h p dc f", p=128, f=128)
        wk_src = wk_in.rearrange("(dc p) (h f) -> h p dc f", p=128, f=128)
        wo_src = wo_in.rearrange("(dc p) (h f) -> h p dc f", p=128, f=128)
        wv_src = wv_in.rearrange("(dc p) f -> p dc f", p=128)
        for b in range(NIC // 2):
            P.dma("gpsimd", lambda g, b=b: g.dma_start(wg_c[b][:, 0], wg_src[b][:, 0]))
            m_wg[b] = P.dma("gpsimd", lambda g, b=b: g.dma_start(wg_c[b][:, 1], wg_src[b][:, 1]))
            P.dma("gpsimd", lambda g, b=b: g.dma_start(wu_c[b][:, 0], wu_src[b][:, 0]))
            m_wu[b] = P.dma("gpsimd", lambda g, b=b: g.dma_start(wu_c[b][:, 1], wu_src[b][:, 1]))
            for j in (2 * b, 2 * b + 1):
                dc, ibt = j // 4, j % 4
                m_wd[dc][ibt] = P.dma("gpsimd", lambda g, dc=dc, ibt=ibt: g.dma_start(
                    wd_c[dc, ibt], wd_src[dc, ibt]))
        m_wk = [P.dma("gpsimd", lambda g, h=h: g.dma_start(wk_c[h], wk_src[h]))
                for h in range(NKV)]
        m_wv = P.dma("gpsimd", lambda g: g.dma_start(wv_c[:, :, :], wv_src))
        m_wq = [P.dma("gpsimd", lambda g, h=h: g.dma_start(wq_c[h], wq_src[h]))
                for h in range(NH)]
        m_wo = [P.dma("gpsimd", lambda g, h=h: g.dma_start(wo_c[h], wo_src[h]))
                for h in range(NDC)]

        P.emit("vector", None, waits=[m_const])
        P.emit("scalar", None, waits=[m_const])
        P.emit("tensor", None, waits=[m_const])

        xs_rr = [0]

        def stream_chunk(src_fm, cols):
            def load(c, phase=0):
                slot = xs_rr[0] % 3
                xs_rr[0] += 1
                m = P.dma("sync", lambda e, c=c, slot=slot: e.dma_start(
                    sb.xs[:, slot, :], src_fm[c, :, cols]),
                    waits=P.war(("xs", slot)))
                P.guard[("xs", slot)] = {}
                return (sb.xs[:, slot, :], [m], ("xs", slot))
            return load

        def emit_ffn(src_tm, src_fm, dst_fm, dst_tm, tag):
            esf = ExitStack()
            with esf:
                xln = esf.enter_context(nc.sbuf_tensor("xln_f_" + tag, [128, NDC, TSUP], BF16))
                h_sp = esf.enter_context(nc.sbuf_tensor("h_sp_" + tag, [128, NIC, TSUP], BF16))
                xtm = esf.enter_context(nc.sbuf_tensor("xtm_" + tag, [128, 2, D], F32))
                wsg = esf.enter_context(nc.sbuf_tensor("wsg_" + tag, [128, 2, NDC, 128], BF16))
                wsu = esf.enter_context(nc.sbuf_tensor("wsu_" + tag, [128, 2, NDC, 128], BF16))
                wsd = esf.enter_context(nc.sbuf_tensor("wsd_" + tag, [128, 2, 16, 128], BF16))
                g_sb = esf.enter_context(nc.sbuf_tensor("g_sb_" + tag, [128, 2, TSUP], F32))
                h_f32 = esf.enter_context(nc.sbuf_tensor("h_f32_" + tag, [128, 2, TSUP], F32))
                a_bf = esf.enter_context(nc.sbuf_tensor("a_bf_" + tag, [128, 2, TSUP], BF16))
                sp_c = esf.enter_context(nc.sbuf_tensor("sp_c_" + tag, [128, TSUP], BF16))

                for st in range(NSUP):
                    cols = slice(st * TSUP, (st + 1) * TSUP)
                    if src_tm is not None:
                        for tt in range(NTT):
                            par = tt % 2
                            row0 = st * TSUP + tt * 128
                            mld = P.dma("sync", lambda e, par=par, row0=row0: e.dma_start(
                                xtm[:, par, :], src_tm[row0:row0 + 128, :]),
                                waits=P.war(("xtm", par)))
                            P.guard[("xtm", par)] = {}
                            for grp in range(4):
                                bk = 6 + grp % 2
                                w = [mld] + (P.war("x_fm") if (tt, grp) == (0, 0) else []) \
                                    + P.war(("pb", bk))
                                for k in range(4):
                                    dc = grp * 4 + k
                                    mt = P.emit("tensor", lambda t, par=par, dc=dc, bk=bk, k=k: t.transpose(
                                        pb[bk][:, k * 128:(k + 1) * 128],
                                        xtm[:, par, dc * 128:(dc + 1) * 128], sb.ident[:, :]),
                                        waits=w if k == 0 else (), inc=1)
                                P.reads(("xtm", par), mt)
                                P.guard[("pb", bk)] = {}
                                dst_ap = sb.x_fm[:, grp * 4:(grp + 1) * 4, tt * 128:(tt + 1) * 128]
                                mcp = P.emit("vector", lambda v, bk=bk, dst_ap=dst_ap: v.tensor_copy(
                                    dst_ap, pb[bk].rearrange("p (k f) -> p k f", k=4)),
                                    waits=[mt], inc=1)
                                P.reads(("pb", bk), mcp)
                        if (st, 0) == (0, 0):
                            P.guard["x_fm"] = {}
                        m_xready = [mcp]
                        def loader(c, phase=0, m_xready=tuple(m_xready)):
                            return (sb.x_fm[:, c, :],
                                    list(m_xready) if c == 0 else [], "x_fm")
                    else:
                        loader = stream_chunk(src_fm, cols)

                    mln = emit_ln(P, pb, sb, loader, xln, "xln")[NDC - 1]

                    m_hsp_last = None
                    pend = None
                    for ic in range(NIC + 1):
                        if ic < NIC:
                            par = ic % 2
                            b, icl = ic // 2, ic % 2
                            bpar = ic % 2
                            mwg_l = P.dma("sync", lambda e, b=b, icl=icl, bpar=bpar: e.dma_start(
                                wsg[:, bpar], wg_c[b][:, icl]),
                                waits=[m_wg[b]] + P.war(("wsg", bpar)))
                            mwu_l = P.dma("sync", lambda e, b=b, icl=icl, bpar=bpar: e.dma_start(
                                wsu[:, bpar], wu_c[b][:, icl]),
                                waits=[m_wu[b]] + P.war(("wsu", bpar)))
                            P.guard[("wsg", bpar)] = {}
                            P.guard[("wsu", bpar)] = {}
                            gbk, ubk = 0 + par, 2 + par
                            w = [mwg_l, mln] + P.war(("pb", gbk))
                            for dc in range(NDC):
                                mg = P.emit("tensor", lambda t, icl=icl, bpar=bpar, dc=dc, gbk=gbk: t.matmul(
                                    pb[gbk][:, :], wsg[:, bpar, dc, :], xln[:, dc, :],
                                    start=(dc == 0), stop=(dc == NDC - 1)),
                                    waits=w if dc == 0 else (), inc=1)
                            P.guard[("pb", gbk)] = {}
                            P.reads(("wsg", bpar), mg)
                            P.reads("xln", mg)
                            w = [mwu_l] + P.war(("pb", ubk))
                            for dc in range(NDC):
                                mu = P.emit("tensor", lambda t, icl=icl, bpar=bpar, dc=dc, ubk=ubk: t.matmul(
                                    pb[ubk][:, :], wsu[:, bpar, dc, :], xln[:, dc, :],
                                    start=(dc == 0), stop=(dc == NDC - 1)),
                                    waits=w if dc == 0 else (), inc=1)
                            P.guard[("pb", ubk)] = {}
                            P.reads(("wsu", bpar), mu)
                            P.reads("xln", mu)
                            msl = P.emit("scalar", lambda s, par=par, gbk=gbk: s.activation(
                                g_sb[:, par, :], pb[gbk][:, :], AF.Silu),
                                waits=[mg] + P.war(("g_sb", par)), inc=1)
                            P.guard[("g_sb", par)] = {}
                            P.reads(("pb", gbk), msl)
                            mh = P.emit("vector", lambda v, par=par, ubk=ubk: v.tensor_tensor(
                                h_f32[:, par, :], pb[ubk][:, :], g_sb[:, par, :], op=ALU.mult),
                                waits=[msl, mu] + P.war(("h_f32", par)), inc=1)
                            P.guard[("h_f32", par)] = {}
                            P.reads(("pb", ubk), mh)
                            P.reads(("g_sb", par), mh)
                            mab = P.emit("scalar", lambda s, par=par: s.activation(
                                a_bf[:, par, :], h_f32[:, par, :], AF.Abs),
                                waits=[mh] + P.war(("a_bf", par)), inc=1)
                            P.guard[("a_bf", par)] = {}
                            P.reads(("h_f32", par), mab)
                            this = (ic, par, mab)
                        if pend is not None:
                            pic, ppar, pmab = pend
                            w = [pmab] + P.war(("pb", 4)) + P.war(("pb", 5)) + P.war(("pb", 6))
                            for k in range(3):
                                msh = P.emit("tensor", lambda t, k=k, ppar=ppar: t.matmul(
                                    pb[4 + k][:, :], sb.sd[:, k, :], a_bf[:, ppar, :],
                                    start=True, stop=True), waits=w if k == 0 else (), inc=1)
                            P.guard[("pb", 4)] = {}
                            P.guard[("pb", 5)] = {}
                            P.guard[("pb", 6)] = {}
                            P.reads(("a_bf", ppar), msh)
                            cs = sb.consts
                            mg1 = P.emit("vector", lambda v: v.tensor_scalar(
                                sp_c[:, :], pb[4][:, :], cs[:, C_TAU1:C_TAU1 + 1], None,
                                op0=ALU.is_ge), waits=[msh], inc=1)
                            P.reads(("pb", 4), mg1)
                            mg2 = P.emit("vector", lambda v: v.scalar_tensor_tensor(
                                sp_c[:, :], pb[5][:, :], cs[:, C_TAU2:C_TAU2 + 1],
                                sp_c[:, :], op0=ALU.is_ge, op1=ALU.add), inc=1)
                            P.reads(("pb", 5), mg2)
                            mg3 = P.emit("vector", lambda v: v.scalar_tensor_tensor(
                                sp_c[:, :], pb[6][:, :], cs[:, C_TAU3:C_TAU3 + 1],
                                sp_c[:, :], op0=ALU.is_ge, op1=ALU.add), inc=1)
                            P.reads(("pb", 6), mg3)
                            m_hsp_last = P.emit("vector", lambda v, ppar=ppar, pic=pic: v.scalar_tensor_tensor(
                                h_sp[:, pic, :], sp_c[:, :], cs[:, C_15:C_15 + 1],
                                h_f32[:, ppar, :], op0=ALU.is_le, op1=ALU.mult),
                                waits=P.war("h_sp") if pic == 0 else (), inc=1)
                            if pic == 0:
                                P.guard["h_sp"] = {}
                            P.reads(("h_f32", ppar), m_hsp_last)
                        if ic < NIC:
                            pend = this

                    loader_b = (stream_chunk(src_fm, cols) if src_fm is not None else None)
                    for dout in range(NDC):
                        ybk = 0 + dout % 2
                        md = None
                        for ibt in range(4):
                            bpar = (dout * 4 + ibt) % 2
                            mwd_l = P.dma("sync", lambda e, dout=dout, ibt=ibt, bpar=bpar: e.dma_start(
                                wsd[:, bpar], wd_c[dout, ibt]),
                                waits=[m_wd[dout][ibt]] + P.war(("wsd", bpar)))
                            P.guard[("wsd", bpar)] = {}
                            w = [mwd_l]
                            if ibt == 0:
                                w += [m_hsp_last] + P.war(("pb", ybk))
                            for ibl in range(16):
                                ic = ibt * 16 + ibl
                                md = P.emit("tensor", lambda t, bpar=bpar, ibl=ibl, ic=ic, ybk=ybk: t.matmul(
                                    pb[ybk][:, :], wsd[:, bpar, ibl, :], h_sp[:, ic, :],
                                    start=(ic == 0), stop=(ic == NIC - 1)),
                                    waits=w if ibl == 0 else (), inc=1)
                            if ibt == 0:
                                P.guard[("pb", ybk)] = {}
                            P.reads(("wsd", bpar), md)
                            P.reads("h_sp", md)
                        opar = dout % 2
                        if dst_fm is not None:
                            mev = P.emit("vector", lambda v, opar=opar, dout=dout, ybk=ybk: v.scalar_tensor_tensor(
                                sb.xout[:, opar, :], pb[ybk][:, :], sb.consts[:, C_HALF:C_HALF + 1],
                                sb.x_fm[:, dout, :], op0=ALU.mult, op1=ALU.add),
                                waits=[md] + P.war(("xout", opar)), inc=1)
                            P.guard[("xout", opar)] = {}
                            P.reads(("pb", ybk), mev)
                            P.reads("x_fm", mev)
                            mst = P.dma("gpsimd", lambda g, opar=opar, dout=dout, cols=cols: g.dma_start(
                                dst_fm[dout, :, cols], sb.xout[:, opar, :]),
                                waits=[mev])
                            P.reads(("xout", opar), mst)
                        else:
                            xap, xw, xkey = loader_b(dout)
                            mev = P.emit("vector", lambda v, dout=dout, ybk=ybk, xap=xap: v.scalar_tensor_tensor(
                                sb.x_fm[:, dout, :], pb[ybk][:, :], sb.consts[:, C_HALF:C_HALF + 1],
                                xap, op0=ALU.mult, op1=ALU.add),
                                waits=[md] + list(xw) + (P.war("x_fm") if dout == 0 else []), inc=1)
                            if dout == 0:
                                P.guard["x_fm"] = {}
                            P.reads(("pb", ybk), mev)
                            P.reads(xkey, mev)
                            P.reads("x_fm", mev)
                            m_evac_last = mev

                    if dst_tm is not None:
                        for tt in range(NTT):
                            par = tt % 2
                            row0 = st * TSUP + tt * 128
                            for grp in range(4):
                                bk = 6 + grp % 2
                                w = [m_evac_last] + P.war(("pb", bk)) if grp < 2 else P.war(("pb", bk))
                                for k in range(4):
                                    dc = grp * 4 + k
                                    mt = P.emit("tensor", lambda t, dc=dc, bk=bk, k=k, tt=tt: t.transpose(
                                        pb[bk][:, k * 128:(k + 1) * 128],
                                        sb.x_fm[:, dc, tt * 128:(tt + 1) * 128], sb.ident[:, :]),
                                        waits=w if k == 0 else (), inc=1)
                                P.reads("x_fm", mt)
                                P.guard[("pb", bk)] = {}
                                dst_ap = xtm[:, par, grp * 512:(grp + 1) * 512]
                                mcp = P.emit("vector", lambda v, bk=bk, dst_ap=dst_ap: v.tensor_copy(
                                    dst_ap, pb[bk][:, :]),
                                    waits=[mt] + (P.war(("xtm", par)) if grp == 0 else []), inc=1)
                                if grp == 0:
                                    P.guard[("xtm", par)] = {}
                                P.reads(("pb", bk), mcp)
                            mst = P.dma("gpsimd", lambda g, par=par, row0=row0: g.dma_start(
                                dst_tm[row0:row0 + 128, :], xtm[:, par, :]),
                                waits=[mcp])
                            P.reads(("xtm", par), mst)
                P.fence()

        def emit_attn(src_fm, dst_fm):
            esf = ExitStack()
            with esf:
                xlnb = esf.enter_context(nc.sbuf_tensor("xlnb", [128, NDC, 512], BF16))
                kr = esf.enter_context(nc.sbuf_tensor("kr", [128, NKV, S], BF16))
                v_tm = esf.enter_context(nc.sbuf_tensor("v_tm", [128, NDC, 512], BF16))
                ct_sb = esf.enter_context(nc.sbuf_tensor("ct_sb", [128, 512], F32))
                st_sb = esf.enter_context(nc.sbuf_tensor("st_sb", [128, 512], F32))
                q_qg = esf.enter_context(nc.sbuf_tensor("q_qg", [128, NH, QG], BF16))
                pt = esf.enter_context(nc.sbuf_tensor("pt", [128, NDC, QG], BF16))
                o_sb = esf.enter_context(nc.sbuf_tensor("o_sb", [128, NH, QG], BF16))
                wsq = esf.enter_context(nc.sbuf_tensor("wsq", [128, 2, NDC, 128], BF16))
                wv_sb = esf.enter_context(nc.sbuf_tensor("wv_sb", [128, NDC, 512], BF16))
                q_f32 = esf.enter_context(nc.sbuf_tensor("q_f32", [128, 2, 512], F32))
                t_rope = esf.enter_context(nc.sbuf_tensor("t_rope", [128, 2, 512], F32))
                rbc = esf.enter_context(nc.sbuf_tensor("rbc", [128, 512], F32))
                cs = sb.consts

                mwv_l = P.dma("sync", lambda e: e.dma_start(wv_sb[:, :, :], wv_c[:, :, :]),
                               waits=[m_wv])

                def load_tabs(ccols):
                    m1 = P.dma("sync", lambda e, ccols=ccols: e.dma_start(
                        ct_sb[:, :], ctab_in[:, ccols]), waits=P.war("tabs"))
                    m2 = P.dma("sync", lambda e, ccols=ccols: e.dma_start(
                        st_sb[:, :], stab_in[:, ccols]), waits=P.war("tabs"))
                    P.guard["tabs"] = {}
                    return [m1, m2]

                def rope(src_bank, par2, ccols, dst_ap, scaled, extra_war_key, ready, mtab):
                    mev = P.emit("vector", lambda v, par2=par2, src_bank=src_bank: v.tensor_copy(
                        q_f32[:, par2, :], pb[src_bank][:, :]),
                        waits=[ready] + P.war(("q_f32", par2)), inc=1)
                    P.guard[("q_f32", par2)] = {}
                    P.reads(("pb", src_bank), mev)
                    msw = P.emit("tensor", lambda t, par2=par2: t.matmul(
                        pb[2 + par2][:, :], sb.swapm[:, :], q_f32[:, par2, :],
                        start=True, stop=True),
                        waits=[mev] + P.war(("pb", 2 + par2)), inc=1)
                    P.guard[("pb", 2 + par2)] = {}
                    P.reads(("q_f32", par2), msw)
                    if scaled:
                        mr1 = P.emit("vector", lambda v, par2=par2, ccols=ccols: v.scalar_tensor_tensor(
                            t_rope[:, par2, :], q_f32[:, par2, :], cs[:, C_ISQHD:C_ISQHD + 1],
                            ct_sb[:, :], op0=ALU.mult, op1=ALU.mult),
                            waits=mtab + [msw] + P.war(("t_rope", par2)), inc=1)
                        mr2 = P.emit("vector", lambda v, par2=par2, ccols=ccols: v.scalar_tensor_tensor(
                            q_f32[:, par2, :], pb[2 + par2][:, :], cs[:, C_ISQHD:C_ISQHD + 1],
                            st_sb[:, :], op0=ALU.mult, op1=ALU.mult), inc=1)
                    else:
                        mr1 = P.emit("vector", lambda v, par2=par2: v.tensor_tensor(
                            t_rope[:, par2, :], q_f32[:, par2, :], ct_sb[:, :], op=ALU.mult),
                            waits=mtab + [msw] + P.war(("t_rope", par2)), inc=1)
                        mr2 = P.emit("vector", lambda v, par2=par2: v.tensor_tensor(
                            q_f32[:, par2, :], pb[2 + par2][:, :], st_sb[:, :], op=ALU.mult),
                            inc=1)
                    P.guard[("t_rope", par2)] = {}
                    P.guard[("q_f32", par2)] = {}
                    P.reads("tabs", mr1)
                    P.reads("tabs", mr2)
                    P.reads(("pb", 2 + par2), mr2)
                    mr3 = P.emit("vector", lambda v, par2=par2, dst_ap=dst_ap: v.tensor_tensor(
                        dst_ap, t_rope[:, par2, :], q_f32[:, par2, :], op=ALU.add),
                        waits=P.war(extra_war_key), inc=1)
                    P.guard[extra_war_key] = {}
                    P.reads(("t_rope", par2), mr3)
                    P.reads(("q_f32", par2), mr3)
                    return mr3

                m_kr = None
                m_v = None
                m_xst = [None] * NSUP
                for st in range(NSUP):
                    cols = slice(st * TSUP, (st + 1) * TSUP)
                    mln = emit_ln(P, pb, sb, stream_chunk(src_fm, cols), xlnb, "xlnf")[NDC - 1]
                    m_xst[st] = P.dma("gpsimd", lambda g, cols=cols: g.dma_start(
                        xlnf_d[:, :, cols].rearrange("c p n -> p c n"), xlnb[:, :, :]),
                        waits=[mln])
                    P.reads("xlnf", m_xst[st])
                    mtab_st = load_tabs(cols)
                    for kv in range(NKV):
                        wpar = kv % 2
                        mwl = P.dma("sync", lambda e, kv=kv, wpar=wpar: e.dma_start(
                            wsq[:, wpar], wk_c[kv]),
                            waits=[m_wk[kv]] + P.war(("wsq", wpar)))
                        P.guard[("wsq", wpar)] = {}
                        bk = kv % 2
                        w = [mwl, mln] + P.war(("pb", bk))
                        for dc in range(NDC):
                            mk = P.emit("tensor", lambda t, wpar=wpar, dc=dc, bk=bk: t.matmul(
                                pb[bk][:, :], wsq[:, wpar, dc, :], xlnb[:, dc, :],
                                start=(dc == 0), stop=(dc == NDC - 1)),
                                waits=w if dc == 0 else (), inc=1)
                        P.guard[("pb", bk)] = {}
                        P.reads(("wsq", wpar), mk)
                        P.reads("xlnf", mk)
                        m_kr = rope(bk, kv % 2, cols, kr[:, kv, cols], False, "kr", mk, mtab_st)
                    for ttl in range(NTT):
                        tc = st * NTT + ttl
                        vbk = 4 + ttl % 2
                        w = [mwv_l, mln] + P.war(("pb", vbk))
                        for dc in range(NDC):
                            mv = P.emit("tensor", lambda t, ttl=ttl, dc=dc, vbk=vbk: t.matmul(
                                pb[vbk][:, :], xlnb[:, dc, ttl * 128:(ttl + 1) * 128],
                                wv_sb[:, dc, :],
                                start=(dc == 0), stop=(dc == NDC - 1)),
                                waits=w if dc == 0 else (), inc=1)
                        P.guard[("pb", vbk)] = {}
                        P.reads("xlnf", mv)
                        m_v = P.emit("vector", lambda v, tc=tc, vbk=vbk: v.tensor_copy(
                            v_tm[:, tc, :], pb[vbk][:, :]),
                            waits=[mv] + (P.war("v_tm") if tc == 0 else []), inc=1)
                        if tc == 0:
                            P.guard["v_tm"] = {}
                        P.reads(("pb", vbk), m_v)

                for qg in range(NQG):
                    ccols = slice(qg * QG, (qg + 1) * QG)
                    mxq = P.dma("sync", lambda e, ccols=ccols: e.dma_start(
                        xlnb[:, :, :], xlnf_d[:, :, ccols].rearrange("c p n -> p c n")),
                        waits=[m_xst[qg]] + P.war("xlnf"))
                    P.guard["xlnf"] = {}
                    m_qr = None
                    mtab_qg = load_tabs(ccols)
                    for h in range(NH):
                        wpar = h % 2
                        mwl = P.dma("sync", lambda e, h=h, wpar=wpar: e.dma_start(
                            wsq[:, wpar], wq_c[h]),
                            waits=[m_wq[h]] + P.war(("wsq", wpar)))
                        P.guard[("wsq", wpar)] = {}
                        bk = h % 2
                        w = [mwl, mxq] + P.war(("pb", bk))
                        for dc in range(NDC):
                            mq = P.emit("tensor", lambda t, wpar=wpar, dc=dc, bk=bk: t.matmul(
                                pb[bk][:, :], wsq[:, wpar, dc, :], xlnb[:, dc, :],
                                start=(dc == 0), stop=(dc == NDC - 1)),
                                waits=w if dc == 0 else (), inc=1)
                        P.guard[("pb", bk)] = {}
                        P.reads(("wsq", wpar), mq)
                        P.reads("xlnf", mq)
                        m_qr = rope(bk, h % 2, ccols, q_qg[:, h, :], True, ("q_qg", h), mq, mtab_qg)

                    for h in range(NH):
                        kvh = h % NKV
                        obk = 5 + h % 2
                        m_exp_pr = [None] * 8
                        msum = None
                        mo = None
                        pend_pr = None

                        def emit_pair_consumers(pr):
                            nonlocal msum, mo
                            kc0 = 2 * pr
                            w = [m_exp_pr[pr]]
                            if pr == 0:
                                w += [P.m_ones] + P.war(("pb", 4))
                            for kc in (kc0, kc0 + 1):
                                msum_ = P.emit("tensor", lambda t, kc=kc: t.matmul(
                                    pb[4][:, :], sb.ones_mat[:, :], pt[:, kc, :],
                                    start=(kc == 0), stop=(kc == NDC - 1)),
                                    waits=w if kc == kc0 else (), inc=1)
                            if pr == 0:
                                P.guard[("pb", 4)] = {}
                            w2 = []
                            if pr == 0:
                                w2 = [m_v] + P.war(("pb", obk))
                            for kc in (kc0, kc0 + 1):
                                mo_ = P.emit("tensor", lambda t, kc=kc, kvh=kvh, obk=obk: t.matmul(
                                    pb[obk][:, :], v_tm[:, kc, kvh * 128:(kvh + 1) * 128], pt[:, kc, :],
                                    start=(kc == 0), stop=(kc == NDC - 1)),
                                    waits=w2 if kc == kc0 else (), inc=1)
                            if pr == 0:
                                P.guard[("pb", obk)] = {}
                            P.reads("pt", mo_)
                            msum, mo = msum_, mo_

                        for pr in range(8):
                            pbase = (pr % 2) * 2
                            w = [m_qr, m_kr] if pr == 0 else []
                            w = w + P.war(("pb", pbase)) + P.war(("pb", pbase + 1))
                            for j in (0, 1):
                                kc = 2 * pr + j
                                msmm = P.emit("tensor", lambda t, kvh=kvh, kc=kc, bk=pbase + j, h=h: t.matmul(
                                    pb[bk][:, :], kr[:, kvh, kc * 128:(kc + 1) * 128], q_qg[:, h, :],
                                    start=True, stop=True), waits=w if j == 0 else (), inc=1)
                            P.guard[("pb", pbase)] = {}
                            P.guard[("pb", pbase + 1)] = {}
                            P.reads(("q_qg", h), msmm)
                            P.reads("kr", msmm)
                            w2 = [msmm]
                            if pr == 0:
                                w2 += P.war("pt")
                                P.guard["pt"] = {}
                            m_exp_pr[pr] = P.emit("scalar", lambda s, pr=pr, pbase=pbase: s.activation(
                                pt[:, 2 * pr:2 * pr + 2, :], pbig[:, pbase:pbase + 2, :],
                                AF.Exp, bias=np.float32(-EXP_SHIFT)),
                                waits=w2, inc=1)
                            P.reads(("pb", pbase), m_exp_pr[pr])
                            P.reads(("pb", pbase + 1), m_exp_pr[pr])
                            if pend_pr is not None:
                                emit_pair_consumers(pend_pr)
                            pend_pr = pr
                        emit_pair_consumers(pend_pr)
                        if debug_streams and qg == 0 and h == 0:
                            mdump = P.dma("gpsimd", lambda g: g.dma_start(
                                pt_dbg[:, :, :], pt[:, :, :]), waits=[mo])
                            P.reads("pt", mdump)
                        mr = P.emit("vector", lambda v: v.reciprocal(
                            sb.st_r[:, :], pb[4][0:1, :]),
                            waits=[msum] + P.war("st_r"), inc=1)
                        P.guard["st_r"] = {}
                        P.reads(("pb", 4), mr)
                        mb = P.emit("tensor", lambda t: t.matmul(
                            pb[7][:, :], sb.ones_row[:1, :], sb.st_r[:, :],
                            start=True, stop=True),
                            waits=[mr] + P.war(("pb", 7)), inc=1)
                        P.guard[("pb", 7)] = {}
                        P.reads("st_r", mb)
                        mbc = P.emit("vector", lambda v: v.tensor_copy(
                            rbc[:, :], pb[7][:, :]),
                            waits=[mb] + P.war("rbc"), inc=1)
                        P.guard["rbc"] = {}
                        P.reads(("pb", 7), mbc)
                        moev = P.emit("vector", lambda v, h=h, obk=obk: v.tensor_tensor(
                            o_sb[:, h, :], pb[obk][:, :], rbc[:, :], op=ALU.mult),
                            waits=[mo, mbc] + (P.war("o_sb") if h == 0 else []), inc=1)
                        if h == 0:
                            P.guard["o_sb"] = {}
                        P.reads(("pb", obk), moev)
                        P.reads("rbc", moev)
                        m_oev = moev

                    if debug_streams and qg == 0:
                        P.dma("gpsimd", lambda g: g.dma_start(o_dbg[:, :, :], o_sb[:, :, :]),
                              waits=[m_oev])
                        P.dma("gpsimd", lambda g: g.dma_start(q_dbg[:, :, :], q_qg[:, :, :]),
                              waits=[m_qr])
                        P.dma("gpsimd", lambda g: g.dma_start(k_dbg[:, :, :], kr[:, :, :]),
                              waits=[m_kr])
                    mxl = P.dma("sync", lambda e, ccols=ccols: e.dma_start(
                        sb.x_fm[:, :, :], src_fm[:, :, ccols].rearrange("c p n -> p c n")),
                        waits=P.war("x_fm"))
                    P.guard["x_fm"] = {}
                    for dout in range(NDC):
                        wpar = dout % 2
                        mwl = P.dma("sync", lambda e, dout=dout, wpar=wpar: e.dma_start(
                            wsq[:, wpar], wo_c[dout]),
                            waits=[m_wo[dout]] + P.war(("wsq", wpar)))
                        P.guard[("wsq", wpar)] = {}
                        ybk = 2 + dout % 2
                        w = [mwl, m_oev] + P.war(("pb", ybk))
                        my = None
                        for dinc in range(NDC):
                            my = P.emit("tensor", lambda t, dout=dout, dinc=dinc, wpar=wpar, ybk=ybk: t.matmul(
                                pb[ybk][:, :], wsq[:, wpar, dinc, :], o_sb[:, dinc, :],
                                start=(dinc == 0), stop=(dinc == NDC - 1)),
                                waits=w if dinc == 0 else (), inc=1)
                        P.guard[("pb", ybk)] = {}
                        P.reads(("wsq", wpar), my)
                        P.reads("o_sb", my)
                        opar = dout % 2
                        mev = P.emit("vector", lambda v, opar=opar, dout=dout, ybk=ybk: v.tensor_tensor(
                            sb.xout[:, opar, :], pb[ybk][:, :], sb.x_fm[:, dout, :], op=ALU.add),
                            waits=[my, mxl] + P.war(("xout", opar)), inc=1)
                        P.guard[("xout", opar)] = {}
                        P.reads(("pb", ybk), mev)
                        P.reads("x_fm", mev)
                        mst = P.dma("gpsimd", lambda g, opar=opar, dout=dout, ccols=ccols: g.dma_start(
                            dst_fm[dout, :, ccols], sb.xout[:, opar, :]),
                            waits=[mev])
                        P.reads(("xout", opar), mst)
                P.fence()

        emit_ffn(src_tm=x_in, src_fm=None, dst_fm=s1, dst_tm=None, tag="f1")
        emit_attn(src_fm=s1, dst_fm=s2)
        emit_ffn(src_tm=None, src_fm=s2, dst_fm=None, dst_tm=out, tag="f2")

        P.emit("gpsimd", None, waits=[(f"gpsimd_d{i}", P.counts[f"gpsimd_d{i}"])
                                      for i in range(P.NDMASEM)])

        block = es.enter_context(nc.Block())

        @block.sync
        def _(e):
            for f in P.steps["sync"]:
                f(e)

        @block.gpsimd
        def _(e):
            for f in P.steps["gpsimd"]:
                f(e)

        @block.tensor
        def _(e):
            for f in P.steps["tensor"]:
                f(e)

        @block.vector
        def _(e):
            for f in P.steps["vector"]:
                f(e)

        @block.scalar
        def _(e):
            for f in P.steps["scalar"]:
                f(e)

    return nc


def host_constants():
    TINY = np.float32(2.0 ** -126)
    consts = np.zeros((128, 16), dtype=np.float32)
    p = np.arange(128)
    for k in (1, 2, 3):
        consts[:, k - 1] = np.where((p % 4) + k < 4, TINY, 0.0)
    consts[:, C_15] = 1.5
    consts[:, C_INV2048] = 1.0 / 2048.0
    consts[:, C_EPS] = 1e-5
    consts[:, C_NEG1] = -1.0
    consts[:, C_HALF] = 0.5
    consts[:, C_ISQHD] = 1.0 / np.sqrt(np.float32(HD))
    consts[:, C_ONE] = 1.0
    ident = np.eye(128, dtype=np.float32)
    sd = np.zeros((3, 128, 128), dtype=np.float32)
    for k in (1, 2, 3):
        for c in range(128):
            r = 4 * (c // 4) + (c + k) % 4
            sd[k - 1, r, c] += 1.0
            sd[k - 1, c, c] -= 1.0
    swapm = np.zeros((128, 128), dtype=np.float32)
    for c in range(128):
        r = c + 1 if c % 2 == 0 else c - 1
        swapm[r, c] = 1.0
    half = HD // 2
    inv = 1.0 / 10000.0 ** (np.arange(half, dtype=np.float64) / half)
    pos = np.arange(S, dtype=np.float64)
    ang = pos[None, :] * inv[:, None]
    ctab = np.repeat(np.cos(ang), 2, axis=0).astype(np.float32)
    sin2 = np.repeat(np.sin(ang), 2, axis=0)
    sign = np.where(np.arange(128) % 2 == 0, -1.0, 1.0)
    stab = (sin2 * sign[:, None]).astype(np.float32)
    return consts, ident, sd, swapm, ctab, stab


def kernel(L0_attn_x, L0_attn_W_q, L0_attn_W_k, L0_attn_W_v, L0_attn_W_o,
           L0_ffn_w_gate, L0_ffn_w_up, L0_ffn_w_down):
    x = np.ascontiguousarray(np.asarray(L0_attn_x, dtype=np.float32))
    consts, ident, sd, swapm, ctab, stab = host_constants()
    common = {
        "wq": np.ascontiguousarray(np.asarray(L0_attn_W_q, dtype=np.float32)),
        "wk": np.ascontiguousarray(np.asarray(L0_attn_W_k, dtype=np.float32)),
        "wv": np.ascontiguousarray(np.asarray(L0_attn_W_v, dtype=np.float32)),
        "wo": np.ascontiguousarray(np.asarray(L0_attn_W_o, dtype=np.float32)),
        "wg": np.ascontiguousarray(np.asarray(L0_ffn_w_gate, dtype=np.float32)),
        "wu": np.ascontiguousarray(np.asarray(L0_ffn_w_up, dtype=np.float32)),
        "wd": np.ascontiguousarray(np.asarray(L0_ffn_w_down, dtype=np.float32)),
        "consts": consts, "ident": ident, "sd": sd, "swapm": swapm,
        "ctab": ctab, "stab": stab,
    }
    nc = build()
    in_maps = [dict(common, x=np.ascontiguousarray(x[i])) for i in range(8)]
    res = run_bass_kernel_spmd(nc, in_maps, core_ids=list(range(8)))
    return np.stack([res.results[i]["out"] for i in range(8)], axis=0)


# version 13
# speedup vs baseline: 1.0701x; 1.0042x over previous
import numpy as np
import concourse.bass as bass
import concourse.mybir as mybir
from concourse.bass_utils import run_bass_kernel_spmd
from contextlib import ExitStack

F32 = mybir.dt.float32
BF16 = mybir.dt.bfloat16
AF = mybir.ActivationFunctionType
ALU = mybir.AluOpType

D = 2048
S = 2048
II = 8192
NH, NKV, HD = 16, 4, 128
NDC = D // 128
NIC = II // 128
TSUP = 512
NSUP = S // TSUP
QG = 512
NQG = S // QG
NTT = TSUP // 128

C_TAU1, C_TAU2, C_TAU3, C_15, C_INV2048, C_EPS, C_NEG1, C_HALF, C_ISQHD, C_ONE = range(10)
EXP_SHIFT = 10.0

ENGS = ("sync", "gpsimd", "tensor", "vector", "scalar")


class Prog:

    NDMASEM = 8

    def __init__(self, nc, es):
        self.nc = nc
        self.steps = {e: [] for e in ENGS}
        self.sems = {e: es.enter_context(nc.semaphore("sem_" + e)) for e in ENGS}
        for q in ("sync", "gpsimd"):
            for i in range(self.NDMASEM):
                n = f"{q}_d{i}"
                self.sems[n] = es.enter_context(nc.semaphore("sem_" + n))
        self.counts = {e: 0 for e in self.sems}
        self.rr = {"sync": 0, "gpsimd": 0}
        self.guard = {}

    def emit(self, eng, fn, waits=(), inc=0):
        sems = self.sems
        wl = []
        for w in waits:
            if w is None:
                continue
            en, val = w
            if val > 0:
                wl.append((en, val))
        wl = tuple(wl)

        def step(e, fn=fn, wl=wl, inc=inc, sem=self.sems[eng]):
            for (en, val) in wl:
                e.wait_ge(sems[en], val)
            ins = fn(e) if fn is not None else None
            if inc:
                ins.then_inc(sem, inc)

        self.steps[eng].append(step)
        if inc:
            self.counts[eng] += inc
        return (eng, self.counts[eng])

    def dma(self, queue, fn, waits=()):
        i = self.rr[queue] % self.NDMASEM
        self.rr[queue] += 1
        sname = f"{queue}_d{i}"
        w = list(waits)
        if self.counts[sname] > 0:
            w.append((sname, self.counts[sname]))
        sems = self.sems
        wl = []
        for x in w:
            if x is None:
                continue
            en, val = x
            if val > 0:
                wl.append((en, val))
        wl = tuple(wl)

        def step(e, fn=fn, wl=wl, sem=self.sems[sname]):
            for (en, val) in wl:
                e.wait_ge(sems[en], val)
            fn(e).then_inc(sem, 16)

        self.steps[queue].append(step)
        self.counts[sname] += 16
        return (sname, self.counts[sname])

    def reads(self, key, milestone):
        d = self.guard.setdefault(key, {})
        d[milestone[0]] = max(d.get(milestone[0], 0), milestone[1])

    def war(self, key):
        d = self.guard.get(key, {})
        return [(e, v) for e, v in d.items()]

    def fence(self):
        snap = dict(self.counts)
        for e in ENGS:
            self.emit(e, None, waits=[(o, v) for o, v in snap.items() if o != e])


class SB:
    pass


def emit_ln(P, pb, sb, load_chunk, xln_dst, xln_key):
    cs = sb.consts
    m_sum = [None] * NDC
    for c in range(NDC):
        par = c % 2
        ap, w, key = load_chunk(c, 0)
        w = list(w) + P.war(("lncast", par))
        mcast = P.emit("scalar", lambda s, ap=ap, par=par: s.activation(
            sb.xb[:, par, :], ap, AF.Copy), waits=w, inc=1)
        msq = P.emit("scalar", lambda s, ap=ap, par=par: s.activation(
            sb.x2b[:, par, :], ap, AF.Square), inc=1)
        P.reads(key, mcast)
        P.reads(key, msq)
        w = [msq]
        if c == 0:
            w += P.war(("pb", 4)) + P.war(("pb", 5)) + [P.m_ones]
        P.emit("tensor", lambda t, c=c, par=par: t.matmul(
            pb[4][:, :], sb.ones_mat[:, :], sb.xb[:, par, :],
            start=(c == 0), stop=(c == NDC - 1)), waits=w, inc=1)
        m_sum[c] = P.emit("tensor", lambda t, c=c, par=par: t.matmul(
            pb[5][:, :], sb.ones_mat[:, :], sb.x2b[:, par, :],
            start=(c == 0), stop=(c == NDC - 1)), inc=1)
        P.guard[("lncast", par)] = {}
        P.reads(("lncast", par), m_sum[c])
    m1 = P.emit("vector", lambda v: v.tensor_scalar(
        sb.bc_m[:, :], pb[4][:, :], cs[:, C_INV2048:C_INV2048 + 1], None,
        op0=ALU.mult), waits=[m_sum[NDC - 1]] + P.war("bc_m"), inc=1)
    m2 = P.emit("vector", lambda v: v.tensor_scalar(
        sb.t_norm[:, :], pb[5][:, :], cs[:, C_INV2048:C_INV2048 + 1],
        cs[:, C_EPS:C_EPS + 1], op0=ALU.mult, op1=ALU.add), inc=1)
    P.guard[("pb", 4)] = {}
    P.guard[("pb", 5)] = {}
    P.guard["bc_m"] = {}
    P.reads(("pb", 4), m1)
    P.reads(("pb", 5), m2)
    m3 = P.emit("vector", lambda v: v.tensor_tensor(
        sb.bc_b[:, :], sb.bc_m[:, :], sb.bc_m[:, :], op=ALU.mult),
        waits=P.war("bc_b"), inc=1)
    m4 = P.emit("vector", lambda v: v.tensor_tensor(
        sb.bc_b[:, :], sb.t_norm[:, :], sb.bc_b[:, :], op=ALU.subtract), inc=1)
    m5 = P.emit("scalar", lambda s: s.activation(
        sb.bc_ln[:, :], sb.bc_b[:, :], AF.Ln), waits=[m4] + P.war("bc_ln"), inc=1)
    m6 = P.emit("scalar", lambda s: s.activation(
        sb.bc_a[:, :], sb.bc_ln[:, :], AF.Exp, scale=np.float32(-0.5)),
        waits=P.war("bc_a"), inc=1)
    P.guard["bc_ln"] = {}
    P.guard["bc_a"] = {}
    P.reads("bc_ln", m6)
    m7 = P.emit("vector", lambda v: v.scalar_tensor_tensor(
        sb.bc_b[:, :], sb.bc_m[:, :], cs[:, C_NEG1:C_NEG1 + 1],
        sb.bc_a[:, :], op0=ALU.mult, op1=ALU.mult),
        waits=[m6], inc=1)
    P.reads("bc_m", m7)
    P.reads("bc_a", m7)
    mnorm = [None] * NDC
    for c in range(NDC):
        ap, w2, key2 = load_chunk(c, 1)
        w = list(w2) + ([m6] if c == 0 else []) + (P.war(xln_key) if c == 0 else [])
        mn1 = P.emit("vector", lambda v, ap=ap: v.tensor_tensor(
            sb.t_norm[:, :], ap, sb.bc_a[:, :], op=ALU.mult),
            waits=w, inc=1)
        P.reads(key2, mn1)
        P.reads("bc_a", mn1)
        mnorm[c] = P.emit("vector", lambda v, c=c: v.tensor_tensor(
            xln_dst[:, c, :], sb.t_norm[:, :], sb.bc_b[:, :], op=ALU.add), inc=1)
        P.reads("bc_b", mnorm[c])
    P.guard[xln_key] = {}
    return mnorm


def build(debug_streams=False):
    nc = bass.Bass("TRN2", target_bir_lowering=False)

    x_in = nc.dram_tensor("x", [S, D], F32, kind="ExternalInput")
    wq_in = nc.dram_tensor("wq", [D, D], F32, kind="ExternalInput")
    wk_in = nc.dram_tensor("wk", [D, 512], F32, kind="ExternalInput")
    wv_in = nc.dram_tensor("wv", [D, 512], F32, kind="ExternalInput")
    wo_in = nc.dram_tensor("wo", [D, D], F32, kind="ExternalInput")
    wg_in = nc.dram_tensor("wg", [D, II], F32, kind="ExternalInput")
    wu_in = nc.dram_tensor("wu", [D, II], F32, kind="ExternalInput")
    wd_in = nc.dram_tensor("wd", [II, D], F32, kind="ExternalInput")
    consts_in = nc.dram_tensor("consts", [128, 16], F32, kind="ExternalInput")
    ident_in = nc.dram_tensor("ident", [128, 128], F32, kind="ExternalInput")
    sd_in = nc.dram_tensor("sd", [3, 128, 128], F32, kind="ExternalInput")
    swap_in = nc.dram_tensor("swapm", [128, 128], F32, kind="ExternalInput")
    ctab_in = nc.dram_tensor("ctab", [128, S], F32, kind="ExternalInput")
    stab_in = nc.dram_tensor("stab", [128, S], F32, kind="ExternalInput")
    out = nc.dram_tensor("out", [S, D], F32, kind="ExternalOutput")

    skind = "ExternalOutput" if debug_streams else "Internal"
    s1 = nc.dram_tensor("s1", [NDC, 128, S], F32, kind=skind)
    s2 = nc.dram_tensor("s2", [NDC, 128, S], F32, kind=skind)
    if debug_streams:
        pt_dbg = nc.dram_tensor("pt_dbg", [128, NDC, QG], F32, kind="ExternalOutput")
        o_dbg = nc.dram_tensor("o_dbg", [128, NH, QG], F32, kind="ExternalOutput")
        q_dbg = nc.dram_tensor("q_dbg", [128, NH, QG], F32, kind="ExternalOutput")
        k_dbg = nc.dram_tensor("k_dbg", [128, NKV, S], F32, kind="ExternalOutput")
    else:
        pt_dbg = o_dbg = q_dbg = k_dbg = None

    xlnf_d = nc.dram_tensor("xlnf_d", [NDC, 128, S], BF16, kind=skind)
    wk_dbg = nc.dram_tensor("wk_c_dbg", [NKV, 128, NDC, 128], BF16) if False else None
    wg_c = nc.dram_tensor("wg_c", [NIC // 2, 128, 2, NDC, 128], BF16)
    wu_c = nc.dram_tensor("wu_c", [NIC // 2, 128, 2, NDC, 128], BF16)
    wd_c = nc.dram_tensor("wd_c", [NDC, 4, 128, 16, 128], BF16)
    wq_c = nc.dram_tensor("wq_c", [NH, 128, NDC, 128], BF16)
    wk_c = nc.dram_tensor("wk_c", [NKV, 128, NDC, 128], BF16)
    wo_c = nc.dram_tensor("wo_c", [NDC, 128, NDC, 128], BF16)
    wv_c = nc.dram_tensor("wv_c", [128, NDC, 512], BF16)

    es = ExitStack()
    with es:
        sb = SB()
        sb.consts = es.enter_context(nc.sbuf_tensor("consts_sb", [128, 16], F32))
        sb.ones_mat = es.enter_context(nc.sbuf_tensor("ones_mat", [128, 128], BF16))
        sb.ident = es.enter_context(nc.sbuf_tensor("ident_sb", [128, 128], F32))
        sb.sd = es.enter_context(nc.sbuf_tensor("sd_sb", [128, 3, 128], BF16))
        sb.swapm = es.enter_context(nc.sbuf_tensor("swap_sb", [128, 128], F32))
        sb.bc_m = es.enter_context(nc.sbuf_tensor("bc_m", [128, 512], F32))
        sb.bc_ln = es.enter_context(nc.sbuf_tensor("bc_ln", [128, 512], F32))
        sb.xb = es.enter_context(nc.sbuf_tensor("xb_sb", [128, 2, 512], BF16))
        sb.x2b = es.enter_context(nc.sbuf_tensor("x2b_sb", [128, 2, 512], BF16))
        sb.bc_a = es.enter_context(nc.sbuf_tensor("bc_a", [128, 512], F32))
        sb.bc_b = es.enter_context(nc.sbuf_tensor("bc_b", [128, 512], F32))
        sb.t_norm = es.enter_context(nc.sbuf_tensor("t_norm", [128, 512], F32))
        sb.x_fm = es.enter_context(nc.sbuf_tensor("x_fm", [128, NDC, 512], F32))
        sb.xs = es.enter_context(nc.sbuf_tensor("xs", [128, 3, 512], F32))
        sb.xout = es.enter_context(nc.sbuf_tensor("xout", [128, 2, 512], F32))

        pbig = es.enter_context(nc.psum_tensor("pbig", [128, 8, 512], F32))
        pb = [pbig[:, i, :] for i in range(8)]

        P = Prog(nc, es)

        P.dma("gpsimd", lambda g: g.dma_start(sb.consts[:, :], consts_in[:, :]))
        P.dma("gpsimd", lambda g: g.dma_start(sb.ident[:, :], ident_in[:, :]))
        P.dma("gpsimd", lambda g: g.dma_start(
            sb.sd[:, :, :], sd_in.rearrange("k p f -> p k f")))
        m_const = P.dma("gpsimd", lambda g: g.dma_start(sb.swapm[:, :], swap_in[:, :]))
        m_ones = P.emit("vector", lambda v: v.memset(sb.ones_mat[:, :], 1.0),
                        waits=[m_const], inc=1)
        P.m_ones = m_ones

        wg_src = wg_in.rearrange("(dc p) (ib icl f) -> ib p icl dc f", p=128, icl=2, f=128)
        wu_src = wu_in.rearrange("(dc p) (ib icl f) -> ib p icl dc f", p=128, icl=2, f=128)
        m_wg = [None] * (NIC // 2)
        m_wu = [None] * (NIC // 2)
        wd_src = wd_in.rearrange("(ibt ibl p) (dc f) -> dc ibt p ibl f", ibl=16, p=128, f=128)
        m_wd = [[None] * 4 for _ in range(NDC)]
        wq_src = wq_in.rearrange("(dc p) (h f) -> h p dc f", p=128, f=128)
        wk_src = wk_in.rearrange("(dc p) (h f) -> h p dc f", p=128, f=128)
        wo_src = wo_in.rearrange("(dc p) (h f) -> h p dc f", p=128, f=128)
        wv_src = wv_in.rearrange("(dc p) f -> p dc f", p=128)
        for b in range(NIC // 2):
            P.dma("gpsimd", lambda g, b=b: g.dma_start(wg_c[b][:, 0], wg_src[b][:, 0]))
            m_wg[b] = P.dma("gpsimd", lambda g, b=b: g.dma_start(wg_c[b][:, 1], wg_src[b][:, 1]))
            P.dma("gpsimd", lambda g, b=b: g.dma_start(wu_c[b][:, 0], wu_src[b][:, 0]))
            m_wu[b] = P.dma("gpsimd", lambda g, b=b: g.dma_start(wu_c[b][:, 1], wu_src[b][:, 1]))
        for j in range(NIC):
            dc, ibt = j // 4, j % 4
            m_wd[dc][ibt] = P.dma("gpsimd", lambda g, dc=dc, ibt=ibt: g.dma_start(
                wd_c[dc, ibt], wd_src[dc, ibt]))
        m_wk = [P.dma("gpsimd", lambda g, h=h: g.dma_start(wk_c[h], wk_src[h]))
                for h in range(NKV)]
        m_wv = P.dma("gpsimd", lambda g: g.dma_start(wv_c[:, :, :], wv_src))
        m_wq = [P.dma("gpsimd", lambda g, h=h: g.dma_start(wq_c[h], wq_src[h]))
                for h in range(NH)]
        m_wo = [P.dma("gpsimd", lambda g, h=h: g.dma_start(wo_c[h], wo_src[h]))
                for h in range(NDC)]

        P.emit("vector", None, waits=[m_const])
        P.emit("scalar", None, waits=[m_const])
        P.emit("tensor", None, waits=[m_const])

        xs_rr = [0]

        def stream_chunk(src_fm, cols):
            def load(c, phase=0):
                slot = xs_rr[0] % 3
                xs_rr[0] += 1
                m = P.dma("sync", lambda e, c=c, slot=slot: e.dma_start(
                    sb.xs[:, slot, :], src_fm[c, :, cols]),
                    waits=P.war(("xs", slot)))
                P.guard[("xs", slot)] = {}
                return (sb.xs[:, slot, :], [m], ("xs", slot))
            return load

        def emit_ffn(src_tm, src_fm, dst_fm, dst_tm, tag):
            esf = ExitStack()
            with esf:
                xln = esf.enter_context(nc.sbuf_tensor("xln_f_" + tag, [128, NDC, TSUP], BF16))
                h_sp = esf.enter_context(nc.sbuf_tensor("h_sp_" + tag, [128, NIC, TSUP], BF16))
                xtm = esf.enter_context(nc.sbuf_tensor("xtm_" + tag, [128, 2, D], F32))
                wsg = esf.enter_context(nc.sbuf_tensor("wsg_" + tag, [128, 2, NDC, 128], BF16))
                wsu = esf.enter_context(nc.sbuf_tensor("wsu_" + tag, [128, 2, NDC, 128], BF16))
                wsd = esf.enter_context(nc.sbuf_tensor("wsd_" + tag, [128, 2, 16, 128], BF16))
                g_sb = esf.enter_context(nc.sbuf_tensor("g_sb_" + tag, [128, 2, TSUP], F32))
                h_f32 = esf.enter_context(nc.sbuf_tensor("h_f32_" + tag, [128, 2, TSUP], F32))
                a_bf = esf.enter_context(nc.sbuf_tensor("a_bf_" + tag, [128, 2, TSUP], BF16))
                sp_c = esf.enter_context(nc.sbuf_tensor("sp_c_" + tag, [128, TSUP], BF16))

                for st in range(NSUP):
                    cols = slice(st * TSUP, (st + 1) * TSUP)
                    if src_tm is not None:
                        for tt in range(NTT):
                            par = tt % 2
                            row0 = st * TSUP + tt * 128
                            mld = P.dma("sync", lambda e, par=par, row0=row0: e.dma_start(
                                xtm[:, par, :], src_tm[row0:row0 + 128, :]),
                                waits=P.war(("xtm", par)))
                            P.guard[("xtm", par)] = {}
                            for grp in range(4):
                                bk = 6 + grp % 2
                                w = [mld] + (P.war("x_fm") if (tt, grp) == (0, 0) else []) \
                                    + P.war(("pb", bk))
                                for k in range(4):
                                    dc = grp * 4 + k
                                    mt = P.emit("tensor", lambda t, par=par, dc=dc, bk=bk, k=k: t.transpose(
                                        pb[bk][:, k * 128:(k + 1) * 128],
                                        xtm[:, par, dc * 128:(dc + 1) * 128], sb.ident[:, :]),
                                        waits=w if k == 0 else (), inc=1)
                                P.reads(("xtm", par), mt)
                                P.guard[("pb", bk)] = {}
                                dst_ap = sb.x_fm[:, grp * 4:(grp + 1) * 4, tt * 128:(tt + 1) * 128]
                                mcp = P.emit("vector", lambda v, bk=bk, dst_ap=dst_ap: v.tensor_copy(
                                    dst_ap, pb[bk].rearrange("p (k f) -> p k f", k=4)),
                                    waits=[mt], inc=1)
                                P.reads(("pb", bk), mcp)
                        if (st, 0) == (0, 0):
                            P.guard["x_fm"] = {}
                        m_xready = [mcp]
                        def loader(c, phase=0, m_xready=tuple(m_xready)):
                            return (sb.x_fm[:, c, :],
                                    list(m_xready) if c == 0 else [], "x_fm")
                    else:
                        loader = stream_chunk(src_fm, cols)

                    mln = emit_ln(P, pb, sb, loader, xln, "xln")[NDC - 1]

                    m_hsp_last = None
                    pend = None
                    for ic in range(NIC + 1):
                        if ic < NIC:
                            par = ic % 2
                            b, icl = ic // 2, ic % 2
                            bpar = ic % 2
                            mwg_l = P.dma("sync", lambda e, b=b, icl=icl, bpar=bpar: e.dma_start(
                                wsg[:, bpar], wg_c[b][:, icl]),
                                waits=[m_wg[b]] + P.war(("wsg", bpar)))
                            mwu_l = P.dma("sync", lambda e, b=b, icl=icl, bpar=bpar: e.dma_start(
                                wsu[:, bpar], wu_c[b][:, icl]),
                                waits=[m_wu[b]] + P.war(("wsu", bpar)))
                            P.guard[("wsg", bpar)] = {}
                            P.guard[("wsu", bpar)] = {}
                            gbk, ubk = 0 + par, 2 + par
                            w = [mwg_l, mln] + P.war(("pb", gbk))
                            for dc in range(NDC):
                                mg = P.emit("tensor", lambda t, icl=icl, bpar=bpar, dc=dc, gbk=gbk: t.matmul(
                                    pb[gbk][:, :], wsg[:, bpar, dc, :], xln[:, dc, :],
                                    start=(dc == 0), stop=(dc == NDC - 1)),
                                    waits=w if dc == 0 else (), inc=1)
                            P.guard[("pb", gbk)] = {}
                            P.reads(("wsg", bpar), mg)
                            P.reads("xln", mg)
                            w = [mwu_l] + P.war(("pb", ubk))
                            for dc in range(NDC):
                                mu = P.emit("tensor", lambda t, icl=icl, bpar=bpar, dc=dc, ubk=ubk: t.matmul(
                                    pb[ubk][:, :], wsu[:, bpar, dc, :], xln[:, dc, :],
                                    start=(dc == 0), stop=(dc == NDC - 1)),
                                    waits=w if dc == 0 else (), inc=1)
                            P.guard[("pb", ubk)] = {}
                            P.reads(("wsu", bpar), mu)
                            P.reads("xln", mu)
                            msl = P.emit("scalar", lambda s, par=par, gbk=gbk: s.activation(
                                g_sb[:, par, :], pb[gbk][:, :], AF.Silu),
                                waits=[mg] + P.war(("g_sb", par)), inc=1)
                            P.guard[("g_sb", par)] = {}
                            P.reads(("pb", gbk), msl)
                            mh = P.emit("vector", lambda v, par=par, ubk=ubk: v.tensor_tensor(
                                h_f32[:, par, :], pb[ubk][:, :], g_sb[:, par, :], op=ALU.mult),
                                waits=[msl, mu] + P.war(("h_f32", par)), inc=1)
                            P.guard[("h_f32", par)] = {}
                            P.reads(("pb", ubk), mh)
                            P.reads(("g_sb", par), mh)
                            mab = P.emit("scalar", lambda s, par=par: s.activation(
                                a_bf[:, par, :], h_f32[:, par, :], AF.Abs),
                                waits=[mh] + P.war(("a_bf", par)), inc=1)
                            P.guard[("a_bf", par)] = {}
                            P.reads(("h_f32", par), mab)
                            this = (ic, par, mab)
                        if pend is not None:
                            pic, ppar, pmab = pend
                            w = [pmab] + P.war(("pb", 4)) + P.war(("pb", 5)) + P.war(("pb", 6))
                            for k in range(3):
                                msh = P.emit("tensor", lambda t, k=k, ppar=ppar: t.matmul(
                                    pb[4 + k][:, :], sb.sd[:, k, :], a_bf[:, ppar, :],
                                    start=True, stop=True), waits=w if k == 0 else (), inc=1)
                            P.guard[("pb", 4)] = {}
                            P.guard[("pb", 5)] = {}
                            P.guard[("pb", 6)] = {}
                            P.reads(("a_bf", ppar), msh)
                            cs = sb.consts
                            mg1 = P.emit("vector", lambda v: v.tensor_scalar(
                                sp_c[:, :], pb[4][:, :], cs[:, C_TAU1:C_TAU1 + 1], None,
                                op0=ALU.is_ge), waits=[msh], inc=1)
                            P.reads(("pb", 4), mg1)
                            mg2 = P.emit("vector", lambda v: v.scalar_tensor_tensor(
                                sp_c[:, :], pb[5][:, :], cs[:, C_TAU2:C_TAU2 + 1],
                                sp_c[:, :], op0=ALU.is_ge, op1=ALU.add), inc=1)
                            P.reads(("pb", 5), mg2)
                            mg3 = P.emit("vector", lambda v: v.scalar_tensor_tensor(
                                sp_c[:, :], pb[6][:, :], cs[:, C_TAU3:C_TAU3 + 1],
                                sp_c[:, :], op0=ALU.is_ge, op1=ALU.add), inc=1)
                            P.reads(("pb", 6), mg3)
                            m_hsp_last = P.emit("vector", lambda v, ppar=ppar, pic=pic: v.scalar_tensor_tensor(
                                h_sp[:, pic, :], sp_c[:, :], cs[:, C_15:C_15 + 1],
                                h_f32[:, ppar, :], op0=ALU.is_le, op1=ALU.mult),
                                waits=P.war("h_sp") if pic == 0 else (), inc=1)
                            if pic == 0:
                                P.guard["h_sp"] = {}
                            P.reads(("h_f32", ppar), m_hsp_last)
                        if ic < NIC:
                            pend = this

                    loader_b = (stream_chunk(src_fm, cols) if src_fm is not None else None)
                    for dout in range(NDC):
                        ybk = 0 + dout % 2
                        md = None
                        for ibt in range(4):
                            bpar = (dout * 4 + ibt) % 2
                            mwd_l = P.dma("sync", lambda e, dout=dout, ibt=ibt, bpar=bpar: e.dma_start(
                                wsd[:, bpar], wd_c[dout, ibt]),
                                waits=[m_wd[dout][ibt]] + P.war(("wsd", bpar)))
                            P.guard[("wsd", bpar)] = {}
                            w = [mwd_l]
                            if ibt == 0:
                                w += [m_hsp_last] + P.war(("pb", ybk))
                            for ibl in range(16):
                                ic = ibt * 16 + ibl
                                md = P.emit("tensor", lambda t, bpar=bpar, ibl=ibl, ic=ic, ybk=ybk: t.matmul(
                                    pb[ybk][:, :], wsd[:, bpar, ibl, :], h_sp[:, ic, :],
                                    start=(ic == 0), stop=(ic == NIC - 1)),
                                    waits=w if ibl == 0 else (), inc=1)
                            if ibt == 0:
                                P.guard[("pb", ybk)] = {}
                            P.reads(("wsd", bpar), md)
                            P.reads("h_sp", md)
                        opar = dout % 2
                        if dst_fm is not None:
                            mev = P.emit("vector", lambda v, opar=opar, dout=dout, ybk=ybk: v.scalar_tensor_tensor(
                                sb.xout[:, opar, :], pb[ybk][:, :], sb.consts[:, C_HALF:C_HALF + 1],
                                sb.x_fm[:, dout, :], op0=ALU.mult, op1=ALU.add),
                                waits=[md] + P.war(("xout", opar)), inc=1)
                            P.guard[("xout", opar)] = {}
                            P.reads(("pb", ybk), mev)
                            P.reads("x_fm", mev)
                            mst = P.dma("gpsimd", lambda g, opar=opar, dout=dout, cols=cols: g.dma_start(
                                dst_fm[dout, :, cols], sb.xout[:, opar, :]),
                                waits=[mev])
                            P.reads(("xout", opar), mst)
                        else:
                            xap, xw, xkey = loader_b(dout)
                            mev = P.emit("vector", lambda v, dout=dout, ybk=ybk, xap=xap: v.scalar_tensor_tensor(
                                sb.x_fm[:, dout, :], pb[ybk][:, :], sb.consts[:, C_HALF:C_HALF + 1],
                                xap, op0=ALU.mult, op1=ALU.add),
                                waits=[md] + list(xw) + (P.war("x_fm") if dout == 0 else []), inc=1)
                            if dout == 0:
                                P.guard["x_fm"] = {}
                            P.reads(("pb", ybk), mev)
                            P.reads(xkey, mev)
                            P.reads("x_fm", mev)
                            m_evac_last = mev

                    if dst_tm is not None:
                        for tt in range(NTT):
                            par = tt % 2
                            row0 = st * TSUP + tt * 128
                            for grp in range(4):
                                bk = 6 + grp % 2
                                w = [m_evac_last] + P.war(("pb", bk)) if grp < 2 else P.war(("pb", bk))
                                for k in range(4):
                                    dc = grp * 4 + k
                                    mt = P.emit("tensor", lambda t, dc=dc, bk=bk, k=k, tt=tt: t.transpose(
                                        pb[bk][:, k * 128:(k + 1) * 128],
                                        sb.x_fm[:, dc, tt * 128:(tt + 1) * 128], sb.ident[:, :]),
                                        waits=w if k == 0 else (), inc=1)
                                P.reads("x_fm", mt)
                                P.guard[("pb", bk)] = {}
                                dst_ap = xtm[:, par, grp * 512:(grp + 1) * 512]
                                mcp = P.emit("vector", lambda v, bk=bk, dst_ap=dst_ap: v.tensor_copy(
                                    dst_ap, pb[bk][:, :]),
                                    waits=[mt] + (P.war(("xtm", par)) if grp == 0 else []), inc=1)
                                if grp == 0:
                                    P.guard[("xtm", par)] = {}
                                P.reads(("pb", bk), mcp)
                            mst = P.dma("gpsimd", lambda g, par=par, row0=row0: g.dma_start(
                                dst_tm[row0:row0 + 128, :], xtm[:, par, :]),
                                waits=[mcp])
                            P.reads(("xtm", par), mst)
                P.fence()

        def emit_attn(src_fm, dst_fm):
            esf = ExitStack()
            with esf:
                xlnb = esf.enter_context(nc.sbuf_tensor("xlnb", [128, NDC, 512], BF16))
                kr = esf.enter_context(nc.sbuf_tensor("kr", [128, NKV, S], BF16))
                v_tm = esf.enter_context(nc.sbuf_tensor("v_tm", [128, NDC, 512], BF16))
                ct_sb = esf.enter_context(nc.sbuf_tensor("ct_sb", [128, 512], F32))
                st_sb = esf.enter_context(nc.sbuf_tensor("st_sb", [128, 512], F32))
                q_qg = esf.enter_context(nc.sbuf_tensor("q_qg", [128, NH, QG], BF16))
                pt = esf.enter_context(nc.sbuf_tensor("pt", [128, NDC, QG], BF16))
                o_sb = esf.enter_context(nc.sbuf_tensor("o_sb", [128, NH, QG], BF16))
                wsq = esf.enter_context(nc.sbuf_tensor("wsq", [128, 2, NDC, 128], BF16))
                wv_sb = esf.enter_context(nc.sbuf_tensor("wv_sb", [128, NDC, 512], BF16))
                q_f32 = esf.enter_context(nc.sbuf_tensor("q_f32", [128, 2, 512], F32))
                t_rope = esf.enter_context(nc.sbuf_tensor("t_rope", [128, 2, 512], F32))
                rbc = esf.enter_context(nc.sbuf_tensor("rbc", [128, 512], F32))
                rln = esf.enter_context(nc.sbuf_tensor("rln", [128, 512], F32))
                cs = sb.consts

                mwv_l = P.dma("sync", lambda e: e.dma_start(wv_sb[:, :, :], wv_c[:, :, :]),
                               waits=[m_wv])

                def load_tabs(ccols):
                    m1 = P.dma("sync", lambda e, ccols=ccols: e.dma_start(
                        ct_sb[:, :], ctab_in[:, ccols]), waits=P.war("tabs"))
                    m2 = P.dma("sync", lambda e, ccols=ccols: e.dma_start(
                        st_sb[:, :], stab_in[:, ccols]), waits=P.war("tabs"))
                    P.guard["tabs"] = {}
                    return [m1, m2]

                def rope(src_bank, par2, ccols, dst_ap, scaled, extra_war_key, ready, mtab):
                    mev = P.emit("vector", lambda v, par2=par2, src_bank=src_bank: v.tensor_copy(
                        q_f32[:, par2, :], pb[src_bank][:, :]),
                        waits=[ready] + P.war(("q_f32", par2)), inc=1)
                    P.guard[("q_f32", par2)] = {}
                    P.reads(("pb", src_bank), mev)
                    msw = P.emit("tensor", lambda t, par2=par2: t.matmul(
                        pb[2 + par2][:, :], sb.swapm[:, :], q_f32[:, par2, :],
                        start=True, stop=True),
                        waits=[mev] + P.war(("pb", 2 + par2)), inc=1)
                    P.guard[("pb", 2 + par2)] = {}
                    P.reads(("q_f32", par2), msw)
                    if scaled:
                        mr1 = P.emit("vector", lambda v, par2=par2, ccols=ccols: v.scalar_tensor_tensor(
                            t_rope[:, par2, :], q_f32[:, par2, :], cs[:, C_ISQHD:C_ISQHD + 1],
                            ct_sb[:, :], op0=ALU.mult, op1=ALU.mult),
                            waits=mtab + [msw] + P.war(("t_rope", par2)), inc=1)
                        mr2 = P.emit("vector", lambda v, par2=par2, ccols=ccols: v.scalar_tensor_tensor(
                            q_f32[:, par2, :], pb[2 + par2][:, :], cs[:, C_ISQHD:C_ISQHD + 1],
                            st_sb[:, :], op0=ALU.mult, op1=ALU.mult), inc=1)
                    else:
                        mr1 = P.emit("vector", lambda v, par2=par2: v.tensor_tensor(
                            t_rope[:, par2, :], q_f32[:, par2, :], ct_sb[:, :], op=ALU.mult),
                            waits=mtab + [msw] + P.war(("t_rope", par2)), inc=1)
                        mr2 = P.emit("vector", lambda v, par2=par2: v.tensor_tensor(
                            q_f32[:, par2, :], pb[2 + par2][:, :], st_sb[:, :], op=ALU.mult),
                            inc=1)
                    P.guard[("t_rope", par2)] = {}
                    P.guard[("q_f32", par2)] = {}
                    P.reads("tabs", mr1)
                    P.reads("tabs", mr2)
                    P.reads(("pb", 2 + par2), mr2)
                    mr3 = P.emit("vector", lambda v, par2=par2, dst_ap=dst_ap: v.tensor_tensor(
                        dst_ap, t_rope[:, par2, :], q_f32[:, par2, :], op=ALU.add),
                        waits=P.war(extra_war_key), inc=1)
                    P.guard[extra_war_key] = {}
                    P.reads(("t_rope", par2), mr3)
                    P.reads(("q_f32", par2), mr3)
                    return mr3

                m_kr = None
                m_v = None
                m_xst = [None] * NSUP
                for st in range(NSUP):
                    cols = slice(st * TSUP, (st + 1) * TSUP)
                    mln = emit_ln(P, pb, sb, stream_chunk(src_fm, cols), xlnb, "xlnf")[NDC - 1]
                    m_xst[st] = P.dma("gpsimd", lambda g, cols=cols: g.dma_start(
                        xlnf_d[:, :, cols].rearrange("c p n -> p c n"), xlnb[:, :, :]),
                        waits=[mln])
                    P.reads("xlnf", m_xst[st])
                    mtab_st = load_tabs(cols)
                    for kv in range(NKV):
                        wpar = kv % 2
                        mwl = P.dma("sync", lambda e, kv=kv, wpar=wpar: e.dma_start(
                            wsq[:, wpar], wk_c[kv]),
                            waits=[m_wk[kv]] + P.war(("wsq", wpar)))
                        P.guard[("wsq", wpar)] = {}
                        bk = kv % 2
                        w = [mwl, mln] + P.war(("pb", bk))
                        for dc in range(NDC):
                            mk = P.emit("tensor", lambda t, wpar=wpar, dc=dc, bk=bk: t.matmul(
                                pb[bk][:, :], wsq[:, wpar, dc, :], xlnb[:, dc, :],
                                start=(dc == 0), stop=(dc == NDC - 1)),
                                waits=w if dc == 0 else (), inc=1)
                        P.guard[("pb", bk)] = {}
                        P.reads(("wsq", wpar), mk)
                        P.reads("xlnf", mk)
                        m_kr = rope(bk, kv % 2, cols, kr[:, kv, cols], False, "kr", mk, mtab_st)
                    for ttl in range(NTT):
                        tc = st * NTT + ttl
                        vbk = 4 + ttl % 2
                        w = [mwv_l, mln] + P.war(("pb", vbk))
                        for dc in range(NDC):
                            mv = P.emit("tensor", lambda t, ttl=ttl, dc=dc, vbk=vbk: t.matmul(
                                pb[vbk][:, :], xlnb[:, dc, ttl * 128:(ttl + 1) * 128],
                                wv_sb[:, dc, :],
                                start=(dc == 0), stop=(dc == NDC - 1)),
                                waits=w if dc == 0 else (), inc=1)
                        P.guard[("pb", vbk)] = {}
                        P.reads("xlnf", mv)
                        m_v = P.emit("vector", lambda v, tc=tc, vbk=vbk: v.tensor_copy(
                            v_tm[:, tc, :], pb[vbk][:, :]),
                            waits=[mv] + (P.war("v_tm") if tc == 0 else []), inc=1)
                        if tc == 0:
                            P.guard["v_tm"] = {}
                        P.reads(("pb", vbk), m_v)

                for qg in range(NQG):
                    ccols = slice(qg * QG, (qg + 1) * QG)
                    mxq = P.dma("sync", lambda e, ccols=ccols: e.dma_start(
                        xlnb[:, :, :], xlnf_d[:, :, ccols].rearrange("c p n -> p c n")),
                        waits=[m_xst[qg]] + P.war("xlnf"))
                    P.guard["xlnf"] = {}
                    m_qr = None
                    mtab_qg = load_tabs(ccols)
                    for h in range(NH):
                        wpar = h % 2
                        mwl = P.dma("sync", lambda e, h=h, wpar=wpar: e.dma_start(
                            wsq[:, wpar], wq_c[h]),
                            waits=[m_wq[h]] + P.war(("wsq", wpar)))
                        P.guard[("wsq", wpar)] = {}
                        bk = h % 2
                        w = [mwl, mxq] + P.war(("pb", bk))
                        for dc in range(NDC):
                            mq = P.emit("tensor", lambda t, wpar=wpar, dc=dc, bk=bk: t.matmul(
                                pb[bk][:, :], wsq[:, wpar, dc, :], xlnb[:, dc, :],
                                start=(dc == 0), stop=(dc == NDC - 1)),
                                waits=w if dc == 0 else (), inc=1)
                        P.guard[("pb", bk)] = {}
                        P.reads(("wsq", wpar), mq)
                        P.reads("xlnf", mq)
                        m_qr = rope(bk, h % 2, ccols, q_qg[:, h, :], True, ("q_qg", h), mq, mtab_qg)

                    for h in range(NH):
                        kvh = h % NKV
                        obk = 5 + h % 2
                        m_exp_pr = [None] * 8
                        msum = None
                        mo = None
                        pend_pr = None

                        def emit_pair_consumers(pr):
                            nonlocal msum, mo
                            kc0 = 2 * pr
                            w = [m_exp_pr[pr]]
                            if pr == 0:
                                w += [P.m_ones] + P.war(("pb", 4))
                            for kc in (kc0, kc0 + 1):
                                msum_ = P.emit("tensor", lambda t, kc=kc: t.matmul(
                                    pb[4][:, :], sb.ones_mat[:, :], pt[:, kc, :],
                                    start=(kc == 0), stop=(kc == NDC - 1)),
                                    waits=w if kc == kc0 else (), inc=1)
                            if pr == 0:
                                P.guard[("pb", 4)] = {}
                            w2 = []
                            if pr == 0:
                                w2 = [m_v] + P.war(("pb", obk))
                            for kc in (kc0, kc0 + 1):
                                mo_ = P.emit("tensor", lambda t, kc=kc, kvh=kvh, obk=obk: t.matmul(
                                    pb[obk][:, :], v_tm[:, kc, kvh * 128:(kvh + 1) * 128], pt[:, kc, :],
                                    start=(kc == 0), stop=(kc == NDC - 1)),
                                    waits=w2 if kc == kc0 else (), inc=1)
                            if pr == 0:
                                P.guard[("pb", obk)] = {}
                            P.reads("pt", mo_)
                            msum, mo = msum_, mo_

                        for pr in range(8):
                            pbase = (pr % 2) * 2
                            w = [m_qr, m_kr] if pr == 0 else []
                            w = w + P.war(("pb", pbase)) + P.war(("pb", pbase + 1))
                            for j in (0, 1):
                                kc = 2 * pr + j
                                msmm = P.emit("tensor", lambda t, kvh=kvh, kc=kc, bk=pbase + j, h=h: t.matmul(
                                    pb[bk][:, :], kr[:, kvh, kc * 128:(kc + 1) * 128], q_qg[:, h, :],
                                    start=True, stop=True), waits=w if j == 0 else (), inc=1)
                            P.guard[("pb", pbase)] = {}
                            P.guard[("pb", pbase + 1)] = {}
                            P.reads(("q_qg", h), msmm)
                            P.reads("kr", msmm)
                            w2 = [msmm]
                            if pr == 0:
                                w2 += P.war("pt")
                                P.guard["pt"] = {}
                            m_exp_pr[pr] = P.emit("scalar", lambda s, pr=pr, pbase=pbase: s.activation(
                                pt[:, 2 * pr:2 * pr + 2, :], pbig[:, pbase:pbase + 2, :],
                                AF.Exp, bias=np.float32(-EXP_SHIFT)),
                                waits=w2, inc=1)
                            P.reads(("pb", pbase), m_exp_pr[pr])
                            P.reads(("pb", pbase + 1), m_exp_pr[pr])
                            if pend_pr is not None:
                                emit_pair_consumers(pend_pr)
                            pend_pr = pr
                        emit_pair_consumers(pend_pr)
                        if debug_streams and qg == 0 and h == 0:
                            mdump = P.dma("gpsimd", lambda g: g.dma_start(
                                pt_dbg[:, :, :], pt[:, :, :]), waits=[mo])
                            P.reads("pt", mdump)
                        ms1 = P.emit("scalar", lambda s: s.activation(
                            rln[:, :], pb[4][:, :], AF.Ln),
                            waits=[msum] + P.war("rln"), inc=1)
                        P.guard["rln"] = {}
                        P.reads(("pb", 4), ms1)
                        ms2 = P.emit("scalar", lambda s: s.activation(
                            rbc[:, :], rln[:, :], AF.Exp, scale=np.float32(-1.0)),
                            waits=P.war("rbc"), inc=1)
                        P.guard["rbc"] = {}
                        P.reads("rln", ms2)
                        moev = P.emit("vector", lambda v, h=h, obk=obk: v.tensor_tensor(
                            o_sb[:, h, :], pb[obk][:, :], rbc[:, :], op=ALU.mult),
                            waits=[mo, ms2] + (P.war("o_sb") if h == 0 else []), inc=1)
                        if h == 0:
                            P.guard["o_sb"] = {}
                        P.reads(("pb", obk), moev)
                        P.reads("rbc", moev)
                        m_oev = moev

                    if debug_streams and qg == 0:
                        P.dma("gpsimd", lambda g: g.dma_start(o_dbg[:, :, :], o_sb[:, :, :]),
                              waits=[m_oev])
                        P.dma("gpsimd", lambda g: g.dma_start(q_dbg[:, :, :], q_qg[:, :, :]),
                              waits=[m_qr])
                        P.dma("gpsimd", lambda g: g.dma_start(k_dbg[:, :, :], kr[:, :, :]),
                              waits=[m_kr])
                    mxl = P.dma("sync", lambda e, ccols=ccols: e.dma_start(
                        sb.x_fm[:, :, :], src_fm[:, :, ccols].rearrange("c p n -> p c n")),
                        waits=P.war("x_fm"))
                    P.guard["x_fm"] = {}
                    for dout in range(NDC):
                        wpar = dout % 2
                        mwl = P.dma("sync", lambda e, dout=dout, wpar=wpar: e.dma_start(
                            wsq[:, wpar], wo_c[dout]),
                            waits=[m_wo[dout]] + P.war(("wsq", wpar)))
                        P.guard[("wsq", wpar)] = {}
                        ybk = 2 + dout % 2
                        w = [mwl, m_oev] + P.war(("pb", ybk))
                        my = None
                        for dinc in range(NDC):
                            my = P.emit("tensor", lambda t, dout=dout, dinc=dinc, wpar=wpar, ybk=ybk: t.matmul(
                                pb[ybk][:, :], wsq[:, wpar, dinc, :], o_sb[:, dinc, :],
                                start=(dinc == 0), stop=(dinc == NDC - 1)),
                                waits=w if dinc == 0 else (), inc=1)
                        P.guard[("pb", ybk)] = {}
                        P.reads(("wsq", wpar), my)
                        P.reads("o_sb", my)
                        opar = dout % 2
                        mev = P.emit("vector", lambda v, opar=opar, dout=dout, ybk=ybk: v.tensor_tensor(
                            sb.xout[:, opar, :], pb[ybk][:, :], sb.x_fm[:, dout, :], op=ALU.add),
                            waits=[my, mxl] + P.war(("xout", opar)), inc=1)
                        P.guard[("xout", opar)] = {}
                        P.reads(("pb", ybk), mev)
                        P.reads("x_fm", mev)
                        mst = P.dma("gpsimd", lambda g, opar=opar, dout=dout, ccols=ccols: g.dma_start(
                            dst_fm[dout, :, ccols], sb.xout[:, opar, :]),
                            waits=[mev])
                        P.reads(("xout", opar), mst)
                P.fence()

        emit_ffn(src_tm=x_in, src_fm=None, dst_fm=s1, dst_tm=None, tag="f1")
        emit_attn(src_fm=s1, dst_fm=s2)
        emit_ffn(src_tm=None, src_fm=s2, dst_fm=None, dst_tm=out, tag="f2")

        P.emit("gpsimd", None, waits=[(f"gpsimd_d{i}", P.counts[f"gpsimd_d{i}"])
                                      for i in range(P.NDMASEM)])

        block = es.enter_context(nc.Block())

        @block.sync
        def _(e):
            for f in P.steps["sync"]:
                f(e)

        @block.gpsimd
        def _(e):
            for f in P.steps["gpsimd"]:
                f(e)

        @block.tensor
        def _(e):
            for f in P.steps["tensor"]:
                f(e)

        @block.vector
        def _(e):
            for f in P.steps["vector"]:
                f(e)

        @block.scalar
        def _(e):
            for f in P.steps["scalar"]:
                f(e)

    return nc


def host_constants():
    TINY = np.float32(2.0 ** -126)
    consts = np.zeros((128, 16), dtype=np.float32)
    p = np.arange(128)
    for k in (1, 2, 3):
        consts[:, k - 1] = np.where((p % 4) + k < 4, TINY, 0.0)
    consts[:, C_15] = 1.5
    consts[:, C_INV2048] = 1.0 / 2048.0
    consts[:, C_EPS] = 1e-5
    consts[:, C_NEG1] = -1.0
    consts[:, C_HALF] = 0.5
    consts[:, C_ISQHD] = 1.0 / np.sqrt(np.float32(HD))
    consts[:, C_ONE] = 1.0
    ident = np.eye(128, dtype=np.float32)
    sd = np.zeros((3, 128, 128), dtype=np.float32)
    for k in (1, 2, 3):
        for c in range(128):
            r = 4 * (c // 4) + (c + k) % 4
            sd[k - 1, r, c] += 1.0
            sd[k - 1, c, c] -= 1.0
    swapm = np.zeros((128, 128), dtype=np.float32)
    for c in range(128):
        r = c + 1 if c % 2 == 0 else c - 1
        swapm[r, c] = 1.0
    half = HD // 2
    inv = 1.0 / 10000.0 ** (np.arange(half, dtype=np.float64) / half)
    pos = np.arange(S, dtype=np.float64)
    ang = pos[None, :] * inv[:, None]
    ctab = np.repeat(np.cos(ang), 2, axis=0).astype(np.float32)
    sin2 = np.repeat(np.sin(ang), 2, axis=0)
    sign = np.where(np.arange(128) % 2 == 0, -1.0, 1.0)
    stab = (sin2 * sign[:, None]).astype(np.float32)
    return consts, ident, sd, swapm, ctab, stab


def kernel(L0_attn_x, L0_attn_W_q, L0_attn_W_k, L0_attn_W_v, L0_attn_W_o,
           L0_ffn_w_gate, L0_ffn_w_up, L0_ffn_w_down):
    x = np.ascontiguousarray(np.asarray(L0_attn_x, dtype=np.float32))
    consts, ident, sd, swapm, ctab, stab = host_constants()
    common = {
        "wq": np.ascontiguousarray(np.asarray(L0_attn_W_q, dtype=np.float32)),
        "wk": np.ascontiguousarray(np.asarray(L0_attn_W_k, dtype=np.float32)),
        "wv": np.ascontiguousarray(np.asarray(L0_attn_W_v, dtype=np.float32)),
        "wo": np.ascontiguousarray(np.asarray(L0_attn_W_o, dtype=np.float32)),
        "wg": np.ascontiguousarray(np.asarray(L0_ffn_w_gate, dtype=np.float32)),
        "wu": np.ascontiguousarray(np.asarray(L0_ffn_w_up, dtype=np.float32)),
        "wd": np.ascontiguousarray(np.asarray(L0_ffn_w_down, dtype=np.float32)),
        "consts": consts, "ident": ident, "sd": sd, "swapm": swapm,
        "ctab": ctab, "stab": stab,
    }
    nc = build()
    in_maps = [dict(common, x=np.ascontiguousarray(x[i])) for i in range(8)]
    res = run_bass_kernel_spmd(nc, in_maps, core_ids=list(range(8)))
    return np.stack([res.results[i]["out"] for i in range(8)], axis=0)


# version 14
# speedup vs baseline: 1.1809x; 1.1036x over previous
import numpy as np
import concourse.bass as bass
import concourse.mybir as mybir
from concourse.bass_utils import run_bass_kernel_spmd
from contextlib import ExitStack

F32 = mybir.dt.float32
BF16 = mybir.dt.bfloat16
AF = mybir.ActivationFunctionType
ALU = mybir.AluOpType

D = 2048
S = 2048
II = 8192
NH, NKV, HD = 16, 4, 128
NDC = D // 128
NIC = II // 128
TSUP = 512
NSUP = S // TSUP
QG = 512
NQG = S // QG
NTT = TSUP // 128

C_TAU1, C_TAU2, C_TAU3, C_15, C_INV2048, C_EPS, C_NEG1, C_HALF, C_ISQHD, C_ONE = range(10)
EXP_SHIFT = 10.0

ENGS = ("sync", "gpsimd", "tensor", "vector", "scalar")


class Prog:

    NDMASEM = 8

    def __init__(self, nc, es):
        self.nc = nc
        self.steps = {e: [] for e in ENGS}
        self.sems = {e: es.enter_context(nc.semaphore("sem_" + e)) for e in ENGS}
        for q in ("sync", "gpsimd", "scalar"):
            for i in range(self.NDMASEM):
                n = f"{q}_d{i}"
                self.sems[n] = es.enter_context(nc.semaphore("sem_" + n))
        self.counts = {e: 0 for e in self.sems}
        self.rr = {"sync": 0, "gpsimd": 0, "scalar": 0}
        self.guard = {}

    def emit(self, eng, fn, waits=(), inc=0):
        sems = self.sems
        wl = []
        for w in waits:
            if w is None:
                continue
            en, val = w
            if val > 0:
                wl.append((en, val))
        wl = tuple(wl)

        def step(e, fn=fn, wl=wl, inc=inc, sem=self.sems[eng]):
            for (en, val) in wl:
                e.wait_ge(sems[en], val)
            ins = fn(e) if fn is not None else None
            if inc:
                ins.then_inc(sem, inc)

        self.steps[eng].append(step)
        if inc:
            self.counts[eng] += inc
        return (eng, self.counts[eng])

    def dma(self, queue, fn, waits=()):
        i = self.rr[queue] % self.NDMASEM
        self.rr[queue] += 1
        sname = f"{queue}_d{i}"
        w = list(waits)
        if self.counts[sname] > 0:
            w.append((sname, self.counts[sname]))
        sems = self.sems
        wl = []
        for x in w:
            if x is None:
                continue
            en, val = x
            if val > 0:
                wl.append((en, val))
        wl = tuple(wl)

        def step(e, fn=fn, wl=wl, sem=self.sems[sname]):
            for (en, val) in wl:
                e.wait_ge(sems[en], val)
            fn(e).then_inc(sem, 16)

        self.steps[queue].append(step)
        self.counts[sname] += 16
        return (sname, self.counts[sname])

    def reads(self, key, milestone):
        d = self.guard.setdefault(key, {})
        d[milestone[0]] = max(d.get(milestone[0], 0), milestone[1])

    def war(self, key):
        d = self.guard.get(key, {})
        return [(e, v) for e, v in d.items()]

    def fence(self):
        snap = dict(self.counts)
        for e in ENGS:
            self.emit(e, None, waits=[(o, v) for o, v in snap.items() if o != e])


class SB:
    pass


def emit_ln(P, pb, sb, load_chunk, xln_dst, xln_key):
    for c in range(NDC):
        ap, w, key = load_chunk(c, 0)
        emit_ln_stat_chunk(P, pb, sb, c, ap, w, key)
    return emit_ln_finish(P, pb, sb, load_chunk, xln_dst, xln_key)


def emit_ln_stat_chunk(P, pb, sb, c, ap, w, key):
    par = c % 2
    w = list(w) + P.war(("lncast", par))
    mcast = P.emit("scalar", lambda s, ap=ap, par=par: s.activation(
        sb.xb[:, par, :], ap, AF.Copy), waits=w, inc=1)
    msq = P.emit("scalar", lambda s, ap=ap, par=par: s.activation(
        sb.x2b[:, par, :], ap, AF.Square), inc=1)
    P.reads(key, mcast)
    P.reads(key, msq)
    w = [msq]
    if c == 0:
        w += P.war(("pb", 4)) + P.war(("pb", 5)) + [P.m_ones]
    P.emit("tensor", lambda t, c=c, par=par: t.matmul(
        pb[4][:, :], sb.ones_mat[:, :], sb.xb[:, par, :],
        start=(c == 0), stop=(c == NDC - 1)), waits=w, inc=1)
    m_sum = P.emit("tensor", lambda t, c=c, par=par: t.matmul(
        pb[5][:, :], sb.ones_mat[:, :], sb.x2b[:, par, :],
        start=(c == 0), stop=(c == NDC - 1)), inc=1)
    P.guard[("lncast", par)] = {}
    P.reads(("lncast", par), m_sum)
    P.last_stat = m_sum
    return m_sum


def emit_ln_finish(P, pb, sb, load_chunk, xln_dst, xln_key):
    cs = sb.consts
    m1 = P.emit("vector", lambda v: v.tensor_scalar(
        sb.bc_m[:, :], pb[4][:, :], cs[:, C_INV2048:C_INV2048 + 1], None,
        op0=ALU.mult), waits=[P.last_stat] + P.war("bc_m"), inc=1)
    m2 = P.emit("vector", lambda v: v.tensor_scalar(
        sb.t_norm[:, :], pb[5][:, :], cs[:, C_INV2048:C_INV2048 + 1],
        cs[:, C_EPS:C_EPS + 1], op0=ALU.mult, op1=ALU.add), inc=1)
    P.guard[("pb", 4)] = {}
    P.guard[("pb", 5)] = {}
    P.guard["bc_m"] = {}
    P.reads(("pb", 4), m1)
    P.reads(("pb", 5), m2)
    m3 = P.emit("vector", lambda v: v.tensor_tensor(
        sb.bc_b[:, :], sb.bc_m[:, :], sb.bc_m[:, :], op=ALU.mult),
        waits=P.war("bc_b"), inc=1)
    m4 = P.emit("vector", lambda v: v.tensor_tensor(
        sb.bc_b[:, :], sb.t_norm[:, :], sb.bc_b[:, :], op=ALU.subtract), inc=1)
    m5 = P.emit("scalar", lambda s: s.activation(
        sb.bc_ln[:, :], sb.bc_b[:, :], AF.Ln), waits=[m4] + P.war("bc_ln"), inc=1)
    m6 = P.emit("scalar", lambda s: s.activation(
        sb.bc_a[:, :], sb.bc_ln[:, :], AF.Exp, scale=np.float32(-0.5)),
        waits=P.war("bc_a"), inc=1)
    P.guard["bc_ln"] = {}
    P.guard["bc_a"] = {}
    P.reads("bc_ln", m6)
    m7 = P.emit("vector", lambda v: v.scalar_tensor_tensor(
        sb.bc_b[:, :], sb.bc_m[:, :], cs[:, C_NEG1:C_NEG1 + 1],
        sb.bc_a[:, :], op0=ALU.mult, op1=ALU.mult),
        waits=[m6], inc=1)
    P.reads("bc_m", m7)
    P.reads("bc_a", m7)
    mnorm = [None] * NDC
    for c in range(NDC):
        ap, w2, key2 = load_chunk(c, 1)
        w = list(w2) + ([m6] if c == 0 else []) + (P.war(xln_key) if c == 0 else [])
        mn1 = P.emit("vector", lambda v, ap=ap: v.tensor_tensor(
            sb.t_norm[:, :], ap, sb.bc_a[:, :], op=ALU.mult),
            waits=w, inc=1)
        P.reads(key2, mn1)
        P.reads("bc_a", mn1)
        mnorm[c] = P.emit("vector", lambda v, c=c: v.tensor_tensor(
            xln_dst[:, c, :], sb.t_norm[:, :], sb.bc_b[:, :], op=ALU.add), inc=1)
        P.reads("bc_b", mnorm[c])
    P.guard[xln_key] = {}
    return mnorm


def build(debug_streams=False):
    nc = bass.Bass("TRN2", target_bir_lowering=False)

    x_in = nc.dram_tensor("x", [S, D], F32, kind="ExternalInput")
    wq_in = nc.dram_tensor("wq", [D, D], F32, kind="ExternalInput")
    wk_in = nc.dram_tensor("wk", [D, 512], F32, kind="ExternalInput")
    wv_in = nc.dram_tensor("wv", [D, 512], F32, kind="ExternalInput")
    wo_in = nc.dram_tensor("wo", [D, D], F32, kind="ExternalInput")
    wg_in = nc.dram_tensor("wg", [D, II], F32, kind="ExternalInput")
    wu_in = nc.dram_tensor("wu", [D, II], F32, kind="ExternalInput")
    wd_in = nc.dram_tensor("wd", [II, D], F32, kind="ExternalInput")
    consts_in = nc.dram_tensor("consts", [128, 16], F32, kind="ExternalInput")
    ident_in = nc.dram_tensor("ident", [128, 128], F32, kind="ExternalInput")
    sd_in = nc.dram_tensor("sd", [3, 128, 128], F32, kind="ExternalInput")
    swap_in = nc.dram_tensor("swapm", [128, 128], F32, kind="ExternalInput")
    ctab_in = nc.dram_tensor("ctab", [128, S], F32, kind="ExternalInput")
    stab_in = nc.dram_tensor("stab", [128, S], F32, kind="ExternalInput")
    out = nc.dram_tensor("out", [S, D], F32, kind="ExternalOutput")

    skind = "ExternalOutput" if debug_streams else "Internal"
    s1 = nc.dram_tensor("s1", [NDC, 128, S], F32, kind=skind)
    s2 = nc.dram_tensor("s2", [NDC, 128, S], F32, kind=skind)
    if debug_streams:
        pt_dbg = nc.dram_tensor("pt_dbg", [128, NDC, QG], F32, kind="ExternalOutput")
        o_dbg = nc.dram_tensor("o_dbg", [128, NH, QG], F32, kind="ExternalOutput")
        q_dbg = nc.dram_tensor("q_dbg", [128, NH, QG], F32, kind="ExternalOutput")
        k_dbg = nc.dram_tensor("k_dbg", [128, NKV, S], F32, kind="ExternalOutput")
    else:
        pt_dbg = o_dbg = q_dbg = k_dbg = None

    xlnf_d = nc.dram_tensor("xlnf_d", [NDC, 128, S], BF16, kind=skind)
    wk_dbg = nc.dram_tensor("wk_c_dbg", [NKV, 128, NDC, 128], BF16) if False else None
    wg_c = nc.dram_tensor("wg_c", [NIC // 2, 128, 2, NDC, 128], BF16)
    wu_c = nc.dram_tensor("wu_c", [NIC // 2, 128, 2, NDC, 128], BF16)
    wd_c = nc.dram_tensor("wd_c", [NDC, 4, 128, 16, 128], BF16)
    wq_c = nc.dram_tensor("wq_c", [NH, 128, NDC, 128], BF16)
    wk_c = nc.dram_tensor("wk_c", [NKV, 128, NDC, 128], BF16)
    wo_c = nc.dram_tensor("wo_c", [NDC, 128, NDC, 128], BF16)
    wv_c = nc.dram_tensor("wv_c", [128, NDC, 512], BF16)

    es = ExitStack()
    with es:
        sb = SB()
        sb.consts = es.enter_context(nc.sbuf_tensor("consts_sb", [128, 16], F32))
        sb.ones_mat = es.enter_context(nc.sbuf_tensor("ones_mat", [128, 128], BF16))
        sb.ident = es.enter_context(nc.sbuf_tensor("ident_sb", [128, 128], F32))
        sb.sd = es.enter_context(nc.sbuf_tensor("sd_sb", [128, 3, 128], BF16))
        sb.swapm = es.enter_context(nc.sbuf_tensor("swap_sb", [128, 128], F32))
        sb.bc_m = es.enter_context(nc.sbuf_tensor("bc_m", [128, 512], F32))
        sb.bc_ln = es.enter_context(nc.sbuf_tensor("bc_ln", [128, 512], F32))
        sb.xb = es.enter_context(nc.sbuf_tensor("xb_sb", [128, 2, 512], BF16))
        sb.x2b = es.enter_context(nc.sbuf_tensor("x2b_sb", [128, 2, 512], BF16))
        sb.bc_a = es.enter_context(nc.sbuf_tensor("bc_a", [128, 512], F32))
        sb.bc_b = es.enter_context(nc.sbuf_tensor("bc_b", [128, 512], F32))
        sb.t_norm = es.enter_context(nc.sbuf_tensor("t_norm", [128, 512], F32))
        sb.x_fm = es.enter_context(nc.sbuf_tensor("x_fm", [128, NDC, 512], F32))
        sb.xs = es.enter_context(nc.sbuf_tensor("xs", [128, 4, 512], F32))
        sb.xout = es.enter_context(nc.sbuf_tensor("xout", [128, 2, 512], F32))

        pbig = es.enter_context(nc.psum_tensor("pbig", [128, 8, 512], F32))
        pb = [pbig[:, i, :] for i in range(8)]

        P = Prog(nc, es)

        P.dma("gpsimd", lambda g: g.dma_start(sb.consts[:, :], consts_in[:, :]))
        P.dma("gpsimd", lambda g: g.dma_start(sb.ident[:, :], ident_in[:, :]))
        P.dma("gpsimd", lambda g: g.dma_start(
            sb.sd[:, :, :], sd_in.rearrange("k p f -> p k f")))
        m_const = P.dma("gpsimd", lambda g: g.dma_start(sb.swapm[:, :], swap_in[:, :]))
        m_ones = P.emit("vector", lambda v: v.memset(sb.ones_mat[:, :], 1.0),
                        waits=[m_const], inc=1)
        P.m_ones = m_ones

        wg_src = wg_in.rearrange("(dc p) (ib icl f) -> ib p icl dc f", p=128, icl=2, f=128)
        wu_src = wu_in.rearrange("(dc p) (ib icl f) -> ib p icl dc f", p=128, icl=2, f=128)
        m_wg = [None] * (NIC // 2)
        m_wu = [None] * (NIC // 2)
        wd_src = wd_in.rearrange("(ibt ibl p) (dc f) -> dc ibt p ibl f", ibl=16, p=128, f=128)
        m_wd = [[None] * 4 for _ in range(NDC)]
        wq_src = wq_in.rearrange("(dc p) (h f) -> h p dc f", p=128, f=128)
        wk_src = wk_in.rearrange("(dc p) (h f) -> h p dc f", p=128, f=128)
        wo_src = wo_in.rearrange("(dc p) (h f) -> h p dc f", p=128, f=128)
        wv_src = wv_in.rearrange("(dc p) f -> p dc f", p=128)
        for b in range(NIC // 2):
            P.dma("gpsimd", lambda g, b=b: g.dma_start(wg_c[b][:, 0], wg_src[b][:, 0]))
            m_wg[b] = P.dma("gpsimd", lambda g, b=b: g.dma_start(wg_c[b][:, 1], wg_src[b][:, 1]))
            P.dma("gpsimd", lambda g, b=b: g.dma_start(wu_c[b][:, 0], wu_src[b][:, 0]))
            m_wu[b] = P.dma("gpsimd", lambda g, b=b: g.dma_start(wu_c[b][:, 1], wu_src[b][:, 1]))
            for j in (2 * b, 2 * b + 1):
                dc, ibt = j // 4, j % 4
                m_wd[dc][ibt] = P.dma("gpsimd", lambda g, dc=dc, ibt=ibt: g.dma_start(
                    wd_c[dc, ibt], wd_src[dc, ibt]))
        m_wk = [P.dma("gpsimd", lambda g, h=h: g.dma_start(wk_c[h], wk_src[h]))
                for h in range(NKV)]
        m_wv = P.dma("gpsimd", lambda g: g.dma_start(wv_c[:, :, :], wv_src))
        m_wq = [P.dma("gpsimd", lambda g, h=h: g.dma_start(wq_c[h], wq_src[h]))
                for h in range(NH)]
        m_wo = [P.dma("gpsimd", lambda g, h=h: g.dma_start(wo_c[h], wo_src[h]))
                for h in range(NDC)]

        P.emit("vector", None, waits=[m_const])
        P.emit("scalar", None, waits=[m_const])
        P.emit("tensor", None, waits=[m_const])

        xs_rr = [0]

        def stream_chunk(src_fm, cols):
            def load(c, phase=0):
                slot = xs_rr[0] % 4
                xs_rr[0] += 1
                m = P.dma("sync", lambda e, c=c, slot=slot: e.dma_start(
                    sb.xs[:, slot, :], src_fm[c, :, cols]),
                    waits=P.war(("xs", slot)))
                P.guard[("xs", slot)] = {}
                return (sb.xs[:, slot, :], [m], ("xs", slot))
            return load

        def emit_ffn(src_tm, src_fm, dst_fm, dst_tm, tag):
            esf = ExitStack()
            with esf:
                xln = esf.enter_context(nc.sbuf_tensor("xln_f_" + tag, [128, NDC, TSUP], BF16))
                h_sp = esf.enter_context(nc.sbuf_tensor("h_sp_" + tag, [128, NIC, TSUP], BF16))
                xtm = esf.enter_context(nc.sbuf_tensor("xtm_" + tag, [128, 2, D], F32))
                wsg = esf.enter_context(nc.sbuf_tensor("wsg_" + tag, [128, 2, NDC, 128], BF16))
                wsu = esf.enter_context(nc.sbuf_tensor("wsu_" + tag, [128, 2, NDC, 128], BF16))
                wsd = esf.enter_context(nc.sbuf_tensor("wsd_" + tag, [128, 2, 16, 128], BF16))
                g_sb = esf.enter_context(nc.sbuf_tensor("g_sb_" + tag, [128, 2, TSUP], F32))
                h_f32 = esf.enter_context(nc.sbuf_tensor("h_f32_" + tag, [128, 2, TSUP], F32))
                a_bf = esf.enter_context(nc.sbuf_tensor("a_bf_" + tag, [128, 2, TSUP], BF16))
                sp_c = esf.enter_context(nc.sbuf_tensor("sp_c_" + tag, [128, TSUP], BF16))

                for st in range(NSUP):
                    cols = slice(st * TSUP, (st + 1) * TSUP)
                    if src_tm is not None:
                        for tt in range(NTT):
                            par = tt % 2
                            row0 = st * TSUP + tt * 128
                            mld = P.dma("sync", lambda e, par=par, row0=row0: e.dma_start(
                                xtm[:, par, :], src_tm[row0:row0 + 128, :]),
                                waits=P.war(("xtm", par)))
                            P.guard[("xtm", par)] = {}
                            for grp in range(4):
                                bk = 6 + grp % 2
                                w = [mld] + (P.war("x_fm") if (tt, grp) == (0, 0) else []) \
                                    + P.war(("pb", bk))
                                for k in range(4):
                                    dc = grp * 4 + k
                                    mt = P.emit("tensor", lambda t, par=par, dc=dc, bk=bk, k=k: t.transpose(
                                        pb[bk][:, k * 128:(k + 1) * 128],
                                        xtm[:, par, dc * 128:(dc + 1) * 128], sb.ident[:, :]),
                                        waits=w if k == 0 else (), inc=1)
                                P.reads(("xtm", par), mt)
                                P.guard[("pb", bk)] = {}
                                dst_ap = sb.x_fm[:, grp * 4:(grp + 1) * 4, tt * 128:(tt + 1) * 128]
                                mcp = P.emit("vector", lambda v, bk=bk, dst_ap=dst_ap: v.tensor_copy(
                                    dst_ap, pb[bk].rearrange("p (k f) -> p k f", k=4)),
                                    waits=[mt], inc=1)
                                P.reads(("pb", bk), mcp)
                        if (st, 0) == (0, 0):
                            P.guard["x_fm"] = {}
                        m_xready = [mcp]
                        def loader(c, phase=0, m_xready=tuple(m_xready)):
                            return (sb.x_fm[:, c, :],
                                    list(m_xready) if c == 0 else [], "x_fm")
                    else:
                        loader = stream_chunk(src_fm, cols)

                    if src_fm is not None and st > 0:
                        mln = emit_ln_finish(P, pb, sb, stream_chunk(src_fm, cols),
                                             xln, "xln")[NDC - 1]
                    else:
                        mln = emit_ln(P, pb, sb, loader, xln, "xln")[NDC - 1]

                    m_hsp_last = None
                    pend = None
                    for ic in range(NIC + 1):
                        if ic < NIC:
                            par = ic % 2
                            b, icl = ic // 2, ic % 2
                            bpar = ic % 2
                            mwg_l = P.dma("sync", lambda e, b=b, icl=icl, bpar=bpar: e.dma_start(
                                wsg[:, bpar], wg_c[b][:, icl]),
                                waits=[m_wg[b]] + P.war(("wsg", bpar)))
                            mwu_l = P.dma("sync", lambda e, b=b, icl=icl, bpar=bpar: e.dma_start(
                                wsu[:, bpar], wu_c[b][:, icl]),
                                waits=[m_wu[b]] + P.war(("wsu", bpar)))
                            P.guard[("wsg", bpar)] = {}
                            P.guard[("wsu", bpar)] = {}
                            gbk, ubk = 0 + par, 2 + par
                            w = [mwg_l, mln] + P.war(("pb", gbk))
                            for dc in range(NDC):
                                mg = P.emit("tensor", lambda t, icl=icl, bpar=bpar, dc=dc, gbk=gbk: t.matmul(
                                    pb[gbk][:, :], wsg[:, bpar, dc, :], xln[:, dc, :],
                                    start=(dc == 0), stop=(dc == NDC - 1)),
                                    waits=w if dc == 0 else (), inc=1)
                            P.guard[("pb", gbk)] = {}
                            P.reads(("wsg", bpar), mg)
                            P.reads("xln", mg)
                            w = [mwu_l] + P.war(("pb", ubk))
                            for dc in range(NDC):
                                mu = P.emit("tensor", lambda t, icl=icl, bpar=bpar, dc=dc, ubk=ubk: t.matmul(
                                    pb[ubk][:, :], wsu[:, bpar, dc, :], xln[:, dc, :],
                                    start=(dc == 0), stop=(dc == NDC - 1)),
                                    waits=w if dc == 0 else (), inc=1)
                            P.guard[("pb", ubk)] = {}
                            P.reads(("wsu", bpar), mu)
                            P.reads("xln", mu)
                            msl = P.emit("scalar", lambda s, par=par, gbk=gbk: s.activation(
                                g_sb[:, par, :], pb[gbk][:, :], AF.Silu),
                                waits=[mg] + P.war(("g_sb", par)), inc=1)
                            P.guard[("g_sb", par)] = {}
                            P.reads(("pb", gbk), msl)
                            mh = P.emit("vector", lambda v, par=par, ubk=ubk: v.tensor_tensor(
                                h_f32[:, par, :], pb[ubk][:, :], g_sb[:, par, :], op=ALU.mult),
                                waits=[msl, mu] + P.war(("h_f32", par)), inc=1)
                            P.guard[("h_f32", par)] = {}
                            P.reads(("pb", ubk), mh)
                            P.reads(("g_sb", par), mh)
                            mab = P.emit("scalar", lambda s, par=par: s.activation(
                                a_bf[:, par, :], h_f32[:, par, :], AF.Abs),
                                waits=[mh] + P.war(("a_bf", par)), inc=1)
                            P.guard[("a_bf", par)] = {}
                            P.reads(("h_f32", par), mab)
                            this = (ic, par, mab)
                        if pend is not None:
                            pic, ppar, pmab = pend
                            w = [pmab] + P.war(("pb", 4)) + P.war(("pb", 5)) + P.war(("pb", 6))
                            for k in range(3):
                                msh = P.emit("tensor", lambda t, k=k, ppar=ppar: t.matmul(
                                    pb[4 + k][:, :], sb.sd[:, k, :], a_bf[:, ppar, :],
                                    start=True, stop=True), waits=w if k == 0 else (), inc=1)
                            P.guard[("pb", 4)] = {}
                            P.guard[("pb", 5)] = {}
                            P.guard[("pb", 6)] = {}
                            P.reads(("a_bf", ppar), msh)
                            cs = sb.consts
                            mg1 = P.emit("vector", lambda v: v.tensor_scalar(
                                sp_c[:, :], pb[4][:, :], cs[:, C_TAU1:C_TAU1 + 1], None,
                                op0=ALU.is_ge), waits=[msh], inc=1)
                            P.reads(("pb", 4), mg1)
                            mg2 = P.emit("vector", lambda v: v.scalar_tensor_tensor(
                                sp_c[:, :], pb[5][:, :], cs[:, C_TAU2:C_TAU2 + 1],
                                sp_c[:, :], op0=ALU.is_ge, op1=ALU.add), inc=1)
                            P.reads(("pb", 5), mg2)
                            mg3 = P.emit("vector", lambda v: v.scalar_tensor_tensor(
                                sp_c[:, :], pb[6][:, :], cs[:, C_TAU3:C_TAU3 + 1],
                                sp_c[:, :], op0=ALU.is_ge, op1=ALU.add), inc=1)
                            P.reads(("pb", 6), mg3)
                            m_hsp_last = P.emit("vector", lambda v, ppar=ppar, pic=pic: v.scalar_tensor_tensor(
                                h_sp[:, pic, :], sp_c[:, :], cs[:, C_15:C_15 + 1],
                                h_f32[:, ppar, :], op0=ALU.is_le, op1=ALU.mult),
                                waits=P.war("h_sp") if pic == 0 else (), inc=1)
                            if pic == 0:
                                P.guard["h_sp"] = {}
                            P.reads(("h_f32", ppar), m_hsp_last)
                        if ic < NIC:
                            pend = this

                    loader_b = (stream_chunk(src_fm, cols) if src_fm is not None else None)
                    pipeline_next = src_fm is not None and st + 1 < NSUP
                    loader_next = (stream_chunk(
                        src_fm, slice((st + 1) * TSUP, (st + 2) * TSUP))
                        if pipeline_next else None)
                    m_evacs = [None] * NDC
                    for dout in range(NDC):
                        ybk = 0 + dout % 2
                        md = None
                        for ibt in range(4):
                            bpar = (dout * 4 + ibt) % 2
                            mwd_l = P.dma("sync", lambda e, dout=dout, ibt=ibt, bpar=bpar: e.dma_start(
                                wsd[:, bpar], wd_c[dout, ibt]),
                                waits=[m_wd[dout][ibt]] + P.war(("wsd", bpar)))
                            P.guard[("wsd", bpar)] = {}
                            w = [mwd_l]
                            if ibt == 0:
                                w += [m_hsp_last] + P.war(("pb", ybk))
                            for ibl in range(16):
                                ic = ibt * 16 + ibl
                                md = P.emit("tensor", lambda t, bpar=bpar, ibl=ibl, ic=ic, ybk=ybk: t.matmul(
                                    pb[ybk][:, :], wsd[:, bpar, ibl, :], h_sp[:, ic, :],
                                    start=(ic == 0), stop=(ic == NIC - 1)),
                                    waits=w if ibl == 0 else (), inc=1)
                            if ibt == 0:
                                P.guard[("pb", ybk)] = {}
                            P.reads(("wsd", bpar), md)
                            P.reads("h_sp", md)
                        opar = dout % 2
                        if dst_fm is not None:
                            mev = P.emit("vector", lambda v, opar=opar, dout=dout, ybk=ybk: v.scalar_tensor_tensor(
                                sb.xout[:, opar, :], pb[ybk][:, :], sb.consts[:, C_HALF:C_HALF + 1],
                                sb.x_fm[:, dout, :], op0=ALU.mult, op1=ALU.add),
                                waits=[md] + P.war(("xout", opar)), inc=1)
                            P.guard[("xout", opar)] = {}
                            P.reads(("pb", ybk), mev)
                            P.reads("x_fm", mev)
                            mst = P.dma("scalar", lambda g, opar=opar, dout=dout, cols=cols: g.dma_start(
                                dst_fm[dout, :, cols], sb.xout[:, opar, :]),
                                waits=[mev])
                            P.reads(("xout", opar), mst)
                        else:
                            xap, xw, xkey = loader_b(dout)
                            mev = P.emit("vector", lambda v, dout=dout, ybk=ybk, xap=xap: v.scalar_tensor_tensor(
                                sb.x_fm[:, dout, :], pb[ybk][:, :], sb.consts[:, C_HALF:C_HALF + 1],
                                xap, op0=ALU.mult, op1=ALU.add),
                                waits=[md] + list(xw) + (P.war("x_fm") if dout == 0 else []), inc=1)
                            if dout == 0:
                                P.guard["x_fm"] = {}
                            P.reads(("pb", ybk), mev)
                            P.reads(xkey, mev)
                            P.reads("x_fm", mev)
                            m_evacs[dout] = mev
                        if pipeline_next:
                            ap3, w3, key3 = loader_next(dout)
                            emit_ln_stat_chunk(P, pb, sb, dout, ap3, w3, key3)

                    if dst_tm is not None:
                        for tt in range(NTT):
                            par = tt % 2
                            row0 = st * TSUP + tt * 128
                            for grp in range(4):
                                bk = 6 + grp % 2
                                w = ([m_evacs[grp * 4 + 3]] if tt == 0 else []) + P.war(("pb", bk))
                                for k in range(4):
                                    dc = grp * 4 + k
                                    mt = P.emit("tensor", lambda t, dc=dc, bk=bk, k=k, tt=tt: t.transpose(
                                        pb[bk][:, k * 128:(k + 1) * 128],
                                        sb.x_fm[:, dc, tt * 128:(tt + 1) * 128], sb.ident[:, :]),
                                        waits=w if k == 0 else (), inc=1)
                                P.reads("x_fm", mt)
                                P.guard[("pb", bk)] = {}
                                dst_ap = xtm[:, par, grp * 512:(grp + 1) * 512]
                                mcp = P.emit("vector", lambda v, bk=bk, dst_ap=dst_ap: v.tensor_copy(
                                    dst_ap, pb[bk][:, :]),
                                    waits=[mt] + (P.war(("xtm", par)) if grp == 0 else []), inc=1)
                                if grp == 0:
                                    P.guard[("xtm", par)] = {}
                                P.reads(("pb", bk), mcp)
                            mst = P.dma("scalar", lambda g, par=par, row0=row0: g.dma_start(
                                dst_tm[row0:row0 + 128, :], xtm[:, par, :]),
                                waits=[mcp])
                            P.reads(("xtm", par), mst)
                P.fence()

        def emit_attn(src_fm, dst_fm):
            esf = ExitStack()
            with esf:
                xlnb = esf.enter_context(nc.sbuf_tensor("xlnb", [128, NDC, 512], BF16))
                kr = esf.enter_context(nc.sbuf_tensor("kr", [128, NKV, S], BF16))
                v_tm = esf.enter_context(nc.sbuf_tensor("v_tm", [128, NDC, 512], BF16))
                ct_sb = esf.enter_context(nc.sbuf_tensor("ct_sb", [128, 512], F32))
                st_sb = esf.enter_context(nc.sbuf_tensor("st_sb", [128, 512], F32))
                q_qg = esf.enter_context(nc.sbuf_tensor("q_qg", [128, NH, QG], BF16))
                pt = esf.enter_context(nc.sbuf_tensor("pt", [128, NDC, QG], BF16))
                o_sb = esf.enter_context(nc.sbuf_tensor("o_sb", [128, NH, QG], BF16))
                wsq = esf.enter_context(nc.sbuf_tensor("wsq", [128, 2, NDC, 128], BF16))
                wv_sb = esf.enter_context(nc.sbuf_tensor("wv_sb", [128, NDC, 512], BF16))
                q_f32 = esf.enter_context(nc.sbuf_tensor("q_f32", [128, 2, 512], F32))
                t_rope = esf.enter_context(nc.sbuf_tensor("t_rope", [128, 2, 512], F32))
                rbc = esf.enter_context(nc.sbuf_tensor("rbc", [128, 512], F32))
                rln = esf.enter_context(nc.sbuf_tensor("rln", [128, 512], F32))
                cs = sb.consts

                mwv_l = P.dma("sync", lambda e: e.dma_start(wv_sb[:, :, :], wv_c[:, :, :]),
                               waits=[m_wv])

                def load_tabs(ccols):
                    m1 = P.dma("sync", lambda e, ccols=ccols: e.dma_start(
                        ct_sb[:, :], ctab_in[:, ccols]), waits=P.war("tabs"))
                    m2 = P.dma("sync", lambda e, ccols=ccols: e.dma_start(
                        st_sb[:, :], stab_in[:, ccols]), waits=P.war("tabs"))
                    P.guard["tabs"] = {}
                    return [m1, m2]

                def rope(src_bank, par2, ccols, dst_ap, scaled, extra_war_key, ready, mtab):
                    mev = P.emit("vector", lambda v, par2=par2, src_bank=src_bank: v.tensor_copy(
                        q_f32[:, par2, :], pb[src_bank][:, :]),
                        waits=[ready] + P.war(("q_f32", par2)), inc=1)
                    P.guard[("q_f32", par2)] = {}
                    P.reads(("pb", src_bank), mev)
                    msw = P.emit("tensor", lambda t, par2=par2: t.matmul(
                        pb[2 + par2][:, :], sb.swapm[:, :], q_f32[:, par2, :],
                        start=True, stop=True),
                        waits=[mev] + P.war(("pb", 2 + par2)), inc=1)
                    P.guard[("pb", 2 + par2)] = {}
                    P.reads(("q_f32", par2), msw)
                    if scaled:
                        mr1 = P.emit("vector", lambda v, par2=par2, ccols=ccols: v.scalar_tensor_tensor(
                            t_rope[:, par2, :], q_f32[:, par2, :], cs[:, C_ISQHD:C_ISQHD + 1],
                            ct_sb[:, :], op0=ALU.mult, op1=ALU.mult),
                            waits=mtab + [msw] + P.war(("t_rope", par2)), inc=1)
                        mr2 = P.emit("vector", lambda v, par2=par2, ccols=ccols: v.scalar_tensor_tensor(
                            q_f32[:, par2, :], pb[2 + par2][:, :], cs[:, C_ISQHD:C_ISQHD + 1],
                            st_sb[:, :], op0=ALU.mult, op1=ALU.mult), inc=1)
                    else:
                        mr1 = P.emit("vector", lambda v, par2=par2: v.tensor_tensor(
                            t_rope[:, par2, :], q_f32[:, par2, :], ct_sb[:, :], op=ALU.mult),
                            waits=mtab + [msw] + P.war(("t_rope", par2)), inc=1)
                        mr2 = P.emit("vector", lambda v, par2=par2: v.tensor_tensor(
                            q_f32[:, par2, :], pb[2 + par2][:, :], st_sb[:, :], op=ALU.mult),
                            inc=1)
                    P.guard[("t_rope", par2)] = {}
                    P.guard[("q_f32", par2)] = {}
                    P.reads("tabs", mr1)
                    P.reads("tabs", mr2)
                    P.reads(("pb", 2 + par2), mr2)
                    mr3 = P.emit("vector", lambda v, par2=par2, dst_ap=dst_ap: v.tensor_tensor(
                        dst_ap, t_rope[:, par2, :], q_f32[:, par2, :], op=ALU.add),
                        waits=P.war(extra_war_key), inc=1)
                    P.guard[extra_war_key] = {}
                    P.reads(("t_rope", par2), mr3)
                    P.reads(("q_f32", par2), mr3)
                    return mr3

                m_kr = None
                m_v = None
                m_xst = [None] * NSUP
                for st in range(NSUP):
                    cols = slice(st * TSUP, (st + 1) * TSUP)
                    mln = emit_ln(P, pb, sb, stream_chunk(src_fm, cols), xlnb, "xlnf")[NDC - 1]
                    m_xst[st] = P.dma("scalar", lambda g, cols=cols: g.dma_start(
                        xlnf_d[:, :, cols].rearrange("c p n -> p c n"), xlnb[:, :, :]),
                        waits=[mln])
                    P.reads("xlnf", m_xst[st])
                    mtab_st = load_tabs(cols)
                    for kv in range(NKV):
                        wpar = kv % 2
                        mwl = P.dma("sync", lambda e, kv=kv, wpar=wpar: e.dma_start(
                            wsq[:, wpar], wk_c[kv]),
                            waits=[m_wk[kv]] + P.war(("wsq", wpar)))
                        P.guard[("wsq", wpar)] = {}
                        bk = kv % 2
                        w = [mwl, mln] + P.war(("pb", bk))
                        for dc in range(NDC):
                            mk = P.emit("tensor", lambda t, wpar=wpar, dc=dc, bk=bk: t.matmul(
                                pb[bk][:, :], wsq[:, wpar, dc, :], xlnb[:, dc, :],
                                start=(dc == 0), stop=(dc == NDC - 1)),
                                waits=w if dc == 0 else (), inc=1)
                        P.guard[("pb", bk)] = {}
                        P.reads(("wsq", wpar), mk)
                        P.reads("xlnf", mk)
                        m_kr = rope(bk, kv % 2, cols, kr[:, kv, cols], False, "kr", mk, mtab_st)
                    for ttl in range(NTT):
                        tc = st * NTT + ttl
                        vbk = 4 + ttl % 2
                        w = [mwv_l, mln] + P.war(("pb", vbk))
                        for dc in range(NDC):
                            mv = P.emit("tensor", lambda t, ttl=ttl, dc=dc, vbk=vbk: t.matmul(
                                pb[vbk][:, :], xlnb[:, dc, ttl * 128:(ttl + 1) * 128],
                                wv_sb[:, dc, :],
                                start=(dc == 0), stop=(dc == NDC - 1)),
                                waits=w if dc == 0 else (), inc=1)
                        P.guard[("pb", vbk)] = {}
                        P.reads("xlnf", mv)
                        m_v = P.emit("vector", lambda v, tc=tc, vbk=vbk: v.tensor_copy(
                            v_tm[:, tc, :], pb[vbk][:, :]),
                            waits=[mv] + (P.war("v_tm") if tc == 0 else []), inc=1)
                        if tc == 0:
                            P.guard["v_tm"] = {}
                        P.reads(("pb", vbk), m_v)

                for qg in range(NQG):
                    ccols = slice(qg * QG, (qg + 1) * QG)
                    mxq = P.dma("sync", lambda e, ccols=ccols: e.dma_start(
                        xlnb[:, :, :], xlnf_d[:, :, ccols].rearrange("c p n -> p c n")),
                        waits=[m_xst[qg]] + P.war("xlnf"))
                    P.guard["xlnf"] = {}
                    m_qr = None
                    mtab_qg = load_tabs(ccols)
                    for h in range(NH):
                        wpar = h % 2
                        mwl = P.dma("sync", lambda e, h=h, wpar=wpar: e.dma_start(
                            wsq[:, wpar], wq_c[h]),
                            waits=[m_wq[h]] + P.war(("wsq", wpar)))
                        P.guard[("wsq", wpar)] = {}
                        bk = h % 2
                        w = [mwl, mxq] + P.war(("pb", bk))
                        for dc in range(NDC):
                            mq = P.emit("tensor", lambda t, wpar=wpar, dc=dc, bk=bk: t.matmul(
                                pb[bk][:, :], wsq[:, wpar, dc, :], xlnb[:, dc, :],
                                start=(dc == 0), stop=(dc == NDC - 1)),
                                waits=w if dc == 0 else (), inc=1)
                        P.guard[("pb", bk)] = {}
                        P.reads(("wsq", wpar), mq)
                        P.reads("xlnf", mq)
                        m_qr = rope(bk, h % 2, ccols, q_qg[:, h, :], True, ("q_qg", h), mq, mtab_qg)

                    for h in range(NH):
                        kvh = h % NKV
                        obk = 5 + h % 2
                        m_exp_pr = [None] * 8
                        msum = None
                        mo = None
                        pend_pr = None

                        def emit_pair_consumers(pr):
                            nonlocal msum, mo
                            kc0 = 2 * pr
                            w = [m_exp_pr[pr]]
                            if pr == 0:
                                w += [P.m_ones] + P.war(("pb", 4))
                            for kc in (kc0, kc0 + 1):
                                msum_ = P.emit("tensor", lambda t, kc=kc: t.matmul(
                                    pb[4][:, :], sb.ones_mat[:, :], pt[:, kc, :],
                                    start=(kc == 0), stop=(kc == NDC - 1)),
                                    waits=w if kc == kc0 else (), inc=1)
                            if pr == 0:
                                P.guard[("pb", 4)] = {}
                            w2 = []
                            if pr == 0:
                                w2 = [m_v] + P.war(("pb", obk))
                            for kc in (kc0, kc0 + 1):
                                mo_ = P.emit("tensor", lambda t, kc=kc, kvh=kvh, obk=obk: t.matmul(
                                    pb[obk][:, :], v_tm[:, kc, kvh * 128:(kvh + 1) * 128], pt[:, kc, :],
                                    start=(kc == 0), stop=(kc == NDC - 1)),
                                    waits=w2 if kc == kc0 else (), inc=1)
                            if pr == 0:
                                P.guard[("pb", obk)] = {}
                            P.reads("pt", mo_)
                            msum, mo = msum_, mo_

                        for pr in range(8):
                            pbase = (pr % 2) * 2
                            w = [m_qr, m_kr] if pr == 0 else []
                            w = w + P.war(("pb", pbase)) + P.war(("pb", pbase + 1))
                            for j in (0, 1):
                                kc = 2 * pr + j
                                msmm = P.emit("tensor", lambda t, kvh=kvh, kc=kc, bk=pbase + j, h=h: t.matmul(
                                    pb[bk][:, :], kr[:, kvh, kc * 128:(kc + 1) * 128], q_qg[:, h, :],
                                    start=True, stop=True), waits=w if j == 0 else (), inc=1)
                            P.guard[("pb", pbase)] = {}
                            P.guard[("pb", pbase + 1)] = {}
                            P.reads(("q_qg", h), msmm)
                            P.reads("kr", msmm)
                            w2 = [msmm]
                            if pr == 0:
                                w2 += P.war("pt")
                                P.guard["pt"] = {}
                            m_exp_pr[pr] = P.emit("scalar", lambda s, pr=pr, pbase=pbase: s.activation(
                                pt[:, 2 * pr:2 * pr + 2, :], pbig[:, pbase:pbase + 2, :],
                                AF.Exp, bias=np.float32(-EXP_SHIFT)),
                                waits=w2, inc=1)
                            P.reads(("pb", pbase), m_exp_pr[pr])
                            P.reads(("pb", pbase + 1), m_exp_pr[pr])
                            if pend_pr is not None:
                                emit_pair_consumers(pend_pr)
                            pend_pr = pr
                        emit_pair_consumers(pend_pr)
                        if debug_streams and qg == 0 and h == 0:
                            mdump = P.dma("gpsimd", lambda g: g.dma_start(
                                pt_dbg[:, :, :], pt[:, :, :]), waits=[mo])
                            P.reads("pt", mdump)
                        ms1 = P.emit("scalar", lambda s: s.activation(
                            rln[:, :], pb[4][:, :], AF.Ln),
                            waits=[msum] + P.war("rln"), inc=1)
                        P.guard["rln"] = {}
                        P.reads(("pb", 4), ms1)
                        ms2 = P.emit("scalar", lambda s: s.activation(
                            rbc[:, :], rln[:, :], AF.Exp, scale=np.float32(-1.0)),
                            waits=P.war("rbc"), inc=1)
                        P.guard["rbc"] = {}
                        P.reads("rln", ms2)
                        moev = P.emit("vector", lambda v, h=h, obk=obk: v.tensor_tensor(
                            o_sb[:, h, :], pb[obk][:, :], rbc[:, :], op=ALU.mult),
                            waits=[mo, ms2] + (P.war("o_sb") if h == 0 else []), inc=1)
                        if h == 0:
                            P.guard["o_sb"] = {}
                        P.reads(("pb", obk), moev)
                        P.reads("rbc", moev)
                        m_oev = moev

                    if debug_streams and qg == 0:
                        P.dma("gpsimd", lambda g: g.dma_start(o_dbg[:, :, :], o_sb[:, :, :]),
                              waits=[m_oev])
                        P.dma("gpsimd", lambda g: g.dma_start(q_dbg[:, :, :], q_qg[:, :, :]),
                              waits=[m_qr])
                        P.dma("gpsimd", lambda g: g.dma_start(k_dbg[:, :, :], kr[:, :, :]),
                              waits=[m_kr])
                    mxl = P.dma("sync", lambda e, ccols=ccols: e.dma_start(
                        sb.x_fm[:, :, :], src_fm[:, :, ccols].rearrange("c p n -> p c n")),
                        waits=P.war("x_fm"))
                    P.guard["x_fm"] = {}
                    for dout in range(NDC):
                        wpar = dout % 2
                        mwl = P.dma("sync", lambda e, dout=dout, wpar=wpar: e.dma_start(
                            wsq[:, wpar], wo_c[dout]),
                            waits=[m_wo[dout]] + P.war(("wsq", wpar)))
                        P.guard[("wsq", wpar)] = {}
                        ybk = 2 + dout % 2
                        w = [mwl, m_oev] + P.war(("pb", ybk))
                        my = None
                        for dinc in range(NDC):
                            my = P.emit("tensor", lambda t, dout=dout, dinc=dinc, wpar=wpar, ybk=ybk: t.matmul(
                                pb[ybk][:, :], wsq[:, wpar, dinc, :], o_sb[:, dinc, :],
                                start=(dinc == 0), stop=(dinc == NDC - 1)),
                                waits=w if dinc == 0 else (), inc=1)
                        P.guard[("pb", ybk)] = {}
                        P.reads(("wsq", wpar), my)
                        P.reads("o_sb", my)
                        opar = dout % 2
                        mev = P.emit("vector", lambda v, opar=opar, dout=dout, ybk=ybk: v.tensor_tensor(
                            sb.xout[:, opar, :], pb[ybk][:, :], sb.x_fm[:, dout, :], op=ALU.add),
                            waits=[my, mxl] + P.war(("xout", opar)), inc=1)
                        P.guard[("xout", opar)] = {}
                        P.reads(("pb", ybk), mev)
                        P.reads("x_fm", mev)
                        mst = P.dma("scalar", lambda g, opar=opar, dout=dout, ccols=ccols: g.dma_start(
                            dst_fm[dout, :, ccols], sb.xout[:, opar, :]),
                            waits=[mev])
                        P.reads(("xout", opar), mst)
                P.fence()

        emit_ffn(src_tm=x_in, src_fm=None, dst_fm=s1, dst_tm=None, tag="f1")
        emit_attn(src_fm=s1, dst_fm=s2)
        emit_ffn(src_tm=None, src_fm=s2, dst_fm=None, dst_tm=out, tag="f2")

        P.emit("gpsimd", None, waits=[(f"gpsimd_d{i}", P.counts[f"gpsimd_d{i}"])
                                      for i in range(P.NDMASEM)] +
                               [(f"scalar_d{i}", P.counts[f"scalar_d{i}"])
                                for i in range(P.NDMASEM)])

        block = es.enter_context(nc.Block())

        @block.sync
        def _(e):
            for f in P.steps["sync"]:
                f(e)

        @block.gpsimd
        def _(e):
            for f in P.steps["gpsimd"]:
                f(e)

        @block.tensor
        def _(e):
            for f in P.steps["tensor"]:
                f(e)

        @block.vector
        def _(e):
            for f in P.steps["vector"]:
                f(e)

        @block.scalar
        def _(e):
            for f in P.steps["scalar"]:
                f(e)

    return nc


def host_constants():
    TINY = np.float32(2.0 ** -126)
    consts = np.zeros((128, 16), dtype=np.float32)
    p = np.arange(128)
    for k in (1, 2, 3):
        consts[:, k - 1] = np.where((p % 4) + k < 4, TINY, 0.0)
    consts[:, C_15] = 1.5
    consts[:, C_INV2048] = 1.0 / 2048.0
    consts[:, C_EPS] = 1e-5
    consts[:, C_NEG1] = -1.0
    consts[:, C_HALF] = 0.5
    consts[:, C_ISQHD] = 1.0 / np.sqrt(np.float32(HD))
    consts[:, C_ONE] = 1.0
    ident = np.eye(128, dtype=np.float32)
    sd = np.zeros((3, 128, 128), dtype=np.float32)
    for k in (1, 2, 3):
        for c in range(128):
            r = 4 * (c // 4) + (c + k) % 4
            sd[k - 1, r, c] += 1.0
            sd[k - 1, c, c] -= 1.0
    swapm = np.zeros((128, 128), dtype=np.float32)
    for c in range(128):
        r = c + 1 if c % 2 == 0 else c - 1
        swapm[r, c] = 1.0
    half = HD // 2
    inv = 1.0 / 10000.0 ** (np.arange(half, dtype=np.float64) / half)
    pos = np.arange(S, dtype=np.float64)
    ang = pos[None, :] * inv[:, None]
    ctab = np.repeat(np.cos(ang), 2, axis=0).astype(np.float32)
    sin2 = np.repeat(np.sin(ang), 2, axis=0)
    sign = np.where(np.arange(128) % 2 == 0, -1.0, 1.0)
    stab = (sin2 * sign[:, None]).astype(np.float32)
    return consts, ident, sd, swapm, ctab, stab


def kernel(L0_attn_x, L0_attn_W_q, L0_attn_W_k, L0_attn_W_v, L0_attn_W_o,
           L0_ffn_w_gate, L0_ffn_w_up, L0_ffn_w_down):
    x = np.ascontiguousarray(np.asarray(L0_attn_x, dtype=np.float32))
    consts, ident, sd, swapm, ctab, stab = host_constants()
    common = {
        "wq": np.ascontiguousarray(np.asarray(L0_attn_W_q, dtype=np.float32)),
        "wk": np.ascontiguousarray(np.asarray(L0_attn_W_k, dtype=np.float32)),
        "wv": np.ascontiguousarray(np.asarray(L0_attn_W_v, dtype=np.float32)),
        "wo": np.ascontiguousarray(np.asarray(L0_attn_W_o, dtype=np.float32)),
        "wg": np.ascontiguousarray(np.asarray(L0_ffn_w_gate, dtype=np.float32)),
        "wu": np.ascontiguousarray(np.asarray(L0_ffn_w_up, dtype=np.float32)),
        "wd": np.ascontiguousarray(np.asarray(L0_ffn_w_down, dtype=np.float32)),
        "consts": consts, "ident": ident, "sd": sd, "swapm": swapm,
        "ctab": ctab, "stab": stab,
    }
    nc = build()
    in_maps = [dict(common, x=np.ascontiguousarray(x[i])) for i in range(8)]
    res = run_bass_kernel_spmd(nc, in_maps, core_ids=list(range(8)))
    return np.stack([res.results[i]["out"] for i in range(8)], axis=0)


# version 15
# speedup vs baseline: 1.2774x; 1.0817x over previous
import numpy as np
import concourse.bass as bass
import concourse.mybir as mybir
from concourse.bass_utils import run_bass_kernel_spmd
from contextlib import ExitStack

F32 = mybir.dt.float32
BF16 = mybir.dt.bfloat16
AF = mybir.ActivationFunctionType
ALU = mybir.AluOpType

D = 2048
S = 2048
II = 8192
NH, NKV, HD = 16, 4, 128
NDC = D // 128
NIC = II // 128
TSUP = 512
NSUP = S // TSUP
QG = 512
NQG = S // QG
NTT = TSUP // 128

C_TAU1, C_TAU2, C_TAU3, C_15, C_INV2048, C_EPS, C_NEG1, C_HALF, C_ISQHD, C_ONE = range(10)
EXP_SHIFT = 10.0

ENGS = ("sync", "gpsimd", "tensor", "vector", "scalar")


class Prog:

    NDMASEM = 8

    def __init__(self, nc, es):
        self.nc = nc
        self.steps = {e: [] for e in ENGS}
        self.sems = {e: es.enter_context(nc.semaphore("sem_" + e)) for e in ENGS}
        for q in ("sync", "gpsimd", "scalar"):
            for i in range(self.NDMASEM):
                n = f"{q}_d{i}"
                self.sems[n] = es.enter_context(nc.semaphore("sem_" + n))
        self.counts = {e: 0 for e in self.sems}
        self.rr = {"sync": 0, "gpsimd": 0, "scalar": 0}
        self.guard = {}

    def emit(self, eng, fn, waits=(), inc=0):
        sems = self.sems
        wl = []
        for w in waits:
            if w is None:
                continue
            en, val = w
            if val > 0:
                wl.append((en, val))
        wl = tuple(wl)

        def step(e, fn=fn, wl=wl, inc=inc, sem=self.sems[eng]):
            for (en, val) in wl:
                e.wait_ge(sems[en], val)
            ins = fn(e) if fn is not None else None
            if inc:
                ins.then_inc(sem, inc)

        self.steps[eng].append(step)
        if inc:
            self.counts[eng] += inc
        return (eng, self.counts[eng])

    def dma(self, queue, fn, waits=()):
        i = self.rr[queue] % self.NDMASEM
        self.rr[queue] += 1
        sname = f"{queue}_d{i}"
        w = list(waits)
        if self.counts[sname] > 0:
            w.append((sname, self.counts[sname]))
        sems = self.sems
        wl = []
        for x in w:
            if x is None:
                continue
            en, val = x
            if val > 0:
                wl.append((en, val))
        wl = tuple(wl)

        def step(e, fn=fn, wl=wl, sem=self.sems[sname]):
            for (en, val) in wl:
                e.wait_ge(sems[en], val)
            fn(e).then_inc(sem, 16)

        self.steps[queue].append(step)
        self.counts[sname] += 16
        return (sname, self.counts[sname])

    def reads(self, key, milestone):
        d = self.guard.setdefault(key, {})
        d[milestone[0]] = max(d.get(milestone[0], 0), milestone[1])

    def war(self, key):
        d = self.guard.get(key, {})
        return [(e, v) for e, v in d.items()]

    def fence(self):
        snap = dict(self.counts)
        for e in ENGS:
            self.emit(e, None, waits=[(o, v) for o, v in snap.items() if o != e])


class SB:
    pass


def emit_ln(P, pb, sb, load_chunk, xln_dst, xln_key):
    for c in range(NDC):
        ap, w, key = load_chunk(c, 0)
        emit_ln_stat_chunk(P, pb, sb, c, ap, w, key)
    return emit_ln_finish(P, pb, sb, load_chunk, xln_dst, xln_key)


def emit_ln_stat_chunk(P, pb, sb, c, ap, w, key):
    par = c % 2
    w = list(w) + P.war(("lncast", par))
    mcast = P.emit("scalar", lambda s, ap=ap, par=par: s.activation(
        sb.xb[:, par, :], ap, AF.Copy), waits=w, inc=1)
    msq = P.emit("scalar", lambda s, ap=ap, par=par: s.activation(
        sb.x2b[:, par, :], ap, AF.Square), inc=1)
    P.reads(key, mcast)
    P.reads(key, msq)
    w = [msq]
    if c == 0:
        w += P.war(("pb", 4)) + P.war(("pb", 5)) + [P.m_ones]
    P.emit("tensor", lambda t, c=c, par=par: t.matmul(
        pb[4][:, :], sb.ones_mat[:, :], sb.xb[:, par, :],
        start=(c == 0), stop=(c == NDC - 1)), waits=w, inc=1)
    m_sum = P.emit("tensor", lambda t, c=c, par=par: t.matmul(
        pb[5][:, :], sb.ones_mat[:, :], sb.x2b[:, par, :],
        start=(c == 0), stop=(c == NDC - 1)), inc=1)
    P.guard[("lncast", par)] = {}
    P.reads(("lncast", par), m_sum)
    P.last_stat = m_sum
    return m_sum


def emit_ln_finish(P, pb, sb, load_chunk, xln_dst, xln_key):
    cs = sb.consts
    m1 = P.emit("vector", lambda v: v.tensor_scalar(
        sb.bc_m[:, :], pb[4][:, :], cs[:, C_INV2048:C_INV2048 + 1], None,
        op0=ALU.mult), waits=[P.last_stat] + P.war("bc_m"), inc=1)
    m2 = P.emit("vector", lambda v: v.tensor_scalar(
        sb.t_norm[:, :], pb[5][:, :], cs[:, C_INV2048:C_INV2048 + 1],
        cs[:, C_EPS:C_EPS + 1], op0=ALU.mult, op1=ALU.add), inc=1)
    P.guard[("pb", 4)] = {}
    P.guard[("pb", 5)] = {}
    P.guard["bc_m"] = {}
    P.reads(("pb", 4), m1)
    P.reads(("pb", 5), m2)
    m3 = P.emit("vector", lambda v: v.tensor_tensor(
        sb.bc_b[:, :], sb.bc_m[:, :], sb.bc_m[:, :], op=ALU.mult),
        waits=P.war("bc_b"), inc=1)
    m4 = P.emit("vector", lambda v: v.tensor_tensor(
        sb.bc_b[:, :], sb.t_norm[:, :], sb.bc_b[:, :], op=ALU.subtract), inc=1)
    m5 = P.emit("scalar", lambda s: s.activation(
        sb.bc_ln[:, :], sb.bc_b[:, :], AF.Ln), waits=[m4] + P.war("bc_ln"), inc=1)
    m6 = P.emit("scalar", lambda s: s.activation(
        sb.bc_a[:, :], sb.bc_ln[:, :], AF.Exp, scale=np.float32(-0.5)),
        waits=P.war("bc_a"), inc=1)
    P.guard["bc_ln"] = {}
    P.guard["bc_a"] = {}
    P.reads("bc_ln", m6)
    m7 = P.emit("vector", lambda v: v.scalar_tensor_tensor(
        sb.bc_b[:, :], sb.bc_m[:, :], cs[:, C_NEG1:C_NEG1 + 1],
        sb.bc_a[:, :], op0=ALU.mult, op1=ALU.mult),
        waits=[m6], inc=1)
    P.reads("bc_m", m7)
    P.reads("bc_a", m7)
    mnorm = [None] * NDC
    for c in range(NDC):
        ap, w2, key2 = load_chunk(c, 1)
        w = list(w2) + ([m6] if c == 0 else []) + (P.war(xln_key) if c == 0 else [])
        mn1 = P.emit("vector", lambda v, ap=ap: v.tensor_tensor(
            sb.t_norm[:, :], ap, sb.bc_a[:, :], op=ALU.mult),
            waits=w, inc=1)
        P.reads(key2, mn1)
        P.reads("bc_a", mn1)
        mnorm[c] = P.emit("vector", lambda v, c=c: v.tensor_tensor(
            xln_dst[:, c, :], sb.t_norm[:, :], sb.bc_b[:, :], op=ALU.add), inc=1)
        P.reads("bc_b", mnorm[c])
    P.guard[xln_key] = {}
    return mnorm


def build(debug_streams=False):
    nc = bass.Bass("TRN2", target_bir_lowering=False)

    x_in = nc.dram_tensor("x", [S, D], F32, kind="ExternalInput")
    wq_in = nc.dram_tensor("wq", [D, D], F32, kind="ExternalInput")
    wk_in = nc.dram_tensor("wk", [D, 512], F32, kind="ExternalInput")
    wv_in = nc.dram_tensor("wv", [D, 512], F32, kind="ExternalInput")
    wo_in = nc.dram_tensor("wo", [D, D], F32, kind="ExternalInput")
    wg_in = nc.dram_tensor("wg", [D, II], F32, kind="ExternalInput")
    wu_in = nc.dram_tensor("wu", [D, II], F32, kind="ExternalInput")
    wd_in = nc.dram_tensor("wd", [II, D], F32, kind="ExternalInput")
    consts_in = nc.dram_tensor("consts", [128, 16], F32, kind="ExternalInput")
    ident_in = nc.dram_tensor("ident", [128, 128], F32, kind="ExternalInput")
    sd_in = nc.dram_tensor("sd", [3, 128, 128], F32, kind="ExternalInput")
    swap_in = nc.dram_tensor("swapm", [128, 128], F32, kind="ExternalInput")
    ctab_in = nc.dram_tensor("ctab", [128, S], F32, kind="ExternalInput")
    stab_in = nc.dram_tensor("stab", [128, S], F32, kind="ExternalInput")
    out = nc.dram_tensor("out", [S, D], F32, kind="ExternalOutput")

    skind = "ExternalOutput" if debug_streams else "Internal"
    s1 = nc.dram_tensor("s1", [NDC, 128, S], F32, kind=skind)
    s2 = nc.dram_tensor("s2", [NDC, 128, S], F32, kind=skind)
    if debug_streams:
        pt_dbg = nc.dram_tensor("pt_dbg", [128, NDC, QG], F32, kind="ExternalOutput")
        o_dbg = nc.dram_tensor("o_dbg", [128, NH, QG], F32, kind="ExternalOutput")
        q_dbg = nc.dram_tensor("q_dbg", [128, NH, QG], F32, kind="ExternalOutput")
        k_dbg = nc.dram_tensor("k_dbg", [128, NKV, S], F32, kind="ExternalOutput")
    else:
        pt_dbg = o_dbg = q_dbg = k_dbg = None

    xlnf_d = nc.dram_tensor("xlnf_d", [NDC, 128, S], BF16, kind=skind)
    wk_dbg = nc.dram_tensor("wk_c_dbg", [NKV, 128, NDC, 128], BF16) if False else None
    wg_c = nc.dram_tensor("wg_c", [NIC // 2, 128, 2, NDC, 128], BF16)
    wu_c = nc.dram_tensor("wu_c", [NIC // 2, 128, 2, NDC, 128], BF16)
    wd_c = nc.dram_tensor("wd_c", [NDC, 4, 128, 16, 128], BF16)
    wq_c = nc.dram_tensor("wq_c", [NH, 128, NDC, 128], BF16)
    wk_c = nc.dram_tensor("wk_c", [NKV, 128, NDC, 128], BF16)
    wo_c = nc.dram_tensor("wo_c", [NDC, 128, NDC, 128], BF16)
    wv_c = nc.dram_tensor("wv_c", [128, NDC, 512], BF16)

    es = ExitStack()
    with es:
        sb = SB()
        sb.consts = es.enter_context(nc.sbuf_tensor("consts_sb", [128, 16], F32))
        sb.ones_mat = es.enter_context(nc.sbuf_tensor("ones_mat", [128, 128], BF16))
        sb.ident = es.enter_context(nc.sbuf_tensor("ident_sb", [128, 128], F32))
        sb.sd = es.enter_context(nc.sbuf_tensor("sd_sb", [128, 3, 128], BF16))
        sb.swapm = es.enter_context(nc.sbuf_tensor("swap_sb", [128, 128], F32))
        sb.bc_m = es.enter_context(nc.sbuf_tensor("bc_m", [128, 512], F32))
        sb.bc_ln = es.enter_context(nc.sbuf_tensor("bc_ln", [128, 512], F32))
        sb.xb = es.enter_context(nc.sbuf_tensor("xb_sb", [128, 2, 512], BF16))
        sb.x2b = es.enter_context(nc.sbuf_tensor("x2b_sb", [128, 2, 512], BF16))
        sb.bc_a = es.enter_context(nc.sbuf_tensor("bc_a", [128, 512], F32))
        sb.bc_b = es.enter_context(nc.sbuf_tensor("bc_b", [128, 512], F32))
        sb.t_norm = es.enter_context(nc.sbuf_tensor("t_norm", [128, 512], F32))
        sb.x_fm = es.enter_context(nc.sbuf_tensor("x_fm", [128, NDC, 512], F32))
        sb.xs = es.enter_context(nc.sbuf_tensor("xs", [128, 4, 512], F32))
        sb.xout = es.enter_context(nc.sbuf_tensor("xout", [128, 2, 512], F32))

        pbig = es.enter_context(nc.psum_tensor("pbig", [128, 8, 512], F32))
        pb = [pbig[:, i, :] for i in range(8)]

        P = Prog(nc, es)

        P.dma("gpsimd", lambda g: g.dma_start(sb.consts[:, :], consts_in[:, :]))
        P.dma("gpsimd", lambda g: g.dma_start(sb.ident[:, :], ident_in[:, :]))
        P.dma("gpsimd", lambda g: g.dma_start(
            sb.sd[:, :, :], sd_in.rearrange("k p f -> p k f")))
        m_const = P.dma("gpsimd", lambda g: g.dma_start(sb.swapm[:, :], swap_in[:, :]))
        m_ones = P.emit("vector", lambda v: v.memset(sb.ones_mat[:, :], 1.0),
                        waits=[m_const], inc=1)
        P.m_ones = m_ones

        wg_src = wg_in.rearrange("(dc p) (ib icl f) -> ib p icl dc f", p=128, icl=2, f=128)
        wu_src = wu_in.rearrange("(dc p) (ib icl f) -> ib p icl dc f", p=128, icl=2, f=128)
        m_wg = [None] * (NIC // 2)
        m_wu = [None] * (NIC // 2)
        wd_src = wd_in.rearrange("(ibt ibl p) (dc f) -> dc ibt p ibl f", ibl=16, p=128, f=128)
        m_wd = [[None] * 4 for _ in range(NDC)]
        wq_src = wq_in.rearrange("(dc p) (h f) -> h p dc f", p=128, f=128)
        wk_src = wk_in.rearrange("(dc p) (h f) -> h p dc f", p=128, f=128)
        wo_src = wo_in.rearrange("(dc p) (h f) -> h p dc f", p=128, f=128)
        wv_src = wv_in.rearrange("(dc p) f -> p dc f", p=128)
        for b in range(NIC // 2):
            P.dma("gpsimd", lambda g, b=b: g.dma_start(wg_c[b][:, 0], wg_src[b][:, 0]))
            m_wg[b] = P.dma("gpsimd", lambda g, b=b: g.dma_start(wg_c[b][:, 1], wg_src[b][:, 1]))
            P.dma("gpsimd", lambda g, b=b: g.dma_start(wu_c[b][:, 0], wu_src[b][:, 0]))
            m_wu[b] = P.dma("gpsimd", lambda g, b=b: g.dma_start(wu_c[b][:, 1], wu_src[b][:, 1]))
        m_wk = [None] * NKV
        m_wv = None
        m_wq = [None] * NH
        m_wo = [None] * NDC
        attn_jobs = []
        for i in range(NKV):
            def jk(w, i=i):
                m_wk[i] = P.dma("gpsimd", lambda g, i=i: g.dma_start(
                    wk_c[i], wk_src[i]), waits=w)
            attn_jobs.append(jk)

        def jv(w):
            nonlocal m_wv
            m_wv = P.dma("gpsimd", lambda g: g.dma_start(wv_c[:, :, :], wv_src),
                         waits=w)
        attn_jobs.append(jv)
        for i in range(NH):
            def jq(w, i=i):
                m_wq[i] = P.dma("gpsimd", lambda g, i=i: g.dma_start(
                    wq_c[i], wq_src[i]), waits=w)
            attn_jobs.append(jq)
        for i in range(NDC):
            def jo(w, i=i):
                m_wo[i] = P.dma("gpsimd", lambda g, i=i: g.dma_start(
                    wo_c[i], wo_src[i]), waits=w)
            attn_jobs.append(jo)

        def cast_hook(st, ic, mg):
            if st == 0:
                dc, ibt = ic // 4, ic % 4
                m_wd[dc][ibt] = P.dma("gpsimd", lambda g, dc=dc, ibt=ibt: g.dma_start(
                    wd_c[dc, ibt], wd_src[dc, ibt]), waits=[mg])
            elif st == 1 and ic < len(attn_jobs):
                attn_jobs[ic]([mg])

        P.emit("vector", None, waits=[m_const])
        P.emit("scalar", None, waits=[m_const])
        P.emit("tensor", None, waits=[m_const])

        xs_rr = [0]

        def stream_chunk(src_fm, cols):
            def load(c, phase=0):
                slot = xs_rr[0] % 4
                xs_rr[0] += 1
                m = P.dma("sync", lambda e, c=c, slot=slot: e.dma_start(
                    sb.xs[:, slot, :], src_fm[c, :, cols]),
                    waits=P.war(("xs", slot)))
                P.guard[("xs", slot)] = {}
                return (sb.xs[:, slot, :], [m], ("xs", slot))
            return load

        def emit_ffn(src_tm, src_fm, dst_fm, dst_tm, tag, cast_hook=None):
            esf = ExitStack()
            with esf:
                xln = esf.enter_context(nc.sbuf_tensor("xln_f_" + tag, [128, NDC, TSUP], BF16))
                h_sp = esf.enter_context(nc.sbuf_tensor("h_sp_" + tag, [128, NIC, TSUP], BF16))
                xtm = esf.enter_context(nc.sbuf_tensor("xtm_" + tag, [128, 2, D], F32))
                wsg = esf.enter_context(nc.sbuf_tensor("wsg_" + tag, [128, 2, NDC, 128], BF16))
                wsu = esf.enter_context(nc.sbuf_tensor("wsu_" + tag, [128, 2, NDC, 128], BF16))
                wsd = esf.enter_context(nc.sbuf_tensor("wsd_" + tag, [128, 2, 16, 128], BF16))
                g_sb = esf.enter_context(nc.sbuf_tensor("g_sb_" + tag, [128, 2, TSUP], F32))
                h_f32 = esf.enter_context(nc.sbuf_tensor("h_f32_" + tag, [128, 2, TSUP], F32))
                a_bf = esf.enter_context(nc.sbuf_tensor("a_bf_" + tag, [128, 2, TSUP], BF16))
                sp_c = esf.enter_context(nc.sbuf_tensor("sp_c_" + tag, [128, TSUP], BF16))

                for st in range(NSUP):
                    cols = slice(st * TSUP, (st + 1) * TSUP)
                    if src_tm is not None:
                        for tt in range(NTT):
                            par = tt % 2
                            row0 = st * TSUP + tt * 128
                            mld = P.dma("sync", lambda e, par=par, row0=row0: e.dma_start(
                                xtm[:, par, :], src_tm[row0:row0 + 128, :]),
                                waits=P.war(("xtm", par)))
                            P.guard[("xtm", par)] = {}
                            for grp in range(4):
                                bk = 6 + grp % 2
                                w = [mld] + (P.war("x_fm") if (tt, grp) == (0, 0) else []) \
                                    + P.war(("pb", bk))
                                for k in range(4):
                                    dc = grp * 4 + k
                                    mt = P.emit("tensor", lambda t, par=par, dc=dc, bk=bk, k=k: t.transpose(
                                        pb[bk][:, k * 128:(k + 1) * 128],
                                        xtm[:, par, dc * 128:(dc + 1) * 128], sb.ident[:, :]),
                                        waits=w if k == 0 else (), inc=1)
                                P.reads(("xtm", par), mt)
                                P.guard[("pb", bk)] = {}
                                dst_ap = sb.x_fm[:, grp * 4:(grp + 1) * 4, tt * 128:(tt + 1) * 128]
                                mcp = P.emit("vector", lambda v, bk=bk, dst_ap=dst_ap: v.tensor_copy(
                                    dst_ap, pb[bk].rearrange("p (k f) -> p k f", k=4)),
                                    waits=[mt], inc=1)
                                P.reads(("pb", bk), mcp)
                        if (st, 0) == (0, 0):
                            P.guard["x_fm"] = {}
                        m_xready = [mcp]
                        def loader(c, phase=0, m_xready=tuple(m_xready)):
                            return (sb.x_fm[:, c, :],
                                    list(m_xready) if c == 0 else [], "x_fm")
                    else:
                        loader = stream_chunk(src_fm, cols)

                    if src_fm is not None and st > 0:
                        mln = emit_ln_finish(P, pb, sb, stream_chunk(src_fm, cols),
                                             xln, "xln")[NDC - 1]
                    else:
                        mln = emit_ln(P, pb, sb, loader, xln, "xln")[NDC - 1]

                    m_hsp_last = None
                    pend = None
                    for ic in range(NIC + 1):
                        if ic < NIC:
                            par = ic % 2
                            b, icl = ic // 2, ic % 2
                            bpar = ic % 2
                            mwg_l = P.dma("sync", lambda e, b=b, icl=icl, bpar=bpar: e.dma_start(
                                wsg[:, bpar], wg_c[b][:, icl]),
                                waits=[m_wg[b]] + P.war(("wsg", bpar)))
                            mwu_l = P.dma("sync", lambda e, b=b, icl=icl, bpar=bpar: e.dma_start(
                                wsu[:, bpar], wu_c[b][:, icl]),
                                waits=[m_wu[b]] + P.war(("wsu", bpar)))
                            P.guard[("wsg", bpar)] = {}
                            P.guard[("wsu", bpar)] = {}
                            gbk, ubk = 0 + par, 2 + par
                            w = [mwg_l, mln] + P.war(("pb", gbk))
                            for dc in range(NDC):
                                mg = P.emit("tensor", lambda t, icl=icl, bpar=bpar, dc=dc, gbk=gbk: t.matmul(
                                    pb[gbk][:, :], wsg[:, bpar, dc, :], xln[:, dc, :],
                                    start=(dc == 0), stop=(dc == NDC - 1)),
                                    waits=w if dc == 0 else (), inc=1)
                            P.guard[("pb", gbk)] = {}
                            P.reads(("wsg", bpar), mg)
                            P.reads("xln", mg)
                            if cast_hook is not None:
                                cast_hook(st, ic, mg)
                            w = [mwu_l] + P.war(("pb", ubk))
                            for dc in range(NDC):
                                mu = P.emit("tensor", lambda t, icl=icl, bpar=bpar, dc=dc, ubk=ubk: t.matmul(
                                    pb[ubk][:, :], wsu[:, bpar, dc, :], xln[:, dc, :],
                                    start=(dc == 0), stop=(dc == NDC - 1)),
                                    waits=w if dc == 0 else (), inc=1)
                            P.guard[("pb", ubk)] = {}
                            P.reads(("wsu", bpar), mu)
                            P.reads("xln", mu)
                            msl = P.emit("scalar", lambda s, par=par, gbk=gbk: s.activation(
                                g_sb[:, par, :], pb[gbk][:, :], AF.Silu),
                                waits=[mg] + P.war(("g_sb", par)), inc=1)
                            P.guard[("g_sb", par)] = {}
                            P.reads(("pb", gbk), msl)
                            mh = P.emit("vector", lambda v, par=par, ubk=ubk: v.tensor_tensor(
                                h_f32[:, par, :], pb[ubk][:, :], g_sb[:, par, :], op=ALU.mult),
                                waits=[msl, mu] + P.war(("h_f32", par)), inc=1)
                            P.guard[("h_f32", par)] = {}
                            P.reads(("pb", ubk), mh)
                            P.reads(("g_sb", par), mh)
                            mab = P.emit("scalar", lambda s, par=par: s.activation(
                                a_bf[:, par, :], h_f32[:, par, :], AF.Abs),
                                waits=[mh] + P.war(("a_bf", par)), inc=1)
                            P.guard[("a_bf", par)] = {}
                            P.reads(("h_f32", par), mab)
                            this = (ic, par, mab)
                        if pend is not None:
                            pic, ppar, pmab = pend
                            w = [pmab] + P.war(("pb", 4)) + P.war(("pb", 5)) + P.war(("pb", 6))
                            for k in range(3):
                                msh = P.emit("tensor", lambda t, k=k, ppar=ppar: t.matmul(
                                    pb[4 + k][:, :], sb.sd[:, k, :], a_bf[:, ppar, :],
                                    start=True, stop=True), waits=w if k == 0 else (), inc=1)
                            P.guard[("pb", 4)] = {}
                            P.guard[("pb", 5)] = {}
                            P.guard[("pb", 6)] = {}
                            P.reads(("a_bf", ppar), msh)
                            cs = sb.consts
                            mg1 = P.emit("vector", lambda v: v.tensor_scalar(
                                sp_c[:, :], pb[4][:, :], cs[:, C_TAU1:C_TAU1 + 1], None,
                                op0=ALU.is_ge), waits=[msh], inc=1)
                            P.reads(("pb", 4), mg1)
                            mg2 = P.emit("vector", lambda v: v.scalar_tensor_tensor(
                                sp_c[:, :], pb[5][:, :], cs[:, C_TAU2:C_TAU2 + 1],
                                sp_c[:, :], op0=ALU.is_ge, op1=ALU.add), inc=1)
                            P.reads(("pb", 5), mg2)
                            mg3 = P.emit("vector", lambda v: v.scalar_tensor_tensor(
                                sp_c[:, :], pb[6][:, :], cs[:, C_TAU3:C_TAU3 + 1],
                                sp_c[:, :], op0=ALU.is_ge, op1=ALU.add), inc=1)
                            P.reads(("pb", 6), mg3)
                            m_hsp_last = P.emit("vector", lambda v, ppar=ppar, pic=pic: v.scalar_tensor_tensor(
                                h_sp[:, pic, :], sp_c[:, :], cs[:, C_15:C_15 + 1],
                                h_f32[:, ppar, :], op0=ALU.is_le, op1=ALU.mult),
                                waits=P.war("h_sp") if pic == 0 else (), inc=1)
                            if pic == 0:
                                P.guard["h_sp"] = {}
                            P.reads(("h_f32", ppar), m_hsp_last)
                        if ic < NIC:
                            pend = this

                    loader_b = (stream_chunk(src_fm, cols) if src_fm is not None else None)
                    pipeline_next = src_fm is not None and st + 1 < NSUP
                    loader_next = (stream_chunk(
                        src_fm, slice((st + 1) * TSUP, (st + 2) * TSUP))
                        if pipeline_next else None)
                    m_evacs = [None] * NDC
                    for dout in range(NDC):
                        ybk = 0 + dout % 2
                        md = None
                        for ibt in range(4):
                            bpar = (dout * 4 + ibt) % 2
                            mwd_l = P.dma("sync", lambda e, dout=dout, ibt=ibt, bpar=bpar: e.dma_start(
                                wsd[:, bpar], wd_c[dout, ibt]),
                                waits=[m_wd[dout][ibt]] + P.war(("wsd", bpar)))
                            P.guard[("wsd", bpar)] = {}
                            w = [mwd_l]
                            if ibt == 0:
                                w += [m_hsp_last] + P.war(("pb", ybk))
                            for ibl in range(16):
                                ic = ibt * 16 + ibl
                                md = P.emit("tensor", lambda t, bpar=bpar, ibl=ibl, ic=ic, ybk=ybk: t.matmul(
                                    pb[ybk][:, :], wsd[:, bpar, ibl, :], h_sp[:, ic, :],
                                    start=(ic == 0), stop=(ic == NIC - 1)),
                                    waits=w if ibl == 0 else (), inc=1)
                            if ibt == 0:
                                P.guard[("pb", ybk)] = {}
                            P.reads(("wsd", bpar), md)
                            P.reads("h_sp", md)
                        opar = dout % 2
                        if dst_fm is not None:
                            mev = P.emit("vector", lambda v, opar=opar, dout=dout, ybk=ybk: v.scalar_tensor_tensor(
                                sb.xout[:, opar, :], pb[ybk][:, :], sb.consts[:, C_HALF:C_HALF + 1],
                                sb.x_fm[:, dout, :], op0=ALU.mult, op1=ALU.add),
                                waits=[md] + P.war(("xout", opar)), inc=1)
                            P.guard[("xout", opar)] = {}
                            P.reads(("pb", ybk), mev)
                            P.reads("x_fm", mev)
                            mst = P.dma("scalar", lambda g, opar=opar, dout=dout, cols=cols: g.dma_start(
                                dst_fm[dout, :, cols], sb.xout[:, opar, :]),
                                waits=[mev])
                            P.reads(("xout", opar), mst)
                        else:
                            xap, xw, xkey = loader_b(dout)
                            mev = P.emit("vector", lambda v, dout=dout, ybk=ybk, xap=xap: v.scalar_tensor_tensor(
                                sb.x_fm[:, dout, :], pb[ybk][:, :], sb.consts[:, C_HALF:C_HALF + 1],
                                xap, op0=ALU.mult, op1=ALU.add),
                                waits=[md] + list(xw) + (P.war("x_fm") if dout == 0 else []), inc=1)
                            if dout == 0:
                                P.guard["x_fm"] = {}
                            P.reads(("pb", ybk), mev)
                            P.reads(xkey, mev)
                            P.reads("x_fm", mev)
                            m_evacs[dout] = mev
                        if pipeline_next:
                            ap3, w3, key3 = loader_next(dout)
                            emit_ln_stat_chunk(P, pb, sb, dout, ap3, w3, key3)

                    if dst_tm is not None:
                        for tt in range(NTT):
                            par = tt % 2
                            row0 = st * TSUP + tt * 128
                            for grp in range(4):
                                bk = 6 + grp % 2
                                w = ([m_evacs[grp * 4 + 3]] if tt == 0 else []) + P.war(("pb", bk))
                                for k in range(4):
                                    dc = grp * 4 + k
                                    mt = P.emit("tensor", lambda t, dc=dc, bk=bk, k=k, tt=tt: t.transpose(
                                        pb[bk][:, k * 128:(k + 1) * 128],
                                        sb.x_fm[:, dc, tt * 128:(tt + 1) * 128], sb.ident[:, :]),
                                        waits=w if k == 0 else (), inc=1)
                                P.reads("x_fm", mt)
                                P.guard[("pb", bk)] = {}
                                dst_ap = xtm[:, par, grp * 512:(grp + 1) * 512]
                                mcp = P.emit("vector", lambda v, bk=bk, dst_ap=dst_ap: v.tensor_copy(
                                    dst_ap, pb[bk][:, :]),
                                    waits=[mt] + (P.war(("xtm", par)) if grp == 0 else []), inc=1)
                                if grp == 0:
                                    P.guard[("xtm", par)] = {}
                                P.reads(("pb", bk), mcp)
                            mst = P.dma("scalar", lambda g, par=par, row0=row0: g.dma_start(
                                dst_tm[row0:row0 + 128, :], xtm[:, par, :]),
                                waits=[mcp])
                            P.reads(("xtm", par), mst)
                P.fence()

        def emit_attn(src_fm, dst_fm):
            esf = ExitStack()
            with esf:
                xlnb = esf.enter_context(nc.sbuf_tensor("xlnb", [128, NDC, 512], BF16))
                kr = esf.enter_context(nc.sbuf_tensor("kr", [128, NKV, S], BF16))
                v_tm = esf.enter_context(nc.sbuf_tensor("v_tm", [128, NDC, 512], BF16))
                ct_sb = esf.enter_context(nc.sbuf_tensor("ct_sb", [128, 512], F32))
                st_sb = esf.enter_context(nc.sbuf_tensor("st_sb", [128, 512], F32))
                q_qg = esf.enter_context(nc.sbuf_tensor("q_qg", [128, NH, QG], BF16))
                pt = esf.enter_context(nc.sbuf_tensor("pt", [128, NDC, QG], BF16))
                o_sb = esf.enter_context(nc.sbuf_tensor("o_sb", [128, NH, QG], BF16))
                wsq = esf.enter_context(nc.sbuf_tensor("wsq", [128, 2, NDC, 128], BF16))
                wv_sb = esf.enter_context(nc.sbuf_tensor("wv_sb", [128, NDC, 512], BF16))
                q_f32 = esf.enter_context(nc.sbuf_tensor("q_f32", [128, 2, 512], F32))
                t_rope = esf.enter_context(nc.sbuf_tensor("t_rope", [128, 2, 512], F32))
                rbc = esf.enter_context(nc.sbuf_tensor("rbc", [128, 512], F32))
                rln = esf.enter_context(nc.sbuf_tensor("rln", [128, 512], F32))
                cs = sb.consts

                mwv_l = P.dma("sync", lambda e: e.dma_start(wv_sb[:, :, :], wv_c[:, :, :]),
                               waits=[m_wv])

                def load_tabs(ccols):
                    m1 = P.dma("sync", lambda e, ccols=ccols: e.dma_start(
                        ct_sb[:, :], ctab_in[:, ccols]), waits=P.war("tabs"))
                    m2 = P.dma("sync", lambda e, ccols=ccols: e.dma_start(
                        st_sb[:, :], stab_in[:, ccols]), waits=P.war("tabs"))
                    P.guard["tabs"] = {}
                    return [m1, m2]

                def rope(src_bank, par2, ccols, dst_ap, scaled, extra_war_key, ready, mtab):
                    mev = P.emit("vector", lambda v, par2=par2, src_bank=src_bank: v.tensor_copy(
                        q_f32[:, par2, :], pb[src_bank][:, :]),
                        waits=[ready] + P.war(("q_f32", par2)), inc=1)
                    P.guard[("q_f32", par2)] = {}
                    P.reads(("pb", src_bank), mev)
                    msw = P.emit("tensor", lambda t, par2=par2: t.matmul(
                        pb[2 + par2][:, :], sb.swapm[:, :], q_f32[:, par2, :],
                        start=True, stop=True),
                        waits=[mev] + P.war(("pb", 2 + par2)), inc=1)
                    P.guard[("pb", 2 + par2)] = {}
                    P.reads(("q_f32", par2), msw)
                    if scaled:
                        mr1 = P.emit("vector", lambda v, par2=par2, ccols=ccols: v.scalar_tensor_tensor(
                            t_rope[:, par2, :], q_f32[:, par2, :], cs[:, C_ISQHD:C_ISQHD + 1],
                            ct_sb[:, :], op0=ALU.mult, op1=ALU.mult),
                            waits=mtab + [msw] + P.war(("t_rope", par2)), inc=1)
                        mr2 = P.emit("vector", lambda v, par2=par2, ccols=ccols: v.scalar_tensor_tensor(
                            q_f32[:, par2, :], pb[2 + par2][:, :], cs[:, C_ISQHD:C_ISQHD + 1],
                            st_sb[:, :], op0=ALU.mult, op1=ALU.mult), inc=1)
                    else:
                        mr1 = P.emit("vector", lambda v, par2=par2: v.tensor_tensor(
                            t_rope[:, par2, :], q_f32[:, par2, :], ct_sb[:, :], op=ALU.mult),
                            waits=mtab + [msw] + P.war(("t_rope", par2)), inc=1)
                        mr2 = P.emit("vector", lambda v, par2=par2: v.tensor_tensor(
                            q_f32[:, par2, :], pb[2 + par2][:, :], st_sb[:, :], op=ALU.mult),
                            inc=1)
                    P.guard[("t_rope", par2)] = {}
                    P.guard[("q_f32", par2)] = {}
                    P.reads("tabs", mr1)
                    P.reads("tabs", mr2)
                    P.reads(("pb", 2 + par2), mr2)
                    mr3 = P.emit("vector", lambda v, par2=par2, dst_ap=dst_ap: v.tensor_tensor(
                        dst_ap, t_rope[:, par2, :], q_f32[:, par2, :], op=ALU.add),
                        waits=P.war(extra_war_key), inc=1)
                    P.guard[extra_war_key] = {}
                    P.reads(("t_rope", par2), mr3)
                    P.reads(("q_f32", par2), mr3)
                    return mr3

                m_kr = None
                m_v = None
                m_xst = [None] * NSUP
                for st in range(NSUP):
                    cols = slice(st * TSUP, (st + 1) * TSUP)
                    mln = emit_ln(P, pb, sb, stream_chunk(src_fm, cols), xlnb, "xlnf")[NDC - 1]
                    m_xst[st] = P.dma("scalar", lambda g, cols=cols: g.dma_start(
                        xlnf_d[:, :, cols].rearrange("c p n -> p c n"), xlnb[:, :, :]),
                        waits=[mln])
                    P.reads("xlnf", m_xst[st])
                    mtab_st = load_tabs(cols)
                    for kv in range(NKV):
                        wpar = kv % 2
                        mwl = P.dma("sync", lambda e, kv=kv, wpar=wpar: e.dma_start(
                            wsq[:, wpar], wk_c[kv]),
                            waits=[m_wk[kv]] + P.war(("wsq", wpar)))
                        P.guard[("wsq", wpar)] = {}
                        bk = kv % 2
                        w = [mwl, mln] + P.war(("pb", bk))
                        for dc in range(NDC):
                            mk = P.emit("tensor", lambda t, wpar=wpar, dc=dc, bk=bk: t.matmul(
                                pb[bk][:, :], wsq[:, wpar, dc, :], xlnb[:, dc, :],
                                start=(dc == 0), stop=(dc == NDC - 1)),
                                waits=w if dc == 0 else (), inc=1)
                        P.guard[("pb", bk)] = {}
                        P.reads(("wsq", wpar), mk)
                        P.reads("xlnf", mk)
                        m_kr = rope(bk, kv % 2, cols, kr[:, kv, cols], False, "kr", mk, mtab_st)
                    for ttl in range(NTT):
                        tc = st * NTT + ttl
                        vbk = 4 + ttl % 2
                        w = [mwv_l, mln] + P.war(("pb", vbk))
                        for dc in range(NDC):
                            mv = P.emit("tensor", lambda t, ttl=ttl, dc=dc, vbk=vbk: t.matmul(
                                pb[vbk][:, :], xlnb[:, dc, ttl * 128:(ttl + 1) * 128],
                                wv_sb[:, dc, :],
                                start=(dc == 0), stop=(dc == NDC - 1)),
                                waits=w if dc == 0 else (), inc=1)
                        P.guard[("pb", vbk)] = {}
                        P.reads("xlnf", mv)
                        m_v = P.emit("vector", lambda v, tc=tc, vbk=vbk: v.tensor_copy(
                            v_tm[:, tc, :], pb[vbk][:, :]),
                            waits=[mv] + (P.war("v_tm") if tc == 0 else []), inc=1)
                        if tc == 0:
                            P.guard["v_tm"] = {}
                        P.reads(("pb", vbk), m_v)

                for qg in range(NQG):
                    ccols = slice(qg * QG, (qg + 1) * QG)
                    mxq = P.dma("sync", lambda e, ccols=ccols: e.dma_start(
                        xlnb[:, :, :], xlnf_d[:, :, ccols].rearrange("c p n -> p c n")),
                        waits=[m_xst[qg]] + P.war("xlnf"))
                    P.guard["xlnf"] = {}
                    m_qr = None
                    mtab_qg = load_tabs(ccols)
                    for h in range(NH):
                        wpar = h % 2
                        mwl = P.dma("sync", lambda e, h=h, wpar=wpar: e.dma_start(
                            wsq[:, wpar], wq_c[h]),
                            waits=[m_wq[h]] + P.war(("wsq", wpar)))
                        P.guard[("wsq", wpar)] = {}
                        bk = h % 2
                        w = [mwl, mxq] + P.war(("pb", bk))
                        for dc in range(NDC):
                            mq = P.emit("tensor", lambda t, wpar=wpar, dc=dc, bk=bk: t.matmul(
                                pb[bk][:, :], wsq[:, wpar, dc, :], xlnb[:, dc, :],
                                start=(dc == 0), stop=(dc == NDC - 1)),
                                waits=w if dc == 0 else (), inc=1)
                        P.guard[("pb", bk)] = {}
                        P.reads(("wsq", wpar), mq)
                        P.reads("xlnf", mq)
                        m_qr = rope(bk, h % 2, ccols, q_qg[:, h, :], True, ("q_qg", h), mq, mtab_qg)

                    pt_bufs = [pt, wv_sb]
                    if qg == 0:
                        P.reads(("ptb", 1), m_v)

                    def emit_consumers(ch, chp, cppt, cm_exp, ckvh, cobk):
                        w = [cm_exp] + ([P.m_ones] if ch == 0 else []) + P.war(("pb", 4))
                        msum = None
                        for kc in range(NDC):
                            msum = P.emit("tensor", lambda t, kc=kc, cppt=cppt: t.matmul(
                                pb[4][:, :], sb.ones_mat[:, :], cppt[:, kc, :],
                                start=(kc == 0), stop=(kc == NDC - 1)),
                                waits=w if kc == 0 else (), inc=1)
                        P.guard[("pb", 4)] = {}
                        w2 = ([m_v] if ch == 0 else []) + P.war(("pb", cobk))
                        mo = None
                        for kc in range(NDC):
                            mo = P.emit("tensor", lambda t, kc=kc, ckvh=ckvh, cobk=cobk, cppt=cppt: t.matmul(
                                pb[cobk][:, :], v_tm[:, kc, ckvh * 128:(ckvh + 1) * 128],
                                cppt[:, kc, :],
                                start=(kc == 0), stop=(kc == NDC - 1)),
                                waits=w2 if kc == 0 else (), inc=1)
                        P.guard[("pb", cobk)] = {}
                        P.reads(("ptb", chp), mo)
                        ms1 = P.emit("scalar", lambda s: s.activation(
                            rln[:, :], pb[4][:, :], AF.Ln),
                            waits=[msum] + P.war("rln"), inc=1)
                        P.guard["rln"] = {}
                        P.reads(("pb", 4), ms1)
                        ms2 = P.emit("scalar", lambda s: s.activation(
                            rbc[:, :], rln[:, :], AF.Exp, scale=np.float32(-1.0)),
                            waits=P.war("rbc"), inc=1)
                        P.guard["rbc"] = {}
                        P.reads("rln", ms2)
                        moev = P.emit("vector", lambda v, ch=ch, cobk=cobk: v.tensor_tensor(
                            o_sb[:, ch, :], pb[cobk][:, :], rbc[:, :], op=ALU.mult),
                            waits=[mo, ms2] + (P.war("o_sb") if ch == 0 else []), inc=1)
                        if ch == 0:
                            P.guard["o_sb"] = {}
                        P.reads(("pb", cobk), moev)
                        P.reads("rbc", moev)
                        return moev

                    prev = None
                    for h in range(NH):
                        hp = h % 2
                        ppt = pt_bufs[hp]
                        kvh = h % NKV
                        obk = 5 + hp
                        m_exp_last = None
                        for pr in range(8):
                            pbase = (pr % 2) * 2
                            w = [m_qr, m_kr] if (pr == 0 and h == 0) else []
                            w = w + P.war(("pb", pbase)) + P.war(("pb", pbase + 1))
                            for j in (0, 1):
                                kc = 2 * pr + j
                                msmm = P.emit("tensor", lambda t, kvh=kvh, kc=kc, bk=pbase + j, h=h: t.matmul(
                                    pb[bk][:, :], kr[:, kvh, kc * 128:(kc + 1) * 128], q_qg[:, h, :],
                                    start=True, stop=True), waits=w if j == 0 else (), inc=1)
                            P.guard[("pb", pbase)] = {}
                            P.guard[("pb", pbase + 1)] = {}
                            P.reads(("q_qg", h), msmm)
                            P.reads("kr", msmm)
                            w2 = [msmm]
                            if pr == 0:
                                w2 += P.war(("ptb", hp))
                                P.guard[("ptb", hp)] = {}
                            m_exp_last = P.emit("scalar", lambda s, pr=pr, pbase=pbase, ppt=ppt: s.activation(
                                ppt[:, 2 * pr:2 * pr + 2, :], pbig[:, pbase:pbase + 2, :],
                                AF.Exp, bias=np.float32(-EXP_SHIFT)),
                                waits=w2, inc=1)
                            P.reads(("pb", pbase), m_exp_last)
                            P.reads(("pb", pbase + 1), m_exp_last)
                        if prev is not None:
                            emit_consumers(*prev)
                        prev = (h, hp, ppt, m_exp_last, kvh, obk)
                    m_oev = emit_consumers(*prev)

                    if debug_streams and qg == 0:
                        P.dma("gpsimd", lambda g: g.dma_start(o_dbg[:, :, :], o_sb[:, :, :]),
                              waits=[m_oev])
                        P.dma("gpsimd", lambda g: g.dma_start(q_dbg[:, :, :], q_qg[:, :, :]),
                              waits=[m_qr])
                        P.dma("gpsimd", lambda g: g.dma_start(k_dbg[:, :, :], kr[:, :, :]),
                              waits=[m_kr])
                    mxl = P.dma("sync", lambda e, ccols=ccols: e.dma_start(
                        sb.x_fm[:, :, :], src_fm[:, :, ccols].rearrange("c p n -> p c n")),
                        waits=P.war("x_fm"))
                    P.guard["x_fm"] = {}
                    for dout in range(NDC):
                        wpar = dout % 2
                        mwl = P.dma("sync", lambda e, dout=dout, wpar=wpar: e.dma_start(
                            wsq[:, wpar], wo_c[dout]),
                            waits=[m_wo[dout]] + P.war(("wsq", wpar)))
                        P.guard[("wsq", wpar)] = {}
                        ybk = 2 + dout % 2
                        w = [mwl, m_oev] + P.war(("pb", ybk))
                        my = None
                        for dinc in range(NDC):
                            my = P.emit("tensor", lambda t, dout=dout, dinc=dinc, wpar=wpar, ybk=ybk: t.matmul(
                                pb[ybk][:, :], wsq[:, wpar, dinc, :], o_sb[:, dinc, :],
                                start=(dinc == 0), stop=(dinc == NDC - 1)),
                                waits=w if dinc == 0 else (), inc=1)
                        P.guard[("pb", ybk)] = {}
                        P.reads(("wsq", wpar), my)
                        P.reads("o_sb", my)
                        opar = dout % 2
                        mev = P.emit("vector", lambda v, opar=opar, dout=dout, ybk=ybk: v.tensor_tensor(
                            sb.xout[:, opar, :], pb[ybk][:, :], sb.x_fm[:, dout, :], op=ALU.add),
                            waits=[my, mxl] + P.war(("xout", opar)), inc=1)
                        P.guard[("xout", opar)] = {}
                        P.reads(("pb", ybk), mev)
                        P.reads("x_fm", mev)
                        mst = P.dma("scalar", lambda g, opar=opar, dout=dout, ccols=ccols: g.dma_start(
                            dst_fm[dout, :, ccols], sb.xout[:, opar, :]),
                            waits=[mev])
                        P.reads(("xout", opar), mst)
                P.fence()

        emit_ffn(src_tm=x_in, src_fm=None, dst_fm=s1, dst_tm=None, tag="f1",
                 cast_hook=cast_hook)
        emit_attn(src_fm=s1, dst_fm=s2)
        emit_ffn(src_tm=None, src_fm=s2, dst_fm=None, dst_tm=out, tag="f2")

        P.emit("gpsimd", None, waits=[(f"gpsimd_d{i}", P.counts[f"gpsimd_d{i}"])
                                      for i in range(P.NDMASEM)] +
                               [(f"scalar_d{i}", P.counts[f"scalar_d{i}"])
                                for i in range(P.NDMASEM)])

        block = es.enter_context(nc.Block())

        @block.sync
        def _(e):
            for f in P.steps["sync"]:
                f(e)

        @block.gpsimd
        def _(e):
            for f in P.steps["gpsimd"]:
                f(e)

        @block.tensor
        def _(e):
            for f in P.steps["tensor"]:
                f(e)

        @block.vector
        def _(e):
            for f in P.steps["vector"]:
                f(e)

        @block.scalar
        def _(e):
            for f in P.steps["scalar"]:
                f(e)

    return nc


def host_constants():
    TINY = np.float32(2.0 ** -126)
    consts = np.zeros((128, 16), dtype=np.float32)
    p = np.arange(128)
    for k in (1, 2, 3):
        consts[:, k - 1] = np.where((p % 4) + k < 4, TINY, 0.0)
    consts[:, C_15] = 1.5
    consts[:, C_INV2048] = 1.0 / 2048.0
    consts[:, C_EPS] = 1e-5
    consts[:, C_NEG1] = -1.0
    consts[:, C_HALF] = 0.5
    consts[:, C_ISQHD] = 1.0 / np.sqrt(np.float32(HD))
    consts[:, C_ONE] = 1.0
    ident = np.eye(128, dtype=np.float32)
    sd = np.zeros((3, 128, 128), dtype=np.float32)
    for k in (1, 2, 3):
        for c in range(128):
            r = 4 * (c // 4) + (c + k) % 4
            sd[k - 1, r, c] += 1.0
            sd[k - 1, c, c] -= 1.0
    swapm = np.zeros((128, 128), dtype=np.float32)
    for c in range(128):
        r = c + 1 if c % 2 == 0 else c - 1
        swapm[r, c] = 1.0
    half = HD // 2
    inv = 1.0 / 10000.0 ** (np.arange(half, dtype=np.float64) / half)
    pos = np.arange(S, dtype=np.float64)
    ang = pos[None, :] * inv[:, None]
    ctab = np.repeat(np.cos(ang), 2, axis=0).astype(np.float32)
    sin2 = np.repeat(np.sin(ang), 2, axis=0)
    sign = np.where(np.arange(128) % 2 == 0, -1.0, 1.0)
    stab = (sin2 * sign[:, None]).astype(np.float32)
    return consts, ident, sd, swapm, ctab, stab


def kernel(L0_attn_x, L0_attn_W_q, L0_attn_W_k, L0_attn_W_v, L0_attn_W_o,
           L0_ffn_w_gate, L0_ffn_w_up, L0_ffn_w_down):
    x = np.ascontiguousarray(np.asarray(L0_attn_x, dtype=np.float32))
    consts, ident, sd, swapm, ctab, stab = host_constants()
    common = {
        "wq": np.ascontiguousarray(np.asarray(L0_attn_W_q, dtype=np.float32)),
        "wk": np.ascontiguousarray(np.asarray(L0_attn_W_k, dtype=np.float32)),
        "wv": np.ascontiguousarray(np.asarray(L0_attn_W_v, dtype=np.float32)),
        "wo": np.ascontiguousarray(np.asarray(L0_attn_W_o, dtype=np.float32)),
        "wg": np.ascontiguousarray(np.asarray(L0_ffn_w_gate, dtype=np.float32)),
        "wu": np.ascontiguousarray(np.asarray(L0_ffn_w_up, dtype=np.float32)),
        "wd": np.ascontiguousarray(np.asarray(L0_ffn_w_down, dtype=np.float32)),
        "consts": consts, "ident": ident, "sd": sd, "swapm": swapm,
        "ctab": ctab, "stab": stab,
    }
    nc = build()
    in_maps = [dict(common, x=np.ascontiguousarray(x[i])) for i in range(8)]
    res = run_bass_kernel_spmd(nc, in_maps, core_ids=list(range(8)))
    return np.stack([res.results[i]["out"] for i in range(8)], axis=0)
